# version 01
# baseline (speedup 1.0000x reference)
import math
import numpy as np
import ml_dtypes
from contextlib import ExitStack

import concourse.bass as bass
import concourse.mybir as mybir
from concourse.bass_utils import run_bass_kernel_spmd

F32 = mybir.dt.float32
BF16 = mybir.dt.bfloat16
U8 = mybir.dt.uint8
AF = mybir.ActivationFunctionType
ALU = mybir.AluOpType
BF = ml_dtypes.bfloat16

NC = 8
S = 8192
D = 4096
TPC = S // NC
NH = 32
NKV = 8
HD = 128
EPS_LN = 1e-5
EPS_RMS = 1e-6
NEG_BIG = -1.0e30
SQRT128 = float(np.sqrt(128.0))

PI = np.arange(D).reshape(D // 256, 128, 2).transpose(0, 2, 1).reshape(-1)

ENGINES = ("sync", "tensor", "vector", "scalar", "gpsimd")


def _alibi_slopes(n_heads):
    start = 2 ** (-(2 ** -(math.log2(n_heads) - 3)))
    return np.array([start * start ** i for i in range(n_heads)], dtype=np.float64)


def _split3(v):
    s0 = v.astype(BF).astype(np.float64)
    s1 = (v - s0).astype(BF).astype(np.float64)
    s2 = (v - s0 - s1).astype(BF)
    return s0.astype(BF), s1.astype(BF), s2


class Em:

    def __init__(self, nc):
        self.nc = nc
        self.es = ExitStack()
        self.ops = {e: [] for e in ENGINES}
        self.sems = {}
        self.cnt = {}
        for name in ("vch", "ach", "pch", "tch", "tbar"):
            self.sems[name] = self.es.enter_context(nc.semaphore(name))
            self.cnt[name] = 0
        self.qstreams = {"sq": set(), "gq": set(), "aq": set()}
        self.auto_i = 0

    def _app(self, eng, fn):
        self.ops[eng].append(fn)

    def _waits(self, waits):
        out = []
        for w in waits:
            if w is None:
                continue
            (n, v) = w
            if v > 0:
                out.append((self.sems[n], v))
        return tuple(out)

    def wait(self, eng, sem_name, val):
        s = self.sems[sem_name]
        self._app(eng, lambda e, s=s, v=val: e.wait_ge(s, v))

    def _chain(self, eng, ch, fn, waits=()):
        self.cnt[ch] += 1
        tgt = self.cnt[ch]
        sem = self.sems[ch]
        w = self._waits(waits)

        def run(e, fn=fn, sem=sem, tgt=tgt, w=w):
            for (ws, wv) in w:
                e.wait_ge(ws, wv)
            if tgt > 1:
                e.wait_ge(sem, tgt - 1)
            fn(e).then_inc(sem, 1)

        self._app(eng, run)
        return tgt

    def vec(self, fn, waits=()):
        return self._chain("vector", "vch", fn, waits)

    def act(self, fn, waits=()):
        return self._chain("scalar", "ach", fn, waits)

    def pool(self, fn, waits=()):
        return self._chain("gpsimd", "pch", fn, waits)

    def mm(self, fn, waits=(), inc=False):
        w = self._waits(waits)
        if inc:
            self.cnt["tch"] += 1
            tgt = self.cnt["tch"]
            sem = self.sems["tch"]

            def run(e, fn=fn, w=w, sem=sem):
                for (ws, wv) in w:
                    e.wait_ge(ws, wv)
                fn(e).then_inc(sem, 1)

            self._app("tensor", run)
            return tgt
        else:
            def run(e, fn=fn, w=w):
                for (ws, wv) in w:
                    e.wait_ge(ws, wv)
                fn(e)

            self._app("tensor", run)
            return None

    def tmark(self, waits=()):
        w = self._waits(waits)
        self.cnt["tbar"] += 1
        tgt = self.cnt["tbar"]
        sem = self.sems["tbar"]

        def run(e, w=w, sem=sem):
            for (ws, wv) in w:
                e.wait_ge(ws, wv)
            e.nop().then_inc(sem, 1)

        self._app("tensor", run)
        return tgt

    def _dma_emit(self, queue, q, stream, fn, waits):
        if stream is None:
            stream = f"auto{self.auto_i}"
            self.auto_i += 1
        n = f"{q}_{stream}"
        if n not in self.sems:
            self.sems[n] = self.es.enter_context(self.nc.semaphore(n))
            self.cnt[n] = 0
            self.qstreams[q].add(n)
        self.cnt[n] += 16
        tgt = self.cnt[n]
        sem = self.sems[n]
        w = self._waits(waits)

        def run(e, fn=fn, sem=sem, w=w):
            for (ws, wv) in w:
                e.wait_ge(ws, wv)
            fn(e).then_inc(sem, 16)

        self._app(queue, run)
        return (n, tgt)

    def dma(self, queue, out, in_, waits=(), stream=None, **kw):
        q = {"sync": "sq", "gpsimd": "gq", "scalar": "aq"}[queue]
        return self._dma_emit(queue, q, stream,
                              lambda e, out=out, in_=in_, kw=kw:
                              e.dma_start(out=out, in_=in_, **kw), waits)

    def dmat(self, queue, out, in_, waits=(), stream=None):
        q = {"sync": "sq", "scalar": "aq"}[queue]
        return self._dma_emit(queue, q, stream,
                              lambda e, out=out, in_=in_:
                              e.dma_start_transpose(out, in_), waits)

    def qbar(self, q):
        return tuple((n, self.cnt[n]) for n in sorted(self.qstreams[q])
                     if self.cnt[n] > 0)

    def finalize(self, final_waits=()):
        for (eng, sem_name, val) in final_waits:
            self.wait(eng, sem_name, val)
        for (eng, q) in (("sync", "sq"), ("gpsimd", "gq"), ("scalar", "aq")):
            for (n, v) in self.qbar(q):
                self.wait(eng, n, v)
        block = self.es.enter_context(self.nc.Block())
        for eng in ENGINES:
            ops = self.ops[eng]
            if not ops:
                continue
            getattr(block, eng)(lambda e, ops=ops: [f(e) for f in ops])
        self.es.close()



class LNBufs:

    def __init__(self, em, sbf, PS):
        self.xb = [sbf("ln_xb0", [128, TPC], BF16), sbf("ln_xb1", [128, TPC], BF16)]
        self.xf = [sbf("ln_xf0", [128, TPC], F32), sbf("ln_xf1", [128, TPC], F32)]
        self.sq = [sbf("ln_sq0", [128, TPC], BF16), sbf("ln_sq1", [128, TPC], BF16)]
        self.sxs = sbf("ln_sxs", [2, 512], F32)
        self.sqs = sbf("ln_sqs", [2, 512], F32)
        self.m2 = sbf("ln_m2", [2, 512], F32)
        self.var = sbf("ln_var", [2, 512], F32)
        self.u2 = sbf("ln_u2", [2, 512], F32)
        self.inv = sbf("ln_inv", [2, 512], F32)
        self.minv = sbf("ln_minv", [2, 512], F32)
        self.tmp2 = sbf("ln_tmp2", [2, 512], F32)
        self.invb = sbf("ln_invb", [128, TPC], F32)
        self.minvb = sbf("ln_minvb", [128, TPC], F32)
        self.t1 = sbf("ln_t1", [128, TPC], F32)
        self.epsc = sbf("ln_epsc", [128, 1], F32)
        self.onescol2 = sbf("ln_oc2", [128, 4], BF16)
        self.sel2 = sbf("ln_sel2", [2, 256], F32)
        self.PS = PS


def bank(PS, i, rows=128, cols=512):
    return PS[0:rows, 512 * i:512 * i + cols]


def emit_ln_z(em, lb, xsrc_dram, zt, prior_psum_free=0, gate_dma=()):
    PS = lb.PS
    stx = bank(PS, 0, rows=2)
    stq = bank(PS, 1, rows=2)
    t0 = em.cnt["tch"]
    v0 = em.cnt["vch"]
    a0 = em.cnt["ach"]
    xb_free = [t0, t0]
    sq_free = [t0, t0]
    for c in range(32):
        i = c % 2
        xb = lb.xb[i]
        ld = em.dma("gpsimd", xb[:, :], xsrc_dram[128 * c:128 * c + 128, :],
                    waits=[("tch", xb_free[i])] + (list(gate_dma) if c < 2 else []),
                    stream=f"lnxb{i}")
        sqt = lb.sq[i]
        sv = em.act(lambda e, xb=xb, sqt=sqt: e.activation(sqt[:, :], xb[:, :], AF.Square),
                    waits=[ld, ("tch", sq_free[i])])
        for h in range(2):
            w0 = []
            if c == 0 and h == 0:
                w0 = [("vch", prior_psum_free)] + list(gate_dma)
            em.mm(lambda e, h=h, c=c, xb=xb:
                  e.matmul(stx[:, :], lb.onescol2[:, 2 * h:2 * h + 2],
                           xb[:, 512 * h:512 * h + 512],
                           start=(c == 0 and h == 0), stop=(c == 31 and h == 1)),
                  waits=w0 + ([ld] if h == 0 else []))
            mk = em.mm(lambda e, h=h, c=c, sqt=sqt:
                       e.matmul(stq[:, :], lb.onescol2[:, 2 * h:2 * h + 2],
                                sqt[:, 512 * h:512 * h + 512],
                                start=(c == 0 and h == 0), stop=(c == 31 and h == 1)),
                       waits=[("ach", sv)] if h == 0 else [], inc=(h == 1))
        xb_free[i] = mk
        sq_free[i] = mk

    tdone = em.cnt["tch"]
    em.vec(lambda e: e.tensor_copy(lb.sxs[:, :], stx[:, :]), waits=[("tch", tdone)])
    em.vec(lambda e: e.tensor_copy(lb.sqs[:, :], stq[:, :]))
    em.vec(lambda e: e.tensor_scalar_mul(lb.m2[:, :], lb.sxs[:, :], 1.0 / D))
    em.vec(lambda e: e.tensor_scalar_mul(lb.var[:, :], lb.sqs[:, :], 1.0 / D))
    em.vec(lambda e: e.tensor_tensor(lb.tmp2[:, :], lb.m2[:, :], lb.m2[:, :], ALU.mult))
    vv = em.vec(lambda e: e.tensor_tensor(lb.var[:, :], lb.var[:, :], lb.tmp2[:, :], ALU.subtract))
    uval = em.act(lambda e: e.activation(lb.u2[:, :], lb.var[:, :], AF.Sqrt,
                                         bias=lb.epsc[0:2, 0:1]),
                  waits=[("vch", vv), ("vch", lb.eps_ready)])
    em.vec(lambda e: e.reciprocal(lb.inv[:, :], lb.u2[:, :]), waits=[("ach", uval)])
    vready = em.vec(lambda e: e.tensor_tensor(lb.minv[:, :], lb.m2[:, :], lb.inv[:, :], ALU.mult))

    prev_evac = 0
    bdone = 0
    for (vecs, dst) in ((lb.inv, lb.invb), (lb.minv, lb.minvb)):
        for h in range(2):
            em.mm(lambda e, vecs=vecs, h=h:
                  e.matmul(bank(PS, 2 + h), lb.sel2[:, 128 * h:128 * h + 128],
                           vecs[:, :], start=True, stop=True),
                  waits=[("vch", vready), ("vch", prev_evac)], inc=(h == 1))
        t = em.cnt["tch"]
        for h in range(2):
            bdone = em.vec(lambda e, dst=dst, h=h:
                           e.tensor_copy(dst[:, 512 * h:512 * h + 512], bank(PS, 2 + h)),
                           waits=[("tch", t)])
        prev_evac = bdone

    xf_free = [v0, v0]
    zready = 0
    for c in range(32):
        i = c % 2
        xf = lb.xf[i]
        ld = em.dma("sync", xf[:, :], xsrc_dram[128 * c:128 * c + 128, :],
                    waits=[("vch", xf_free[i])] + (list(gate_dma) if c < 2 else []),
                    stream=f"lnxf{i}")
        em.vec(lambda e, xf=xf: e.tensor_tensor(lb.t1[:, :], xf[:, :], lb.invb[:, :], ALU.mult),
               waits=[ld, ("vch", bdone)])
        zready = em.vec(lambda e, c=c: e.tensor_tensor(
            zt[:, TPC * c:TPC * c + TPC], lb.t1[:, :], lb.minvb[:, :], ALU.subtract))
        xf_free[i] = zready
    return zready


def emit_sumz(em, lb, zt, zrow2, ones1, sumz_gate=0):
    PS = lb.PS
    t = None
    for h in range(2):
        ps = bank(PS, h, rows=1)
        for c in range(32):
            t = em.mm(lambda e, ps=ps, c=c, h=h:
                      e.matmul(ps[:, :], ones1[:, :],
                               zt[:, TPC * c + 512 * h:TPC * c + 512 * h + 512],
                               start=(c == 0), stop=(c == 31)),
                      waits=[("vch", sumz_gate)] if (c == 0 and h == 0) else (),
                      inc=(h == 1 and c == 31))
    v = 0
    for h in range(2):
        v = em.vec(lambda e, h=h: e.tensor_copy(zrow2[0:1, 512 * h:512 * h + 512],
                                                bank(PS, h, rows=1)),
                   waits=[("tch", t)])
    return v


def emit_ffn(em, lb, F, zt, z_ready, sumz_v, xin_dram, xout_dram):
    PS = lb.PS
    v0 = em.cnt["vch"]
    t0 = em.cnt["tch"]
    psum_free = [v0] * 8
    wby_free = [v0, v0]
    x1b_free = [v0, v0]
    x1o_free = [None, None]
    last_store = None
    F["ext_free"] = [t0, t0]
    for og in range(8):
        grp_started = False
        for bc in range(16):
            i2 = bc % 2
            wb = F["wby"][i2]
            ld = em.dma("sync", wb[:, :],
                        F["wpt"][128 * bc:128 * bc + 128, 512 * og:512 * og + 512],
                        waits=[("vch", wby_free[i2])], stream=f"wby{i2}")
            lo8, hi8 = F["lo8"][i2], F["hi8"][i2]
            lo, hi = F["lo"][i2], F["hi"][i2]
            em.vec(lambda e, wb=wb, lo8=lo8: e.tensor_scalar(
                lo8[:, :], wb[:, :], 15, None, ALU.bitwise_and),
                waits=[ld, ("tch", F["ext_free"][i2])])
            em.vec(lambda e, wb=wb, hi8=hi8: e.tensor_scalar(
                hi8[:, :], wb[:, :], 4, None, ALU.logical_shift_right))
            em.vec(lambda e, lo8=lo8, lo=lo: e.tensor_copy(lo[:, :], lo8[:, :]))
            vex = em.vec(lambda e, hi8=hi8, hi=hi: e.tensor_copy(hi[:, :], hi8[:, :]))
            wby_free[i2] = vex
            for par, wsb in ((0, lo), (1, hi)):
                ic = 2 * bc + par
                for th in range(2):
                    for oc in range(4):
                        psi = 4 * th + oc
                        wl = []
                        if bc == 0 and par == 0:
                            if th == 0 and oc == 0:
                                wl = [("vch", max(z_ready, sumz_v, vex)),
                                      F["zr1_done"]]
                            wl.append(("vch", psum_free[psi]))
                        elif par == 0 and th == 0 and oc == 0:
                            wl = [("vch", vex)]
                        mk = em.mm(lambda e, psi=psi, wsb=wsb, oc=oc, ic=ic, th=th,
                                   s=(bc == 0 and par == 0):
                                   e.matmul(bank(PS, psi),
                                            wsb[:, 128 * oc:128 * oc + 128],
                                            zt[:, TPC * ic + 512 * th:TPC * ic + 512 * th + 512],
                                            start=s, stop=False),
                                   waits=wl,
                                   inc=(par == 1 and th == 1 and oc == 3))
            F["ext_free"][i2] = mk
        for th in range(2):
            for oc in range(4):
                psi = 4 * th + oc
                oco = 4 * og + oc
                tm = em.mm(lambda e, psi=psi, oco=oco, th=th:
                           e.matmul(bank(PS, psi),
                                    F["zpb2"][:, 128 * oco:128 * oco + 128],
                                    F["zrow2"][:, 512 * th:512 * th + 512],
                                    start=False, stop=True),
                           inc=True)
                i2 = psi % 2
                xb_ = F["x1b"][i2]
                xo_ = F["x1o"][i2]
                ldx = em.dma("gpsimd", xb_[:, :],
                             xin_dram[128 * oco:128 * oco + 128, 512 * th:512 * th + 512],
                             waits=[("vch", x1b_free[i2])] + list(F["xin_gate"]),
                             stream=f"ffnx{i2}")
                ev = em.vec(lambda e, psi=psi, oco=oco, xb_=xb_, xo_=xo_:
                            e.scalar_tensor_tensor(xo_[:, :], bank(PS, psi),
                                                   F["hs"][:, oco:oco + 1],
                                                   xb_[:, :], ALU.mult, ALU.add),
                            waits=[("tch", tm), ldx, x1o_free[i2]])
                psum_free[psi] = ev
                x1b_free[i2] = ev
                last_store = em.dma("sync",
                                    xout_dram[128 * oco:128 * oco + 128,
                                              512 * th:512 * th + 512],
                                    xo_[:, :], waits=[("vch", ev)],
                                    stream=f"ffno{i2}")
                x1o_free[i2] = last_store
    return last_store



def build_l1(trunc=None):
    nc = bass.Bass("TRN2", target_bir_lowering=False)
    xt = nc.dram_tensor("xt", [D, TPC], F32, kind="ExternalInput").ap()
    wqt = nc.dram_tensor("wqt", [D, D], F32, kind="ExternalInput").ap()
    wkt = nc.dram_tensor("wkt", [D, NKV * HD], F32, kind="ExternalInput").ap()
    wvt = nc.dram_tensor("wvt", [D, NKV * HD], F32, kind="ExternalInput").ap()
    wpt = nc.dram_tensor("wpt", [D // 2, D], U8, kind="ExternalInput").ap()
    zp32 = nc.dram_tensor("zp32", [32, 128], F32, kind="ExternalInput").ap()
    bf32 = nc.dram_tensor("bf32", [32, 128], F32, kind="ExternalInput").ap()
    sc32 = nc.dram_tensor("sc32", [32, 128], F32, kind="ExternalInput").ap()
    qnw = nc.dram_tensor("qnw", [1, 128], F32, kind="ExternalInput").ap()
    knw = nc.dram_tensor("knw", [1, 128], F32, kind="ExternalInput").ap()
    sel2 = nc.dram_tensor("sel2", [2, 256], F32, kind="ExternalInput").ap()
    onescol2 = nc.dram_tensor("onescol2", [128, 4], BF16, kind="ExternalInput").ap()
    identf = nc.dram_tensor("identf", [128, 128], F32, kind="ExternalInput").ap()
    qselt = nc.dram_tensor("qselt", [128, 32 * 32], BF16, kind="ExternalInput").ap()
    hsel = nc.dram_tensor("hsel", [32, 32 * 128], F32, kind="ExternalInput").ap()

    qraw_d = nc.dram_tensor("qraw_d", [NH, HD, TPC], BF16, kind="Internal").ap()
    x1t = nc.dram_tensor("x1t", [D, TPC], F32, kind="ExternalOutput").ap()
    kt = nc.dram_tensor("kt", [NKV, HD, TPC], BF16, kind="ExternalOutput").ap()
    vv = nc.dram_tensor("vv", [TPC, NKV * HD], BF16, kind="ExternalOutput").ap()
    qt = nc.dram_tensor("qt", [NH, HD, TPC], BF16, kind="ExternalOutput").ap()

    em = Em(nc)
    es = em.es
    sbf = lambda name, shape, dt: es.enter_context(nc.sbuf_tensor(name, shape, dt))
    PS = es.enter_context(nc.psum_tensor("PS", [128, 4096], F32))

    lb = LNBufs(em, sbf, PS)
    zt = sbf("zt", [128, 32 * TPC], BF16)

    F = {}
    F["wpt"] = wpt
    F["wby"] = [sbf("wby0", [128, 512], U8), sbf("wby1", [128, 512], U8)]
    F["lo8"] = [sbf("lo80", [128, 512], U8), sbf("lo81", [128, 512], U8)]
    F["hi8"] = [sbf("hi80", [128, 512], U8), sbf("hi81", [128, 512], U8)]
    F["lo"] = [sbf("lo0", [128, 512], BF16), sbf("lo1", [128, 512], BF16)]
    F["hi"] = [sbf("hi0", [128, 512], BF16), sbf("hi1", [128, 512], BF16)]
    F["zpb2"] = sbf("zpb2", [2, D], BF16)
    F["zrow2"] = sbf("zrow2", [2, TPC], BF16)
    F["hs"] = sbf("hs", [128, 32], F32)
    F["x1b"] = [sbf("x1b0", [128, 512], F32), sbf("x1b1", [128, 512], F32)]
    F["x1o"] = [sbf("x1o0", [128, 512], F32), sbf("x1o1", [128, 512], F32)]
    F["ext_free"] = [0, 0]

    srow1b = sbf("srow1b", [1, TPC], BF16)
    zp32s = sbf("zp32s", [32, 128], F32)
    bf32s = sbf("bf32s", [32, 128], F32)
    nzp32 = sbf("nzp32", [32, 128], BF16)
    bs32 = sbf("bs32", [32, 128], BF16)
    rs32 = sbf("rs32", [32, 128], F32)
    sc32s = sbf("sc32s", [32, 128], F32)
    identfs = sbf("identfs", [128, 128], F32)
    ones1 = sbf("ones1", [128, 1], BF16)
    qsels = sbf("qsels", [128, 32 * 32], BF16)
    hsels = sbf("hsels", [32, 32 * 128], F32)
    sqq = sbf("sqq", [128, TPC], BF16)
    stq32 = sbf("stq32", [32, TPC], F32)
    u32 = sbf("u32", [32, TPC], F32)
    inv32 = sbf("inv32", [32, TPC], F32)
    invqb = sbf("invqb", [128, TPC], F32)
    epsr = sbf("epsr", [128, 1], F32)
    qnwc = sbf("qnwc", [128, 1], F32)
    knwc = sbf("knwc", [128, 1], F32)
    hbuf = [sbf("hbuf0", [128, TPC], BF16), sbf("hbuf1", [128, TPC], BF16)]
    qout = [sbf("qout0", [128, TPC], BF16), sbf("qout1", [128, TPC], BF16)]
    wk_b = [sbf("wkb0", [128, 256], BF16), sbf("wkb1", [128, 256], BF16)]
    wq_b = [sbf("wqb0", [128, 512], BF16), sbf("wqb1", [128, 512], BF16)]
    wv_b = [sbf("wvb0", [128, 1024], BF16), sbf("wvb1", [128, 1024], BF16)]

    l_oc2 = em.dma("sync", lb.onescol2[:, :], onescol2[:, :])
    l_sel2 = em.dma("sync", lb.sel2[:, :], sel2[:, :])
    l_zpf = em.dma("sync", zp32s[:, :], zp32[:, :])
    l_bff = em.dma("sync", bf32s[:, :], bf32[:, :])
    l_sc32 = em.dma("sync", sc32s[:, :], sc32[:, :])
    l_idf = em.dma("sync", identfs[:, :], identf[:, :])
    l_qsel = em.dma("sync", qsels[:, :], qselt[:, :])
    l_hsel = em.dma("sync", hsels[:, :], hsel[:, :])
    l_qnw = em.dma("sync", qnwc[:, 0:1], qnw.rearrange("1 p -> p 1"))
    l_knw = em.dma("sync", knwc[:, 0:1], knw.rearrange("1 p -> p 1"))
    em.vec(lambda e: e.memset(ones1[:, :], 1.0))
    lb.eps_ready = em.vec(lambda e: e.memset(lb.epsc[:, :], EPS_LN))
    v_epsr = em.vec(lambda e: e.memset(epsr[:, :], 128.0 * EPS_RMS))
    vnzp = em.vec(lambda e: e.tensor_scalar_mul(nzp32[:, :], zp32s[:, :], -1.0),
                  waits=[l_zpf])
    em.vec(lambda e: e.reciprocal(rs32[:, :], sc32s[:, :]),
           waits=[l_sc32])
    vdiv = em.vec(lambda e: e.tensor_tensor(bs32[:, :], bf32s[:, :], rs32[:, :],
                                            ALU.mult),
                  waits=[l_bff])
    d_zpb0 = em.dma("gpsimd", F["zpb2"][0:1, :], nzp32[:, :], waits=[("vch", vnzp)], stream="zpb0")
    d_zpb1 = em.dma("gpsimd", F["zpb2"][1:2, :], bs32[:, :], waits=[("vch", vdiv)], stream="zpb1")
    vone = em.vec(lambda e: e.memset(srow1b[0:1, 0:TPC], 1.0))
    F["zr1_done"] = em.dma("gpsimd", F["zrow2"][1:2, :], srow1b[0:1, 0:TPC],
                           waits=[("vch", vone), d_zpb0, d_zpb1], stream="zr1")
    t_hs = em.mm(lambda e: e.transpose(bank(PS, 7)[:, 0:32], sc32s[:, :], identfs[0:32, 0:32]),
                 waits=[l_idf, l_sc32], inc=True)
    v_hs = em.vec(lambda e: e.tensor_scalar_mul(F["hs"][:, :], bank(PS, 7)[:, 0:32], 0.5),
                  waits=[("tch", t_hs)])

    if trunc == "pro":
        em.finalize([("vector", "vch", em.cnt["vch"])])
        return nc
    gate_consts = em.qbar("sq")
    z1 = emit_ln_z(em, lb, xt, zt, prior_psum_free=v_hs, gate_dma=gate_consts)
    sz1 = emit_sumz(em, lb, zt, F["zrow2"], ones1, sumz_gate=z1)

    if trunc == "ln1":
        em.finalize([("vector", "vch", em.cnt["vch"])])
        return nc
    F["xin_gate"] = ()
    ffn1_sq = emit_ffn(em, lb, F, zt, z1, sz1, xt, x1t)

    if trunc == "ffn1":
        em.finalize([("vector", "vch", em.cnt["vch"])])
        return nc
    z2 = emit_ln_z(em, lb, x1t, zt, prior_psum_free=em.cnt["vch"], gate_dma=em.qbar("sq"))

    if trunc == "ln2":
        em.finalize([("vector", "vch", em.cnt["vch"])])
        return nc
    pstat = PS[0:32, 4 * 512:4 * 512 + 1024]

    kraw = sbf("kraw", [128, 8 * TPC], BF16)

    def stats_flags(h, th, first, total):
        return (h == first and th == 0, h == first + total - 1 and th == 1)

    kproj_psum_free = 0
    kstats_mark = 0
    ngroups = 1 if (trunc or "").startswith("kg0") else 4
    nostat = (trunc or "").endswith("nostat")
    nosq = (trunc or "").endswith("nosq")
    if (trunc or "").endswith("nokraw"):
        pass
    for g in range(ngroups):
        heads = (2 * g, 2 * g + 1)
        wb_free = [em.cnt["tch"], em.cnt["tch"]]
        for i in range(32):
            i2 = i % 2
            wb = wk_b[i2]
            ld = em.dma("gpsimd", wb[:, :],
                        wkt[128 * i:128 * i + 128, HD * heads[0]:HD * heads[0] + 256],
                        waits=[("tch", wb_free[i2])], stream=f"wk{i2}")
            for hh in range(2):
                for th in range(2):
                    wl = []
                    if hh == 0 and th == 0:
                        wl.append(ld)
                    if i == 0 and hh == 0 and th == 0:
                        wl.append(("vch", z2))
                        wl.append(("tbar", kproj_psum_free))
                    mk = em.mm(lambda e, hh=hh, th=th, i=i, wb=wb:
                               e.matmul(PS[0:128, (2 * hh + th) * 512:(2 * hh + th) * 512 + 512],
                                        wb[:, 128 * hh:128 * hh + 128],
                                        zt[:, TPC * i + 512 * th:TPC * i + 512 * th + 512],
                                        start=(i == 0), stop=(i == 31)),
                               waits=wl, inc=(hh == 1 and th == 1))
            wb_free[i2] = mk
        tdone = em.cnt["tch"]
        for hh, h in enumerate(heads):
            em.vec(lambda e, hh=hh, h=h: e.tensor_copy(
                kraw[:, TPC * h:TPC * h + TPC],
                PS[0:128, (2 * hh) * 512:(2 * hh) * 512 + 1024]),
                waits=[("tch", tdone)])
            sv = em.vec(lambda e, h=h: e.tensor_tensor(
                sqq[:, :], kraw[:, TPC * h:TPC * h + TPC],
                kraw[:, TPC * h:TPC * h + TPC], ALU.mult),
                waits=[("tch", kstats_mark)])
            for th in range(2):
                kstats_mark = em.mm(lambda e, h=h, th=th:
                      e.matmul(pstat[:, 512 * th:512 * th + 512],
                               qsels[:, 32 * h:32 * h + 32],
                               sqq[:, 512 * th:512 * th + 512],
                               start=(h == 0), stop=(h == 7 if ngroups == 4 else h == 1)),
                      waits=[("vch", sv)] if th == 0 else (), inc=(th == 1)) or kstats_mark
        kproj_psum_free = em.tmark(waits=[("vch", em.cnt["vch"])])
    t_kstats = em.cnt["tch"]

    if trunc is not None and trunc.startswith(("kgrp", "kg0")):
        em.finalize([("vector", "vch", em.cnt["vch"])])
        return nc
    vks = em.vec(lambda e: e.tensor_copy(stq32[:, :], pstat[:, :]),
                 waits=[("tch", t_kstats)])
    uk = em.act(lambda e: e.activation(u32[:, :], stq32[:, :], AF.Sqrt,
                                       bias=epsr[0:32, 0:1]),
                waits=[("vch", vks), ("vch", v_epsr)])
    em.vec(lambda e: e.reciprocal(inv32[:, :], u32[:, :]), waits=[("ach", uk)])
    vk_inv = em.vec(lambda e: e.tensor_scalar_mul(inv32[:, :], inv32[:, :], SQRT128))
    kstat_free = vk_inv

    if trunc == "kstat":
        em.finalize([("vector", "vch", em.cnt["vch"])])
        return nc
    kn_store_p = [None, None]
    prev_vb = vk_inv
    for h in range(NKV):
        for th in range(2):
            em.mm(lambda e, h=h, th=th:
                  e.matmul(bank(PS, 6 + th), hsels[:, 128 * h:128 * h + 128],
                           inv32[:, 512 * th:512 * th + 512], start=True, stop=True),
                  waits=[("vch", prev_vb)], inc=(th == 1))
        tb = em.cnt["tch"]
        vb = 0
        for th in range(2):
            vb = em.vec(lambda e, th=th: e.tensor_copy(
                invqb[:, 512 * th:512 * th + 512], bank(PS, 6 + th)),
                waits=[("tch", tb)])
        i2 = h % 2
        prev_vb = vb
        ev = em.vec(lambda e, h=h, i2=i2: e.scalar_tensor_tensor(
            hbuf[i2][:, :], kraw[:, TPC * h:TPC * h + TPC], knwc[:, 0:1],
            invqb[:, :], ALU.mult, ALU.mult),
            waits=[("vch", vb), kn_store_p[i2], l_knw])
        kn_store_p[i2] = em.dma("sync", kt[h, :, :], hbuf[i2][:, :], waits=[("vch", ev)], stream=f"kst{i2}")

    if trunc == "kproj":
        em.finalize([("vector", "vch", em.cnt["vch"])])
        return nc
    v_psum_free = em.tmark(waits=[("vch", em.cnt["vch"])])
    v_store_p = [kn_store_p[0], kn_store_p[1]]
    for ttg in range(2):
        wb_free = [em.cnt["tch"], em.cnt["tch"]]
        for i in range(32):
            i2 = i % 2
            wb = wv_b[i2]
            ld = em.dma("gpsimd", wb[:, :], wvt[128 * i:128 * i + 128, :],
                        waits=[("tch", wb_free[i2])], stream=f"wv{i2}")
            for tt in range(4):
                t_idx = 4 * ttg + tt
                for th in range(2):
                    wl = []
                    if tt == 0 and th == 0:
                        wl.append(ld)
                    if i == 0 and tt == 0 and th == 0:
                        wl.append(("tbar", v_psum_free))
                    mk = em.mm(lambda e, i=i, tt=tt, th=th, t_idx=t_idx, wb=wb:
                               e.matmul(bank(PS, 2 * tt + th),
                                        zt[:, TPC * i + 128 * t_idx:TPC * i + 128 * t_idx + 128],
                                        wb[:, 512 * th:512 * th + 512],
                                        start=(i == 0), stop=(i == 31)),
                               waits=wl, inc=(tt == 3 and th == 1))
            wb_free[i2] = mk
        tv = em.cnt["tch"]
        for tt in range(4):
            t_idx = 4 * ttg + tt
            i2 = tt % 2
            em.vec(lambda e, tt=tt, i2=i2: e.tensor_copy(
                hbuf[i2][:, 0:512], bank(PS, 2 * tt)),
                waits=[("tch", tv), v_store_p[i2]])
            ev2 = em.vec(lambda e, tt=tt, i2=i2: e.tensor_copy(
                hbuf[i2][:, 512:1024], bank(PS, 2 * tt + 1)))
            v_store_p[i2] = em.dma("sync", vv[128 * t_idx:128 * t_idx + 128, :],
                                   hbuf[i2][:, :], waits=[("vch", ev2)], stream=f"vst{i2}")
        v_psum_free = em.tmark(waits=[("vch", em.cnt["vch"])])

    if trunc == "v":
        em.finalize([("vector", "vch", em.cnt["vch"])])
        return nc
    q_psum_free = v_psum_free
    qstats_mark = kstats_mark
    qraw_store = [v_store_p[0], v_store_p[1]]
    for g in range(16):
        heads = (2 * g, 2 * g + 1)
        wb_free = [em.cnt["tch"], em.cnt["tch"]]
        for i in range(32):
            i2 = i % 2
            wb = wq_b[i2]
            ld = em.dma("gpsimd", wb[:, 0:256],
                        wqt[128 * i:128 * i + 128, HD * heads[0]:HD * heads[0] + 256],
                        waits=[("tch", wb_free[i2])], stream=f"wq{i2}")
            for hh in range(2):
                for th in range(2):
                    wl = []
                    if hh == 0 and th == 0:
                        wl.append(ld)
                    if i == 0 and hh == 0 and th == 0:
                        wl.append(("tbar", q_psum_free))
                    mk = em.mm(lambda e, hh=hh, th=th, i=i, wb=wb:
                               e.matmul(PS[0:128, (2 * hh + th) * 512:(2 * hh + th) * 512 + 512],
                                        wb[:, 128 * hh:128 * hh + 128],
                                        zt[:, TPC * i + 512 * th:TPC * i + 512 * th + 512],
                                        start=(i == 0), stop=(i == 31)),
                               waits=wl, inc=(hh == 1 and th == 1))
            wb_free[i2] = mk
        tdone = em.cnt["tch"]
        for hh, h in enumerate(heads):
            qev = em.vec(lambda e, hh=hh: e.tensor_copy(
                hbuf[hh][:, :],
                PS[0:128, (2 * hh) * 512:(2 * hh) * 512 + 1024]),
                waits=[("tch", tdone), qraw_store[hh]])
            sv = em.vec(lambda e, hh=hh: e.tensor_tensor(
                sqq[:, :], hbuf[hh][:, :], hbuf[hh][:, :], ALU.mult),
                waits=[("tch", qstats_mark)])
            for th in range(2):
                qstats_mark = em.mm(lambda e, h=h, th=th:
                      e.matmul(pstat[:, 512 * th:512 * th + 512],
                               qsels[:, 32 * h:32 * h + 32],
                               sqq[:, 512 * th:512 * th + 512],
                               start=(h == 0), stop=(h == 31)),
                      waits=[("vch", sv), ("vch", kstat_free)] if th == 0 else (),
                      inc=(th == 1)) or qstats_mark
            qraw_store[hh] = em.dma("sync", qraw_d[h, :, :], hbuf[hh][:, :],
                                    waits=[("vch", qev)], stream=f"qrst{hh}")
        q_psum_free = em.tmark(waits=[("vch", em.cnt["vch"])])
    t_qstats = em.cnt["tch"]

    vqs = em.vec(lambda e: e.tensor_copy(stq32[:, :], pstat[:, :]),
                 waits=[("tch", t_qstats)])
    uq = em.act(lambda e: e.activation(u32[:, :], stq32[:, :], AF.Sqrt,
                                       bias=epsr[0:32, 0:1]),
                waits=[("vch", vqs)])
    vq_inv = em.vec(lambda e: e.reciprocal(inv32[:, :], u32[:, :]), waits=[("ach", uq)])

    qn_store_p = [None, None]
    qn_read = [em.cnt["vch"], em.cnt["vch"]]
    prev_vb = vq_inv
    for h in range(NH):
        for th in range(2):
            em.mm(lambda e, h=h, th=th:
                  e.matmul(bank(PS, 6 + th), hsels[:, 128 * (h % 32):128 * (h % 32) + 128],
                           inv32[:, 512 * th:512 * th + 512], start=True, stop=True),
                  waits=[("vch", prev_vb)], inc=(th == 1))
        tb = em.cnt["tch"]
        vb = 0
        for th in range(2):
            vb = em.vec(lambda e, th=th: e.tensor_copy(
                invqb[:, 512 * th:512 * th + 512], bank(PS, 6 + th)),
                waits=[("tch", tb)])
        i2 = h % 2
        lr = em.dma("gpsimd", hbuf[i2][:, :], qraw_d[h, :, :],
                    waits=[("vch", qn_read[i2])] +
                          ([qraw_store[0], qraw_store[1]] if h < 2 else []),
                    stream=f"qlr{i2}")
        prev_vb = vb
        ev = em.vec(lambda e, h=h, i2=i2: e.scalar_tensor_tensor(
            qout[i2][:, :], hbuf[i2][:, :], qnwc[:, 0:1],
            invqb[:, :], ALU.mult, ALU.mult),
            waits=[("vch", vb), lr, l_qnw, qn_store_p[i2]])
        qn_read[i2] = ev
        qn_store_p[i2] = em.dma("sync", qt[h, :, :], qout[i2][:, :], waits=[("vch", ev)], stream=f"qst{i2}")

    em.finalize([("vector", "vch", em.cnt["vch"])])
    return nc




def build_l2(core_mod=None):
    nc = bass.Bass("TRN2", target_bir_lowering=False)
    ktall = nc.dram_tensor("ktall", [NKV, HD, S], BF16, kind="ExternalInput").ap()
    vpe = nc.dram_tensor("vpe", [NKV, 128, 64, 128], BF16, kind="ExternalInput").ap()
    qt = nc.dram_tensor("qt", [NH, HD, TPC], BF16, kind="ExternalInput").ap()
    x1t = nc.dram_tensor("x1t", [D, TPC], F32, kind="ExternalInput").ap()
    wot = nc.dram_tensor("wot", [D, D], F32, kind="ExternalInput").ap()
    wpt = nc.dram_tensor("wpt", [D // 2, D], U8, kind="ExternalInput").ap()
    zp32 = nc.dram_tensor("zp32", [32, 128], F32, kind="ExternalInput").ap()
    bf32 = nc.dram_tensor("bf32", [32, 128], F32, kind="ExternalInput").ap()
    sc32 = nc.dram_tensor("sc32", [32, 128], F32, kind="ExternalInput").ap()
    albl = nc.dram_tensor("albl", [NH, 6, TPC], BF16, kind="ExternalInput").ap()
    albr = nc.dram_tensor("albr", [NH, 6, S], BF16, kind="ExternalInput").ap()
    mtin = nc.dram_tensor("mtin", [128, 8 * 128], BF16, kind="ExternalInput").ap()
    identb = nc.dram_tensor("identb", [128, 128], BF16, kind="ExternalInput").ap()
    identf = nc.dram_tensor("identf", [128, 128], F32, kind="ExternalInput").ap()
    sel2 = nc.dram_tensor("sel2", [2, 256], F32, kind="ExternalInput").ap()
    onescol2 = nc.dram_tensor("onescol2", [128, 4], BF16, kind="ExternalInput").ap()

    yt = nc.dram_tensor("yt", [D, TPC], F32, kind="ExternalOutput").ap()
    o_park = nc.dram_tensor("o_park", [TPC, D], BF16, kind="Internal").ap()
    x2t = nc.dram_tensor("x2t", [D, TPC], F32, kind="Internal").ap()

    em = Em(nc)
    es = em.es
    sbf = lambda name, shape, dt: es.enter_context(nc.sbuf_tensor(name, shape, dt))
    PS = es.enter_context(nc.psum_tensor("PS", [128, 4096], F32))

    lb = LNBufs(em, sbf, PS)
    F = {}
    F["wpt"] = wpt
    F["wby"] = [sbf("wby0", [128, 512], U8), sbf("wby1", [128, 512], U8)]
    F["lo8"] = [sbf("lo80", [128, 512], U8), sbf("lo81", [128, 512], U8)]
    F["hi8"] = [sbf("hi80", [128, 512], U8), sbf("hi81", [128, 512], U8)]
    F["lo"] = [sbf("lo0", [128, 512], BF16), sbf("lo1", [128, 512], BF16)]
    F["hi"] = [sbf("hi0", [128, 512], BF16), sbf("hi1", [128, 512], BF16)]
    F["zpb2"] = sbf("zpb2", [2, D], BF16)
    F["zrow2"] = sbf("zrow2", [2, TPC], BF16)
    F["hs"] = sbf("hs", [128, 32], F32)
    F["x1b"] = [sbf("x1b0", [128, 512], F32), sbf("x1b1", [128, 512], F32)]
    F["x1o"] = [sbf("x1o0", [128, 512], F32), sbf("x1o1", [128, 512], F32)]
    F["ext_free"] = [0, 0]
    srow1b = sbf("srow1b", [1, TPC], BF16)
    zp32s = sbf("zp32s", [32, 128], F32)
    bf32s = sbf("bf32s", [32, 128], F32)
    nzp32 = sbf("nzp32", [32, 128], BF16)
    bs32 = sbf("bs32", [32, 128], BF16)
    rs32 = sbf("rs32", [32, 128], F32)
    sc32s = sbf("sc32s", [32, 128], F32)
    identfs = sbf("identfs", [128, 128], F32)
    identbs = sbf("identbs", [128, 128], BF16)
    ones1 = sbf("ones1", [128, 1], BF16)
    neginf = sbf("neginf", [128, 1], F32)
    mbuf = [sbf("mbuf0", [128, 1], F32), sbf("mbuf1", [128, 1], F32)]
    negm_b = sbf("negm_b", [128, 1], F32)
    lbuf = sbf("lbuf", [128, 1], F32)
    linv_b = sbf("linv_b", [128, 1], F32)
    osb = [sbf("osb0", [128, 128], BF16), sbf("osb1", [128, 128], BF16)]
    mt = sbf("mt", [128, 8 * 128], BF16)

    l_oc2 = em.dma("sync", lb.onescol2[:, :], onescol2[:, :])
    l_sel2 = em.dma("sync", lb.sel2[:, :], sel2[:, :])
    l_zpf = em.dma("sync", zp32s[:, :], zp32[:, :])
    l_bff = em.dma("sync", bf32s[:, :], bf32[:, :])
    l_sc32 = em.dma("sync", sc32s[:, :], sc32[:, :])
    l_idf = em.dma("sync", identfs[:, :], identf[:, :])
    l_idb = em.dma("sync", identbs[:, :], identb[:, :])
    l_mt = em.dma("sync", mt[:, :], mtin[:, :])
    em.vec(lambda e: e.memset(ones1[:, :], 1.0))
    lb.eps_ready = em.vec(lambda e: e.memset(lb.epsc[:, :], EPS_LN))
    em.vec(lambda e: e.memset(neginf[:, :], -3.0e38))
    vnzp = em.vec(lambda e: e.tensor_scalar_mul(nzp32[:, :], zp32s[:, :], -1.0),
                  waits=[l_zpf])
    em.vec(lambda e: e.reciprocal(rs32[:, :], sc32s[:, :]), waits=[l_sc32])
    vdiv = em.vec(lambda e: e.tensor_tensor(bs32[:, :], bf32s[:, :], rs32[:, :],
                                            ALU.mult), waits=[l_bff])
    d_zpb0 = em.dma("gpsimd", F["zpb2"][0:1, :], nzp32[:, :], waits=[("vch", vnzp)], stream="zpb0")
    d_zpb1 = em.dma("gpsimd", F["zpb2"][1:2, :], bs32[:, :], waits=[("vch", vdiv)], stream="zpb1")
    vone = em.vec(lambda e: e.memset(srow1b[0:1, 0:TPC], 1.0))
    F["zr1_done"] = em.dma("gpsimd", F["zrow2"][1:2, :], srow1b[0:1, 0:TPC],
                           waits=[("vch", vone), d_zpb0, d_zpb1], stream="zr1")
    t_hs = em.mm(lambda e: e.transpose(bank(PS, 7)[:, 0:32], sc32s[:, :], identfs[0:32, 0:32]),
                 waits=[l_idf, l_sc32], inc=True)
    v_hs = em.vec(lambda e: e.tensor_scalar_mul(F["hs"][:, :], bank(PS, 7)[:, 0:32], 0.5),
                  waits=[("tch", t_hs)])
    pro_gq = em.qbar("gq")

    es_a = ExitStack()
    sba = lambda name, shape, dt: es_a.enter_context(nc.sbuf_tensor(name, shape, dt))
    kth = sba("kth", [128, S], BF16)
    vh = sba("vh", [128, S], BF16)
    qth = sba("qth", [128, TPC], BF16)
    arh = sba("arh", [6, S], BF16)
    alh = sba("alh", [6, TPC], BF16)
    srow = sba("srow", [128, S], F32)
    prow = sba("prow", [128, S], BF16)
    ptrow = sba("ptrow", [128, S], BF16)

    kv_last_use = 0
    q_last_use = 0
    prev_exp = 0
    prev_recip = 0
    prev_avmark = 0
    o_evac_prev = 0
    ost_p = [None, None]
    tr_last = {}
    psum_free = [v_hs] * 4
    combo_i = 0

    for h in range(NKV):
        ld_k = em.dma("gpsimd", kth[:, :], ktall[h, :, :],
                      waits=[("tch", kv_last_use)], stream="kth")
        ld_v = em.dma("gpsimd", vh[:, :], vpe[h, :, :, :],
                      waits=[("tch", kv_last_use)], stream="vh")
        for qh in range(4 * h, 4 * h + 4):
            ld_q = em.dma("gpsimd", qth[:, :], qt[qh, :, :],
                          waits=[("tch", q_last_use)], stream="qth")
            ld_ar = em.dma("gpsimd", arh[:, :], albr[qh, :, :],
                           waits=[("tch", q_last_use)], stream="arh")
            ld_al = em.dma("gpsimd", alh[:, :], albl[qh, :, :],
                           waits=[("tch", q_last_use)], stream="alh")
            for T in range(8):
                nch = 2 * (T + 1)
                ch_mark = {}
                for k in range(nch):
                    b = k % 4
                    wl = [("vch", psum_free[b])]
                    if k == 0:
                        wl += [ld_k, ld_v, ld_q, ld_ar, ld_al, l_mt, l_idb,
                               ("ach", prev_exp)]
                    em.mm(lambda e, T=T, k=k, b=b:
                          e.matmul(bank(PS, b), alh[0:6, 128 * T:128 * T + 128],
                                   arh[0:6, 512 * k:512 * k + 512],
                                   start=True, stop=False),
                          waits=wl)
                    diag = (k >= 2 * T)
                    if diag:
                        for u in range(4):
                            blk = 4 * (k - 2 * T) + u
                            em.mm(lambda e, b=b, u=u, blk=blk:
                                  e.matmul(bank(PS, b)[:, 128 * u:128 * u + 128],
                                           identbs[:, :], mt[:, 128 * blk:128 * blk + 128],
                                           start=False, stop=False))
                    mk = em.mm(lambda e, T=T, k=k, b=b:
                               e.matmul(bank(PS, b), qth[:, 128 * T:128 * T + 128],
                                        kth[:, 512 * k:512 * k + 512],
                                        start=False, stop=True),
                               inc=True)
                    ch_mark[k] = mk
                    ev = em.vec(lambda e, k=k, b=b, first=(k == 0):
                                e.tensor_scalar(srow[:, 512 * k:512 * k + 512],
                                                bank(PS, b), 0.0,
                                                neginf[:, 0:1] if first else mbuf[(k - 1) % 2][:, 0:1],
                                                ALU.add, ALU.max,
                                                accum_out=mbuf[k % 2][:, 0:1]),
                                waits=[("tch", mk)] + ([("ach", prev_exp)] if k == 0 else []))
                    psum_free[b] = ev
                L = 1024 * (T + 1)
                vneg = em.vec(lambda e, k=nch - 1: e.tensor_scalar_mul(
                    negm_b[:, :], mbuf[k % 2][:, :], -1.0))
                expv = em.act(lambda e, L=L: e.activation(
                    prow[:, 0:L], srow[:, 0:L], AF.Exp, bias=negm_b[:, 0:1],
                    accum_out=lbuf[:, 0:1]),
                    waits=[("vch", vneg), ("vch", prev_recip), ("tch", prev_avmark)])
                prev_exp = expv
                vrec = em.vec(lambda e: e.reciprocal(linv_b[:, :], lbuf[:, :]),
                              waits=[("ach", expv)])
                prev_recip = vrec
                nblk = 8 * (T + 1)
                trt = {}
                wvmark = {}
                av = None
                for wave in range(nblk // 8):
                    for st_i in range(8):
                        bb = 8 * wave + st_i
                        pw = wvmark[wave - 1] if wave > 0 else tr_last.get(st_i, 0)
                        trt[bb] = em.dmat("sync", ptrow[:, 128 * bb:128 * bb + 128],
                                          prow[:, 128 * bb:128 * bb + 128],
                                          waits=[("ach", expv), ("tch", pw)]
                                          + (list(pro_gq) if combo_i == 0 and bb == 0 else []),
                                          stream=f"tr{st_i}")
                    for st_i in range(8):
                        bb = 8 * wave + st_i
                        wl = [trt[bb]]
                        if bb == 0:
                            wl.append(("vch", o_evac_prev))
                        av = em.mm(lambda e, bb=bb, nblk=nblk:
                                   e.matmul(PS[0:128, 4 * 512:4 * 512 + 128],
                                            ptrow[:, 128 * bb:128 * bb + 128],
                                            vh[:, 128 * bb:128 * bb + 128],
                                            start=(bb == 0), stop=(bb == nblk - 1)),
                                   waits=wl, inc=(st_i == 7))
                    wvmark[wave] = av
                prev_avmark = av
                for st_i in range(min(8, nblk)):
                    tr_last[st_i] = av
                kv_last_use = av
                q_last_use = av
                par = combo_i % 2
                oev = em.vec(lambda e, par=par: e.tensor_scalar(
                    osb[par][:, :], PS[0:128, 4 * 512:4 * 512 + 128],
                    linv_b[:, 0:1], None, ALU.mult),
                    waits=[("tch", av), ("vch", vrec), ost_p[par]])
                o_evac_prev = oev
                ost_p[par] = em.dma("sync",
                                    o_park[128 * T:128 * T + 128,
                                           128 * qh:128 * qh + 128],
                                    osb[par][:, :], waits=[("vch", oev)],
                                    stream=f"ost{par}")
                combo_i += 1

    attn_done = [("ach", prev_exp), ("tch", prev_avmark),
                 ("vch", o_evac_prev), ost_p[0], ost_p[1]]
    es_a.close()

    es_w = ExitStack()
    sbw = lambda name, shape, dt: es_w.enter_context(nc.sbuf_tensor(name, shape, dt))
    zt = sbw("zt", [128, 32 * TPC], BF16)
    wob = [sbw("wob0", [128, 512], BF16), sbw("wob1", [128, 512], BF16)]
    otb = [sbw("otb00", [128, 512], BF16), sbw("otb01", [128, 512], BF16),
           sbw("otb10", [128, 512], BF16), sbw("otb11", [128, 512], BF16)]
    x2b = [sbw("x2b0", [128, 512], F32), sbw("x2b1", [128, 512], F32)]
    x2o = [sbw("x2o0", [128, 512], F32), sbw("x2o1", [128, 512], F32)]

    wo_psum_free = [o_evac_prev] * 8
    wob_free = [0, 0]
    otb_free = [0, 0, 0, 0]
    x2b_free = [em.cnt["vch"], em.cnt["vch"]]
    x2o_free = [None, None]
    x2_last = None
    for og in range(8):
        for dc in range(32):
            i2 = dc % 2
            wb = wob[i2]
            lw = em.dma("gpsimd", wb[:, :],
                        wot[128 * dc:128 * dc + 128, 512 * og:512 * og + 512],
                        waits=[("tch", wob_free[i2])] +
                              (attn_done if (og == 0 and dc == 0) else []),
                        stream=f"wob{i2}")
            ots = []
            for th in range(2):
                oi = 2 * (dc % 2) + th
                lo_ = em.dmat("sync", otb[oi][:, :],
                              o_park[512 * th:512 * th + 512, 128 * dc:128 * dc + 128],
                              waits=[("tch", otb_free[oi])] +
                                    (attn_done if (og == 0 and dc < 2) else []),
                              stream=f"ot{oi}")
                ots.append(lo_)
            for th in range(2):
                for oc in range(4):
                    psi = 4 * th + oc
                    oi = 2 * (dc % 2) + th
                    wl = []
                    if oc == 0:
                        wl = [lw, ots[th]]
                    if dc == 0 and th == 0 and oc == 0:
                        wl.append(("vch", max(wo_psum_free)))
                    mk = em.mm(lambda e, psi=psi, oc=oc, oi=oi, wb=wb, dc=dc:
                               e.matmul(bank(PS, psi), wb[:, 128 * oc:128 * oc + 128],
                                        otb[oi][:, :], start=(dc == 0), stop=(dc == 31)),
                               waits=wl, inc=(th == 1 and oc == 3))
            wob_free[i2] = mk
            otb_free[2 * (dc % 2)] = mk
            otb_free[2 * (dc % 2) + 1] = mk
        tm = em.cnt["tch"]
        for th in range(2):
            for oc in range(4):
                psi = 4 * th + oc
                oco = 4 * og + oc
                i2 = psi % 2
                lx = em.dma("gpsimd", x2b[i2][:, :],
                            x1t[128 * oco:128 * oco + 128, 512 * th:512 * th + 512],
                            waits=[("vch", x2b_free[i2])], stream=f"x2l{i2}")
                ev = em.vec(lambda e, psi=psi, i2=i2: e.tensor_tensor(
                    x2o[i2][:, :], bank(PS, psi), x2b[i2][:, :], ALU.add),
                    waits=[("tch", tm), lx, x2o_free[i2]])
                wo_psum_free[psi] = ev
                x2b_free[i2] = ev
                x2_last = em.dma("sync", x2t[128 * oco:128 * oco + 128,
                                             512 * th:512 * th + 512],
                                 x2o[i2][:, :], waits=[("vch", ev)], stream=f"x2s{i2}")
                x2o_free[i2] = x2_last

    z3 = emit_ln_z(em, lb, x2t, zt, prior_psum_free=em.cnt["vch"],
                   gate_dma=em.qbar("sq"))
    sz3 = emit_sumz(em, lb, zt, F["zrow2"], ones1, sumz_gate=z3)
    F["xin_gate"] = em.qbar("sq")
    emit_ffn(em, lb, F, zt, z3, sz3, x2t, yt)

    es_w.close()
    em.finalize([("vector", "vch", em.cnt["vch"])])
    return nc


def _consts_l1(inputs):
    Wq = np.asarray(inputs["L0_attn_Wq"])
    Wk = np.asarray(inputs["L0_attn_Wk"])
    Wv = np.asarray(inputs["L0_attn_Wv"])
    wp = np.asarray(inputs["L0_ffn_weight_packed"])
    scale = np.asarray(inputs["L0_ffn_scale"])[:, 0]
    zp = np.asarray(inputs["L0_ffn_zero_point"])[:, 0]
    biasf = np.asarray(inputs["L0_ffn_bias"])
    qn = np.asarray(inputs["L0_attn_q_norm_w"])
    kn = np.asarray(inputs["L0_attn_k_norm_w"])

    wqt = np.ascontiguousarray(Wq.T[PI])
    wkt = np.ascontiguousarray(Wk.T[PI])
    wvt = np.ascontiguousarray(Wv.T[PI])
    bytes_ = np.ascontiguousarray(wp).view(np.uint8).reshape(-1, 4)[:, 0]
    wpt = np.ascontiguousarray(bytes_.reshape(D, D // 2).T[:, PI])
    scp = np.ascontiguousarray(scale[PI])
    zpp = np.ascontiguousarray(zp[PI])
    bfp = np.ascontiguousarray(biasf[PI])

    sel2 = np.zeros((2, 256), np.float32)
    sel2[0, 0:128] = 1.0
    sel2[1, 128:256] = 1.0
    onescol2 = np.zeros((128, 4), BF)
    onescol2[:, 0] = 1
    onescol2[:, 3] = 1
    identf = np.eye(128, dtype=np.float32)
    qsel = np.zeros((128, 32 * 32), BF)
    for h in range(32):
        qsel[:, 32 * h + h] = 1
    hsel = np.zeros((32, 32 * 128), np.float32)
    for h in range(32):
        hsel[h, 128 * h:128 * h + 128] = 1

    return dict(wqt=wqt, wkt=wkt, wvt=wvt, wpt=wpt,
                zp32=np.ascontiguousarray(zpp.reshape(32, 128)),
                bf32=np.ascontiguousarray(bfp.reshape(32, 128)),
                sc32=np.ascontiguousarray(scp.reshape(32, 128)),
                sel2=sel2, onescol2=onescol2, identf=identf,
                qnw=np.ascontiguousarray(qn[None, :]),
                knw=np.ascontiguousarray(kn[None, :]),
                qselt=qsel, hsel=hsel)


def stage_l1(inputs):
    x = np.asarray(inputs["L0_attn_x"])[0]
    shared = _consts_l1(inputs)
    in_maps = []
    for c in range(NC):
        m = dict(shared)
        m["xt"] = np.ascontiguousarray(x[c::NC].T[PI])
        in_maps.append(m)
    return in_maps


_CACHE = {}


def run_l1(inputs):
    in_maps = stage_l1(inputs)
    if "l1" not in _CACHE:
        _CACHE["l1"] = build_l1()
    res = run_bass_kernel_spmd(_CACHE["l1"], in_maps, core_ids=list(range(NC)))
    return res.results


def stage_l2(inputs, r1):
    Wo = np.asarray(inputs["L0_attn_Wo"])
    wot = np.ascontiguousarray(Wo.T[:, PI])
    c1 = _consts_l1(inputs)

    ktall = np.empty((NKV, HD, S), BF)
    vall = np.empty((NKV, S, HD), BF)
    for c in range(NC):
        ktall[:, :, c::NC] = np.asarray(r1[c]["kt"]).reshape(NKV, HD, TPC)
        vall[:, c::NC, :] = np.asarray(r1[c]["vv"]).reshape(TPC, NKV, HD).transpose(1, 0, 2)
    vpe = np.ascontiguousarray(vall.reshape(NKV, 64, 128, HD).transpose(0, 2, 1, 3))

    slopes = _alibi_slopes(NH)
    g = np.arange(S, dtype=np.float64)
    albr = np.zeros((NH, 6, S), BF)
    for qh in range(NH):
        s0, s1, s2 = _split3(slopes[qh] * g)
        albr[qh, 0] = s0
        albr[qh, 1] = s1
        albr[qh, 2] = s2
        albr[qh, 3:6] = -1.0

    identb = np.eye(128, dtype=BF)

    shared = dict(wot=wot, wpt=c1["wpt"], zp32=c1["zp32"], bf32=c1["bf32"],
                  sc32=c1["sc32"], albr=albr, identb=identb,
                  identf=c1["identf"], sel2=c1["sel2"], onescol2=c1["onescol2"])
    in_maps = []
    jj = np.arange(TPC, dtype=np.float64)
    ii = np.arange(128)
    for c in range(NC):
        m = dict(shared)
        m["ktall"] = ktall
        m["vpe"] = vpe
        m["qt"] = np.asarray(r1[c]["qt"]).reshape(NH, HD, TPC)
        m["x1t"] = np.asarray(r1[c]["x1t"])
        p = 8.0 * jj + c
        albl = np.zeros((NH, 6, TPC), BF)
        for qh in range(NH):
            s0, s1, s2 = _split3(slopes[qh] * p)
            albl[qh, 0:3] = 1.0
            albl[qh, 3] = s0
            albl[qh, 4] = s1
            albl[qh, 5] = s2
        m["albl"] = albl
        mt = np.zeros((128, 8 * 128), np.float32)
        for bb in range(8):
            allowed = (128 * bb + ii[None, :]) <= (8 * np.arange(128)[:, None] + c)
            mt[:, 128 * bb:128 * bb + 128] = np.where(allowed, 0.0, NEG_BIG)
        m["mtin"] = mt.astype(BF)
        in_maps.append(m)
    return in_maps


def kernel(**inputs):
    in1 = stage_l1(inputs)
    if "l1" not in _CACHE:
        _CACHE["l1"] = build_l1()
    r1 = run_bass_kernel_spmd(_CACHE["l1"], in1, core_ids=list(range(NC))).results
    in2 = stage_l2(inputs, r1)
    if "l2" not in _CACHE:
        _CACHE["l2"] = build_l2()
    r2 = run_bass_kernel_spmd(_CACHE["l2"], in2, core_ids=list(range(NC))).results
    out = np.empty((S, D), np.float32)
    for c in range(NC):
        xc = np.empty((TPC, D), np.float32)
        xc[:, PI] = np.asarray(r2[c]["yt"]).T
        out[c::NC] = xc
    return out[None, :, :]


# version 02
# speedup vs baseline: 1.0547x; 1.0547x over previous
import math
import numpy as np
import ml_dtypes
from contextlib import ExitStack

import concourse.bass as bass
import concourse.mybir as mybir
from concourse.bass_utils import run_bass_kernel_spmd

F32 = mybir.dt.float32
BF16 = mybir.dt.bfloat16
U8 = mybir.dt.uint8
AF = mybir.ActivationFunctionType
ALU = mybir.AluOpType
BF = ml_dtypes.bfloat16

NC = 8
S = 8192
D = 4096
TPC = S // NC
NH = 32
NKV = 8
HD = 128
EPS_LN = 1e-5
EPS_RMS = 1e-6
NEG_BIG = -1.0e30
SQRT128 = float(np.sqrt(128.0))

PI = np.arange(D).reshape(D // 256, 128, 2).transpose(0, 2, 1).reshape(-1)

ENGINES = ("sync", "tensor", "vector", "scalar", "gpsimd")


def _alibi_slopes(n_heads):
    start = 2 ** (-(2 ** -(math.log2(n_heads) - 3)))
    return np.array([start * start ** i for i in range(n_heads)], dtype=np.float64)


def _split3(v):
    s0 = v.astype(BF).astype(np.float64)
    s1 = (v - s0).astype(BF).astype(np.float64)
    s2 = (v - s0 - s1).astype(BF)
    return s0.astype(BF), s1.astype(BF), s2


class Em:

    def __init__(self, nc):
        self.nc = nc
        self.es = ExitStack()
        self.ops = {e: [] for e in ENGINES}
        self.sems = {}
        self.cnt = {}
        for name in ("vch", "ach", "pch", "tch", "tbar"):
            self.sems[name] = self.es.enter_context(nc.semaphore(name))
            self.cnt[name] = 0
        self.qstreams = {"sq": set(), "gq": set(), "aq": set()}
        self.auto_i = 0

    def _app(self, eng, fn):
        self.ops[eng].append(fn)

    def _waits(self, waits):
        out = []
        for w in waits:
            if w is None:
                continue
            (n, v) = w
            if v > 0:
                out.append((self.sems[n], v))
        return tuple(out)

    def wait(self, eng, sem_name, val):
        s = self.sems[sem_name]
        self._app(eng, lambda e, s=s, v=val: e.wait_ge(s, v))

    def _chain(self, eng, ch, fn, waits=()):
        self.cnt[ch] += 1
        tgt = self.cnt[ch]
        sem = self.sems[ch]
        w = self._waits(waits)

        def run(e, fn=fn, sem=sem, tgt=tgt, w=w):
            for (ws, wv) in w:
                e.wait_ge(ws, wv)
            if tgt > 1:
                e.wait_ge(sem, tgt - 1)
            fn(e).then_inc(sem, 1)

        self._app(eng, run)
        return tgt

    def vec(self, fn, waits=()):
        return self._chain("vector", "vch", fn, waits)

    def act(self, fn, waits=()):
        return self._chain("scalar", "ach", fn, waits)

    def pool(self, fn, waits=()):
        return self._chain("gpsimd", "pch", fn, waits)

    def mm(self, fn, waits=(), inc=False):
        w = self._waits(waits)
        if inc:
            self.cnt["tch"] += 1
            tgt = self.cnt["tch"]
            sem = self.sems["tch"]

            def run(e, fn=fn, w=w, sem=sem):
                for (ws, wv) in w:
                    e.wait_ge(ws, wv)
                fn(e).then_inc(sem, 1)

            self._app("tensor", run)
            return tgt
        else:
            def run(e, fn=fn, w=w):
                for (ws, wv) in w:
                    e.wait_ge(ws, wv)
                fn(e)

            self._app("tensor", run)
            return None

    def tmark(self, waits=()):
        w = self._waits(waits)
        self.cnt["tbar"] += 1
        tgt = self.cnt["tbar"]
        sem = self.sems["tbar"]

        def run(e, w=w, sem=sem):
            for (ws, wv) in w:
                e.wait_ge(ws, wv)
            e.nop().then_inc(sem, 1)

        self._app("tensor", run)
        return tgt

    def _dma_emit(self, queue, q, stream, fn, waits):
        if stream is None:
            stream = f"auto{self.auto_i}"
            self.auto_i += 1
        n = f"{q}_{stream}"
        if n not in self.sems:
            self.sems[n] = self.es.enter_context(self.nc.semaphore(n))
            self.cnt[n] = 0
            self.qstreams[q].add(n)
        self.cnt[n] += 16
        tgt = self.cnt[n]
        sem = self.sems[n]
        w = self._waits(waits)

        def run(e, fn=fn, sem=sem, w=w):
            for (ws, wv) in w:
                e.wait_ge(ws, wv)
            fn(e).then_inc(sem, 16)

        self._app(queue, run)
        return (n, tgt)

    def dma(self, queue, out, in_, waits=(), stream=None, **kw):
        q = {"sync": "sq", "gpsimd": "gq", "scalar": "aq"}[queue]
        return self._dma_emit(queue, q, stream,
                              lambda e, out=out, in_=in_, kw=kw:
                              e.dma_start(out=out, in_=in_, **kw), waits)

    def dmat(self, queue, out, in_, waits=(), stream=None):
        q = {"sync": "sq", "scalar": "aq"}[queue]
        return self._dma_emit(queue, q, stream,
                              lambda e, out=out, in_=in_:
                              e.dma_start_transpose(out, in_), waits)

    def qbar(self, q):
        return tuple((n, self.cnt[n]) for n in sorted(self.qstreams[q])
                     if self.cnt[n] > 0)

    def finalize(self, final_waits=()):
        for (eng, sem_name, val) in final_waits:
            self.wait(eng, sem_name, val)
        for (eng, q) in (("sync", "sq"), ("gpsimd", "gq"), ("scalar", "aq")):
            for (n, v) in self.qbar(q):
                self.wait(eng, n, v)
        block = self.es.enter_context(self.nc.Block())
        for eng in ENGINES:
            ops = self.ops[eng]
            if not ops:
                continue
            getattr(block, eng)(lambda e, ops=ops: [f(e) for f in ops])
        self.es.close()



class LNBufs:

    def __init__(self, em, sbf, PS):
        self.xb = [sbf("ln_xb0", [128, TPC], BF16), sbf("ln_xb1", [128, TPC], BF16)]
        self.xf = [sbf("ln_xf0", [128, TPC], F32), sbf("ln_xf1", [128, TPC], F32)]
        self.sq = [sbf("ln_sq0", [128, TPC], BF16), sbf("ln_sq1", [128, TPC], BF16)]
        self.sxs = sbf("ln_sxs", [2, 512], F32)
        self.sqs = sbf("ln_sqs", [2, 512], F32)
        self.m2 = sbf("ln_m2", [2, 512], F32)
        self.var = sbf("ln_var", [2, 512], F32)
        self.u2 = sbf("ln_u2", [2, 512], F32)
        self.inv = sbf("ln_inv", [2, 512], F32)
        self.minv = sbf("ln_minv", [2, 512], F32)
        self.tmp2 = sbf("ln_tmp2", [2, 512], F32)
        self.invb = sbf("ln_invb", [128, TPC], F32)
        self.minvb = sbf("ln_minvb", [128, TPC], F32)
        self.t1 = sbf("ln_t1", [128, TPC], F32)
        self.epsc = sbf("ln_epsc", [128, 1], F32)
        self.onescol2 = sbf("ln_oc2", [128, 4], BF16)
        self.sel2 = sbf("ln_sel2", [2, 256], F32)
        self.PS = PS


def bank(PS, i, rows=128, cols=512):
    return PS[0:rows, 512 * i:512 * i + cols]


def emit_ln_z(em, lb, xsrc_dram, zt, prior_psum_free=0, gate_dma=()):
    PS = lb.PS
    stx = bank(PS, 0, rows=2)
    stq = bank(PS, 1, rows=2)
    t0 = em.cnt["tch"]
    v0 = em.cnt["vch"]
    a0 = em.cnt["ach"]
    xb_free = [t0, t0]
    sq_free = [t0, t0]
    for c in range(32):
        i = c % 2
        xb = lb.xb[i]
        ld = em.dma("gpsimd", xb[:, :], xsrc_dram[128 * c:128 * c + 128, :],
                    waits=[("tch", xb_free[i])] + (list(gate_dma) if c < 2 else []),
                    stream=f"lnxb{i}")
        sqt = lb.sq[i]
        sv = em.act(lambda e, xb=xb, sqt=sqt: e.activation(sqt[:, :], xb[:, :], AF.Square),
                    waits=[ld, ("tch", sq_free[i])])
        for h in range(2):
            w0 = []
            if c == 0 and h == 0:
                w0 = [("vch", prior_psum_free)] + list(gate_dma)
            em.mm(lambda e, h=h, c=c, xb=xb:
                  e.matmul(stx[:, :], lb.onescol2[:, 2 * h:2 * h + 2],
                           xb[:, 512 * h:512 * h + 512],
                           start=(c == 0 and h == 0), stop=(c == 31 and h == 1)),
                  waits=w0 + ([ld] if h == 0 else []))
            mk = em.mm(lambda e, h=h, c=c, sqt=sqt:
                       e.matmul(stq[:, :], lb.onescol2[:, 2 * h:2 * h + 2],
                                sqt[:, 512 * h:512 * h + 512],
                                start=(c == 0 and h == 0), stop=(c == 31 and h == 1)),
                       waits=[("ach", sv)] if h == 0 else [], inc=(h == 1))
        xb_free[i] = mk
        sq_free[i] = mk

    tdone = em.cnt["tch"]
    em.vec(lambda e: e.tensor_copy(lb.sxs[:, :], stx[:, :]), waits=[("tch", tdone)])
    em.vec(lambda e: e.tensor_copy(lb.sqs[:, :], stq[:, :]))
    em.vec(lambda e: e.tensor_scalar_mul(lb.m2[:, :], lb.sxs[:, :], 1.0 / D))
    em.vec(lambda e: e.tensor_scalar_mul(lb.var[:, :], lb.sqs[:, :], 1.0 / D))
    em.vec(lambda e: e.tensor_tensor(lb.tmp2[:, :], lb.m2[:, :], lb.m2[:, :], ALU.mult))
    vv = em.vec(lambda e: e.tensor_tensor(lb.var[:, :], lb.var[:, :], lb.tmp2[:, :], ALU.subtract))
    uval = em.act(lambda e: e.activation(lb.u2[:, :], lb.var[:, :], AF.Sqrt,
                                         bias=lb.epsc[0:2, 0:1]),
                  waits=[("vch", vv), ("vch", lb.eps_ready)])
    em.vec(lambda e: e.reciprocal(lb.inv[:, :], lb.u2[:, :]), waits=[("ach", uval)])
    vready = em.vec(lambda e: e.tensor_tensor(lb.minv[:, :], lb.m2[:, :], lb.inv[:, :], ALU.mult))

    prev_evac = 0
    bdone = 0
    for (vecs, dst) in ((lb.inv, lb.invb), (lb.minv, lb.minvb)):
        for h in range(2):
            em.mm(lambda e, vecs=vecs, h=h:
                  e.matmul(bank(PS, 2 + h), lb.sel2[:, 128 * h:128 * h + 128],
                           vecs[:, :], start=True, stop=True),
                  waits=[("vch", vready), ("vch", prev_evac)], inc=(h == 1))
        t = em.cnt["tch"]
        for h in range(2):
            bdone = em.vec(lambda e, dst=dst, h=h:
                           e.tensor_copy(dst[:, 512 * h:512 * h + 512], bank(PS, 2 + h)),
                           waits=[("tch", t)])
        prev_evac = bdone

    xf_free = [v0, v0]
    zready = 0
    for c in range(32):
        i = c % 2
        xf = lb.xf[i]
        ld = em.dma("sync", xf[:, :], xsrc_dram[128 * c:128 * c + 128, :],
                    waits=[("vch", xf_free[i])] + (list(gate_dma) if c < 2 else []),
                    stream=f"lnxf{i}")
        em.vec(lambda e, xf=xf: e.tensor_tensor(lb.t1[:, :], xf[:, :], lb.invb[:, :], ALU.mult),
               waits=[ld, ("vch", bdone)])
        zready = em.vec(lambda e, c=c: e.tensor_tensor(
            zt[:, TPC * c:TPC * c + TPC], lb.t1[:, :], lb.minvb[:, :], ALU.subtract))
        xf_free[i] = zready
    return zready


def emit_sumz(em, lb, zt, zrow2, ones1, sumz_gate=0):
    PS = lb.PS
    t = None
    for h in range(2):
        ps = bank(PS, h, rows=1)
        for c in range(32):
            t = em.mm(lambda e, ps=ps, c=c, h=h:
                      e.matmul(ps[:, :], ones1[:, :],
                               zt[:, TPC * c + 512 * h:TPC * c + 512 * h + 512],
                               start=(c == 0), stop=(c == 31)),
                      waits=[("vch", sumz_gate)] if (c == 0 and h == 0) else (),
                      inc=(h == 1 and c == 31))
    v = 0
    for h in range(2):
        v = em.vec(lambda e, h=h: e.tensor_copy(zrow2[0:1, 512 * h:512 * h + 512],
                                                bank(PS, h, rows=1)),
                   waits=[("tch", t)])
    return v


def emit_ffn(em, lb, F, zt, z_ready, sumz_v, xin_dram, xout_dram):
    PS = lb.PS
    v0 = em.cnt["vch"]
    t0 = em.cnt["tch"]
    psum_free = [v0] * 8
    wby_free = [v0, v0]
    x1b_free = [v0, v0]
    x1o_free = [None, None]
    last_store = None
    F["ext_free"] = [t0, t0]
    for og in range(8):
        grp_started = False
        for bc in range(16):
            i2 = bc % 2
            wb = F["wby"][i2]
            ld = em.dma("sync", wb[:, :],
                        F["wpt"][128 * bc:128 * bc + 128, 512 * og:512 * og + 512],
                        waits=[("vch", wby_free[i2])], stream=f"wby{i2}")
            lo8, hi8 = F["lo8"][i2], F["hi8"][i2]
            lo, hi = F["lo"][i2], F["hi"][i2]
            em.vec(lambda e, wb=wb, lo8=lo8: e.tensor_scalar(
                lo8[:, :], wb[:, :], 15, None, ALU.bitwise_and),
                waits=[ld, ("tch", F["ext_free"][i2])])
            em.vec(lambda e, wb=wb, hi8=hi8: e.tensor_scalar(
                hi8[:, :], wb[:, :], 4, None, ALU.logical_shift_right))
            em.vec(lambda e, lo8=lo8, lo=lo: e.tensor_copy(lo[:, :], lo8[:, :]))
            vex = em.vec(lambda e, hi8=hi8, hi=hi: e.tensor_copy(hi[:, :], hi8[:, :]))
            wby_free[i2] = vex
            for par, wsb in ((0, lo), (1, hi)):
                ic = 2 * bc + par
                for th in range(2):
                    for oc in range(4):
                        psi = 4 * th + oc
                        wl = []
                        if bc == 0 and par == 0:
                            if th == 0 and oc == 0:
                                wl = [("vch", max(z_ready, sumz_v, vex)),
                                      F["zr1_done"]]
                            wl.append(("vch", psum_free[psi]))
                        elif par == 0 and th == 0 and oc == 0:
                            wl = [("vch", vex)]
                        mk = em.mm(lambda e, psi=psi, wsb=wsb, oc=oc, ic=ic, th=th,
                                   s=(bc == 0 and par == 0):
                                   e.matmul(bank(PS, psi),
                                            wsb[:, 128 * oc:128 * oc + 128],
                                            zt[:, TPC * ic + 512 * th:TPC * ic + 512 * th + 512],
                                            start=s, stop=False),
                                   waits=wl,
                                   inc=(par == 1 and th == 1 and oc == 3))
            F["ext_free"][i2] = mk
        for th in range(2):
            for oc in range(4):
                psi = 4 * th + oc
                oco = 4 * og + oc
                tm = em.mm(lambda e, psi=psi, oco=oco, th=th:
                           e.matmul(bank(PS, psi),
                                    F["zpb2"][:, 128 * oco:128 * oco + 128],
                                    F["zrow2"][:, 512 * th:512 * th + 512],
                                    start=False, stop=True),
                           inc=True)
                i2 = psi % 2
                xb_ = F["x1b"][i2]
                xo_ = F["x1o"][i2]
                ldx = em.dma("gpsimd", xb_[:, :],
                             xin_dram[128 * oco:128 * oco + 128, 512 * th:512 * th + 512],
                             waits=[("vch", x1b_free[i2])] + list(F["xin_gate"]),
                             stream=f"ffnx{i2}")
                ev = em.vec(lambda e, psi=psi, oco=oco, xb_=xb_, xo_=xo_:
                            e.scalar_tensor_tensor(xo_[:, :], bank(PS, psi),
                                                   F["hs"][:, oco:oco + 1],
                                                   xb_[:, :], ALU.mult, ALU.add),
                            waits=[("tch", tm), ldx, x1o_free[i2]])
                psum_free[psi] = ev
                x1b_free[i2] = ev
                last_store = em.dma("sync",
                                    xout_dram[128 * oco:128 * oco + 128,
                                              512 * th:512 * th + 512],
                                    xo_[:, :], waits=[("vch", ev)],
                                    stream=f"ffno{i2}")
                x1o_free[i2] = last_store
    return last_store



def build_l1(trunc=None):
    nc = bass.Bass("TRN2", target_bir_lowering=False)
    xt = nc.dram_tensor("xt", [D, TPC], F32, kind="ExternalInput").ap()
    wqt = nc.dram_tensor("wqt", [D, D], F32, kind="ExternalInput").ap()
    wkt = nc.dram_tensor("wkt", [D, NKV * HD], F32, kind="ExternalInput").ap()
    wvt = nc.dram_tensor("wvt", [D, NKV * HD], F32, kind="ExternalInput").ap()
    wpt = nc.dram_tensor("wpt", [D // 2, D], U8, kind="ExternalInput").ap()
    zp32 = nc.dram_tensor("zp32", [32, 128], F32, kind="ExternalInput").ap()
    bf32 = nc.dram_tensor("bf32", [32, 128], F32, kind="ExternalInput").ap()
    sc32 = nc.dram_tensor("sc32", [32, 128], F32, kind="ExternalInput").ap()
    qnw = nc.dram_tensor("qnw", [1, 128], F32, kind="ExternalInput").ap()
    knw = nc.dram_tensor("knw", [1, 128], F32, kind="ExternalInput").ap()
    sel2 = nc.dram_tensor("sel2", [2, 256], F32, kind="ExternalInput").ap()
    onescol2 = nc.dram_tensor("onescol2", [128, 4], BF16, kind="ExternalInput").ap()
    identf = nc.dram_tensor("identf", [128, 128], F32, kind="ExternalInput").ap()
    qselt = nc.dram_tensor("qselt", [128, 32 * 32], BF16, kind="ExternalInput").ap()
    hsel = nc.dram_tensor("hsel", [32, 32 * 128], F32, kind="ExternalInput").ap()

    qraw_d = nc.dram_tensor("qraw_d", [NH, HD, TPC], BF16, kind="Internal").ap()
    x1t = nc.dram_tensor("x1t", [D, TPC], F32, kind="ExternalOutput").ap()
    kt = nc.dram_tensor("kt", [NKV, HD, TPC], BF16, kind="ExternalOutput").ap()
    vv = nc.dram_tensor("vv", [TPC, NKV * HD], BF16, kind="ExternalOutput").ap()
    qt = nc.dram_tensor("qt", [NH, HD, TPC], BF16, kind="ExternalOutput").ap()

    em = Em(nc)
    es = em.es
    sbf = lambda name, shape, dt: es.enter_context(nc.sbuf_tensor(name, shape, dt))
    PS = es.enter_context(nc.psum_tensor("PS", [128, 4096], F32))

    lb = LNBufs(em, sbf, PS)
    zt = sbf("zt", [128, 32 * TPC], BF16)

    F = {}
    F["wpt"] = wpt
    F["wby"] = [sbf("wby0", [128, 512], U8), sbf("wby1", [128, 512], U8)]
    F["lo8"] = [sbf("lo80", [128, 512], U8), sbf("lo81", [128, 512], U8)]
    F["hi8"] = [sbf("hi80", [128, 512], U8), sbf("hi81", [128, 512], U8)]
    F["lo"] = [sbf("lo0", [128, 512], BF16), sbf("lo1", [128, 512], BF16)]
    F["hi"] = [sbf("hi0", [128, 512], BF16), sbf("hi1", [128, 512], BF16)]
    F["zpb2"] = sbf("zpb2", [2, D], BF16)
    F["zrow2"] = sbf("zrow2", [2, TPC], BF16)
    F["hs"] = sbf("hs", [128, 32], F32)
    F["x1b"] = [sbf("x1b0", [128, 512], F32), sbf("x1b1", [128, 512], F32)]
    F["x1o"] = [sbf("x1o0", [128, 512], F32), sbf("x1o1", [128, 512], F32)]
    F["ext_free"] = [0, 0]

    srow1b = sbf("srow1b", [1, TPC], BF16)
    zp32s = sbf("zp32s", [32, 128], F32)
    bf32s = sbf("bf32s", [32, 128], F32)
    nzp32 = sbf("nzp32", [32, 128], BF16)
    bs32 = sbf("bs32", [32, 128], BF16)
    rs32 = sbf("rs32", [32, 128], F32)
    sc32s = sbf("sc32s", [32, 128], F32)
    identfs = sbf("identfs", [128, 128], F32)
    ones1 = sbf("ones1", [128, 1], BF16)
    qsels = sbf("qsels", [128, 32 * 32], BF16)
    hsels = sbf("hsels", [32, 32 * 128], F32)
    sqq = sbf("sqq", [128, TPC], BF16)
    stq32 = sbf("stq32", [32, TPC], F32)
    u32 = sbf("u32", [32, TPC], F32)
    inv32 = sbf("inv32", [32, TPC], F32)
    invqb = sbf("invqb", [128, TPC], F32)
    epsr = sbf("epsr", [128, 1], F32)
    qnwc = sbf("qnwc", [128, 1], F32)
    knwc = sbf("knwc", [128, 1], F32)
    hbuf = [sbf("hbuf0", [128, TPC], BF16), sbf("hbuf1", [128, TPC], BF16)]
    qout = [sbf("qout0", [128, TPC], BF16), sbf("qout1", [128, TPC], BF16)]
    wk_b = [sbf("wkb0", [128, 256], BF16), sbf("wkb1", [128, 256], BF16)]
    wq_b = [sbf("wqb0", [128, 512], BF16), sbf("wqb1", [128, 512], BF16)]
    wv_b = [sbf("wvb0", [128, 1024], BF16), sbf("wvb1", [128, 1024], BF16)]

    l_oc2 = em.dma("sync", lb.onescol2[:, :], onescol2[:, :])
    l_sel2 = em.dma("sync", lb.sel2[:, :], sel2[:, :])
    l_zpf = em.dma("sync", zp32s[:, :], zp32[:, :])
    l_bff = em.dma("sync", bf32s[:, :], bf32[:, :])
    l_sc32 = em.dma("sync", sc32s[:, :], sc32[:, :])
    l_idf = em.dma("sync", identfs[:, :], identf[:, :])
    l_qsel = em.dma("sync", qsels[:, :], qselt[:, :])
    l_hsel = em.dma("sync", hsels[:, :], hsel[:, :])
    l_qnw = em.dma("sync", qnwc[:, 0:1], qnw.rearrange("1 p -> p 1"))
    l_knw = em.dma("sync", knwc[:, 0:1], knw.rearrange("1 p -> p 1"))
    em.vec(lambda e: e.memset(ones1[:, :], 1.0))
    lb.eps_ready = em.vec(lambda e: e.memset(lb.epsc[:, :], EPS_LN))
    v_epsr = em.vec(lambda e: e.memset(epsr[:, :], 128.0 * EPS_RMS))
    vnzp = em.vec(lambda e: e.tensor_scalar_mul(nzp32[:, :], zp32s[:, :], -1.0),
                  waits=[l_zpf])
    em.vec(lambda e: e.reciprocal(rs32[:, :], sc32s[:, :]),
           waits=[l_sc32])
    vdiv = em.vec(lambda e: e.tensor_tensor(bs32[:, :], bf32s[:, :], rs32[:, :],
                                            ALU.mult),
                  waits=[l_bff])
    d_zpb0 = em.dma("gpsimd", F["zpb2"][0:1, :], nzp32[:, :], waits=[("vch", vnzp)], stream="zpb0")
    d_zpb1 = em.dma("gpsimd", F["zpb2"][1:2, :], bs32[:, :], waits=[("vch", vdiv)], stream="zpb1")
    vone = em.vec(lambda e: e.memset(srow1b[0:1, 0:TPC], 1.0))
    F["zr1_done"] = em.dma("gpsimd", F["zrow2"][1:2, :], srow1b[0:1, 0:TPC],
                           waits=[("vch", vone), d_zpb0, d_zpb1], stream="zr1")
    t_hs = em.mm(lambda e: e.transpose(bank(PS, 7)[:, 0:32], sc32s[:, :], identfs[0:32, 0:32]),
                 waits=[l_idf, l_sc32], inc=True)
    v_hs = em.vec(lambda e: e.tensor_scalar_mul(F["hs"][:, :], bank(PS, 7)[:, 0:32], 0.5),
                  waits=[("tch", t_hs)])

    if trunc == "pro":
        em.finalize([("vector", "vch", em.cnt["vch"])])
        return nc
    gate_consts = em.qbar("sq")
    z1 = emit_ln_z(em, lb, xt, zt, prior_psum_free=v_hs, gate_dma=gate_consts)
    sz1 = emit_sumz(em, lb, zt, F["zrow2"], ones1, sumz_gate=z1)

    if trunc == "ln1":
        em.finalize([("vector", "vch", em.cnt["vch"])])
        return nc
    F["xin_gate"] = ()
    ffn1_sq = emit_ffn(em, lb, F, zt, z1, sz1, xt, x1t)

    if trunc == "ffn1":
        em.finalize([("vector", "vch", em.cnt["vch"])])
        return nc
    z2 = emit_ln_z(em, lb, x1t, zt, prior_psum_free=em.cnt["vch"], gate_dma=em.qbar("sq"))

    if trunc == "ln2":
        em.finalize([("vector", "vch", em.cnt["vch"])])
        return nc
    pstat = PS[0:32, 4 * 512:4 * 512 + 1024]

    kraw = sbf("kraw", [128, 8 * TPC], BF16)

    def stats_flags(h, th, first, total):
        return (h == first and th == 0, h == first + total - 1 and th == 1)

    kproj_psum_free = 0
    kstats_mark = 0
    ngroups = 1 if (trunc or "").startswith("kg0") else 4
    nostat = (trunc or "").endswith("nostat")
    nosq = (trunc or "").endswith("nosq")
    if (trunc or "").endswith("nokraw"):
        pass
    for g in range(ngroups):
        heads = (2 * g, 2 * g + 1)
        wb_free = [em.cnt["tch"], em.cnt["tch"]]
        for i in range(32):
            i2 = i % 2
            wb = wk_b[i2]
            ld = em.dma("gpsimd", wb[:, :],
                        wkt[128 * i:128 * i + 128, HD * heads[0]:HD * heads[0] + 256],
                        waits=[("tch", wb_free[i2])], stream=f"wk{i2}")
            for hh in range(2):
                for th in range(2):
                    wl = []
                    if hh == 0 and th == 0:
                        wl.append(ld)
                    if i == 0 and hh == 0 and th == 0:
                        wl.append(("vch", z2))
                        wl.append(("tbar", kproj_psum_free))
                    mk = em.mm(lambda e, hh=hh, th=th, i=i, wb=wb:
                               e.matmul(PS[0:128, (2 * hh + th) * 512:(2 * hh + th) * 512 + 512],
                                        wb[:, 128 * hh:128 * hh + 128],
                                        zt[:, TPC * i + 512 * th:TPC * i + 512 * th + 512],
                                        start=(i == 0), stop=(i == 31)),
                               waits=wl, inc=(hh == 1 and th == 1))
            wb_free[i2] = mk
        tdone = em.cnt["tch"]
        for hh, h in enumerate(heads):
            em.vec(lambda e, hh=hh, h=h: e.tensor_copy(
                kraw[:, TPC * h:TPC * h + TPC],
                PS[0:128, (2 * hh) * 512:(2 * hh) * 512 + 1024]),
                waits=[("tch", tdone)])
            sv = em.vec(lambda e, h=h: e.tensor_tensor(
                sqq[:, :], kraw[:, TPC * h:TPC * h + TPC],
                kraw[:, TPC * h:TPC * h + TPC], ALU.mult),
                waits=[("tch", kstats_mark)])
            for th in range(2):
                kstats_mark = em.mm(lambda e, h=h, th=th:
                      e.matmul(pstat[:, 512 * th:512 * th + 512],
                               qsels[:, 32 * h:32 * h + 32],
                               sqq[:, 512 * th:512 * th + 512],
                               start=(h == 0), stop=(h == 7 if ngroups == 4 else h == 1)),
                      waits=[("vch", sv)] if th == 0 else (), inc=(th == 1)) or kstats_mark
        kproj_psum_free = em.tmark(waits=[("vch", em.cnt["vch"])])
    t_kstats = em.cnt["tch"]

    if trunc is not None and trunc.startswith(("kgrp", "kg0")):
        em.finalize([("vector", "vch", em.cnt["vch"])])
        return nc
    vks = em.vec(lambda e: e.tensor_copy(stq32[:, :], pstat[:, :]),
                 waits=[("tch", t_kstats)])
    uk = em.act(lambda e: e.activation(u32[:, :], stq32[:, :], AF.Sqrt,
                                       bias=epsr[0:32, 0:1]),
                waits=[("vch", vks), ("vch", v_epsr)])
    em.vec(lambda e: e.reciprocal(inv32[:, :], u32[:, :]), waits=[("ach", uk)])
    vk_inv = em.vec(lambda e: e.tensor_scalar_mul(inv32[:, :], inv32[:, :], SQRT128))
    kstat_free = vk_inv

    if trunc == "kstat":
        em.finalize([("vector", "vch", em.cnt["vch"])])
        return nc
    kn_store_p = [None, None]
    prev_vb = vk_inv
    for h in range(NKV):
        for th in range(2):
            em.mm(lambda e, h=h, th=th:
                  e.matmul(bank(PS, 6 + th), hsels[:, 128 * h:128 * h + 128],
                           inv32[:, 512 * th:512 * th + 512], start=True, stop=True),
                  waits=[("vch", prev_vb)], inc=(th == 1))
        tb = em.cnt["tch"]
        vb = 0
        for th in range(2):
            vb = em.vec(lambda e, th=th: e.tensor_copy(
                invqb[:, 512 * th:512 * th + 512], bank(PS, 6 + th)),
                waits=[("tch", tb)])
        i2 = h % 2
        prev_vb = vb
        ev = em.vec(lambda e, h=h, i2=i2: e.scalar_tensor_tensor(
            hbuf[i2][:, :], kraw[:, TPC * h:TPC * h + TPC], knwc[:, 0:1],
            invqb[:, :], ALU.mult, ALU.mult),
            waits=[("vch", vb), kn_store_p[i2], l_knw])
        kn_store_p[i2] = em.dma("sync", kt[h, :, :], hbuf[i2][:, :], waits=[("vch", ev)], stream=f"kst{i2}")

    if trunc == "kproj":
        em.finalize([("vector", "vch", em.cnt["vch"])])
        return nc
    v_psum_free = em.tmark(waits=[("vch", em.cnt["vch"])])
    v_store_p = [kn_store_p[0], kn_store_p[1]]
    for ttg in range(2):
        wb_free = [em.cnt["tch"], em.cnt["tch"]]
        for i in range(32):
            i2 = i % 2
            wb = wv_b[i2]
            ld = em.dma("gpsimd", wb[:, :], wvt[128 * i:128 * i + 128, :],
                        waits=[("tch", wb_free[i2])], stream=f"wv{i2}")
            for tt in range(4):
                t_idx = 4 * ttg + tt
                for th in range(2):
                    wl = []
                    if tt == 0 and th == 0:
                        wl.append(ld)
                    if i == 0 and tt == 0 and th == 0:
                        wl.append(("tbar", v_psum_free))
                    mk = em.mm(lambda e, i=i, tt=tt, th=th, t_idx=t_idx, wb=wb:
                               e.matmul(bank(PS, 2 * tt + th),
                                        zt[:, TPC * i + 128 * t_idx:TPC * i + 128 * t_idx + 128],
                                        wb[:, 512 * th:512 * th + 512],
                                        start=(i == 0), stop=(i == 31)),
                               waits=wl, inc=(tt == 3 and th == 1))
            wb_free[i2] = mk
        tv = em.cnt["tch"]
        for tt in range(4):
            t_idx = 4 * ttg + tt
            i2 = tt % 2
            em.vec(lambda e, tt=tt, i2=i2: e.tensor_copy(
                hbuf[i2][:, 0:512], bank(PS, 2 * tt)),
                waits=[("tch", tv), v_store_p[i2]])
            ev2 = em.vec(lambda e, tt=tt, i2=i2: e.tensor_copy(
                hbuf[i2][:, 512:1024], bank(PS, 2 * tt + 1)))
            v_store_p[i2] = em.dma("sync", vv[128 * t_idx:128 * t_idx + 128, :],
                                   hbuf[i2][:, :], waits=[("vch", ev2)], stream=f"vst{i2}")
        v_psum_free = em.tmark(waits=[("vch", em.cnt["vch"])])

    if trunc == "v":
        em.finalize([("vector", "vch", em.cnt["vch"])])
        return nc
    q_psum_free = v_psum_free
    qstats_mark = kstats_mark
    qraw_store = [v_store_p[0], v_store_p[1]]
    for g in range(16):
        heads = (2 * g, 2 * g + 1)
        wb_free = [em.cnt["tch"], em.cnt["tch"]]
        for i in range(32):
            i2 = i % 2
            wb = wq_b[i2]
            ld = em.dma("gpsimd", wb[:, 0:256],
                        wqt[128 * i:128 * i + 128, HD * heads[0]:HD * heads[0] + 256],
                        waits=[("tch", wb_free[i2])], stream=f"wq{i2}")
            for hh in range(2):
                for th in range(2):
                    wl = []
                    if hh == 0 and th == 0:
                        wl.append(ld)
                    if i == 0 and hh == 0 and th == 0:
                        wl.append(("tbar", q_psum_free))
                    mk = em.mm(lambda e, hh=hh, th=th, i=i, wb=wb:
                               e.matmul(PS[0:128, (2 * hh + th) * 512:(2 * hh + th) * 512 + 512],
                                        wb[:, 128 * hh:128 * hh + 128],
                                        zt[:, TPC * i + 512 * th:TPC * i + 512 * th + 512],
                                        start=(i == 0), stop=(i == 31)),
                               waits=wl, inc=(hh == 1 and th == 1))
            wb_free[i2] = mk
        tdone = em.cnt["tch"]
        for hh, h in enumerate(heads):
            qev = em.vec(lambda e, hh=hh: e.tensor_copy(
                hbuf[hh][:, :],
                PS[0:128, (2 * hh) * 512:(2 * hh) * 512 + 1024]),
                waits=[("tch", tdone), qraw_store[hh]])
            sv = em.vec(lambda e, hh=hh: e.tensor_tensor(
                sqq[:, :], hbuf[hh][:, :], hbuf[hh][:, :], ALU.mult),
                waits=[("tch", qstats_mark)])
            for th in range(2):
                qstats_mark = em.mm(lambda e, h=h, th=th:
                      e.matmul(pstat[:, 512 * th:512 * th + 512],
                               qsels[:, 32 * h:32 * h + 32],
                               sqq[:, 512 * th:512 * th + 512],
                               start=(h == 0), stop=(h == 31)),
                      waits=[("vch", sv), ("vch", kstat_free)] if th == 0 else (),
                      inc=(th == 1)) or qstats_mark
            qraw_store[hh] = em.dma("sync", qraw_d[h, :, :], hbuf[hh][:, :],
                                    waits=[("vch", qev)], stream=f"qrst{hh}")
        q_psum_free = em.tmark(waits=[("vch", em.cnt["vch"])])
    t_qstats = em.cnt["tch"]

    vqs = em.vec(lambda e: e.tensor_copy(stq32[:, :], pstat[:, :]),
                 waits=[("tch", t_qstats)])
    uq = em.act(lambda e: e.activation(u32[:, :], stq32[:, :], AF.Sqrt,
                                       bias=epsr[0:32, 0:1]),
                waits=[("vch", vqs)])
    vq_inv = em.vec(lambda e: e.reciprocal(inv32[:, :], u32[:, :]), waits=[("ach", uq)])

    qn_store_p = [None, None]
    qn_read = [em.cnt["vch"], em.cnt["vch"]]
    prev_vb = vq_inv
    for h in range(NH):
        for th in range(2):
            em.mm(lambda e, h=h, th=th:
                  e.matmul(bank(PS, 6 + th), hsels[:, 128 * (h % 32):128 * (h % 32) + 128],
                           inv32[:, 512 * th:512 * th + 512], start=True, stop=True),
                  waits=[("vch", prev_vb)], inc=(th == 1))
        tb = em.cnt["tch"]
        vb = 0
        for th in range(2):
            vb = em.vec(lambda e, th=th: e.tensor_copy(
                invqb[:, 512 * th:512 * th + 512], bank(PS, 6 + th)),
                waits=[("tch", tb)])
        i2 = h % 2
        lr = em.dma("gpsimd", hbuf[i2][:, :], qraw_d[h, :, :],
                    waits=[("vch", qn_read[i2])] +
                          ([qraw_store[0], qraw_store[1]] if h < 2 else []),
                    stream=f"qlr{i2}")
        prev_vb = vb
        ev = em.vec(lambda e, h=h, i2=i2: e.scalar_tensor_tensor(
            qout[i2][:, :], hbuf[i2][:, :], qnwc[:, 0:1],
            invqb[:, :], ALU.mult, ALU.mult),
            waits=[("vch", vb), lr, l_qnw, qn_store_p[i2]])
        qn_read[i2] = ev
        qn_store_p[i2] = em.dma("sync", qt[h, :, :], qout[i2][:, :], waits=[("vch", ev)], stream=f"qst{i2}")

    em.finalize([("vector", "vch", em.cnt["vch"])])
    return nc




def build_l2(core_mod=None):
    nc = bass.Bass("TRN2", target_bir_lowering=False)
    ktall = nc.dram_tensor("ktall", [NKV, HD, S], BF16, kind="ExternalInput").ap()
    vpe = nc.dram_tensor("vpe", [NKV, 128, 64, 128], BF16, kind="ExternalInput").ap()
    qt = nc.dram_tensor("qt", [NH, HD, TPC], BF16, kind="ExternalInput").ap()
    x1t = nc.dram_tensor("x1t", [D, TPC], F32, kind="ExternalInput").ap()
    wot = nc.dram_tensor("wot", [D, D], F32, kind="ExternalInput").ap()
    wpt = nc.dram_tensor("wpt", [D // 2, D], U8, kind="ExternalInput").ap()
    zp32 = nc.dram_tensor("zp32", [32, 128], F32, kind="ExternalInput").ap()
    bf32 = nc.dram_tensor("bf32", [32, 128], F32, kind="ExternalInput").ap()
    sc32 = nc.dram_tensor("sc32", [32, 128], F32, kind="ExternalInput").ap()
    albl = nc.dram_tensor("albl", [NH, 6, TPC], BF16, kind="ExternalInput").ap()
    albr = nc.dram_tensor("albr", [NH, 6, S], BF16, kind="ExternalInput").ap()
    mtin = nc.dram_tensor("mtin", [128, 8 * 128], BF16, kind="ExternalInput").ap()
    identb = nc.dram_tensor("identb", [128, 128], BF16, kind="ExternalInput").ap()
    identf = nc.dram_tensor("identf", [128, 128], F32, kind="ExternalInput").ap()
    sel2 = nc.dram_tensor("sel2", [2, 256], F32, kind="ExternalInput").ap()
    onescol2 = nc.dram_tensor("onescol2", [128, 4], BF16, kind="ExternalInput").ap()

    yt = nc.dram_tensor("yt", [D, TPC], F32, kind="ExternalOutput").ap()
    o_park = nc.dram_tensor("o_park", [TPC, D], BF16, kind="Internal").ap()
    x2t = nc.dram_tensor("x2t", [D, TPC], F32, kind="Internal").ap()

    em = Em(nc)
    es = em.es
    sbf = lambda name, shape, dt: es.enter_context(nc.sbuf_tensor(name, shape, dt))
    PS = es.enter_context(nc.psum_tensor("PS", [128, 4096], F32))

    F = {}
    F["wpt"] = wpt
    F["zpb2"] = sbf("zpb2", [2, D], BF16)
    F["zrow2"] = sbf("zrow2", [2, TPC], BF16)
    F["hs"] = sbf("hs", [128, 32], F32)
    F["ext_free"] = [0, 0]
    oc2e = sbf("oc2e", [128, 4], BF16)
    sel2e = sbf("sel2e", [2, 256], F32)
    epse = sbf("epse", [128, 1], F32)
    srow1b = sbf("srow1b", [1, TPC], BF16)
    zp32s = sbf("zp32s", [32, 128], F32)
    bf32s = sbf("bf32s", [32, 128], F32)
    nzp32 = sbf("nzp32", [32, 128], BF16)
    bs32 = sbf("bs32", [32, 128], BF16)
    rs32 = sbf("rs32", [32, 128], F32)
    sc32s = sbf("sc32s", [32, 128], F32)
    identfs = sbf("identfs", [128, 128], F32)
    identbs = sbf("identbs", [128, 128], BF16)
    ones1 = sbf("ones1", [128, 1], BF16)
    neginf = sbf("neginf", [128, 1], F32)
    mbuf = [sbf("mbuf0", [128, 1], F32), sbf("mbuf1", [128, 1], F32),
            sbf("mbuf2", [128, 1], F32), sbf("mbuf3", [128, 1], F32)]
    negm_b = [sbf("negm_b0", [128, 1], F32), sbf("negm_b1", [128, 1], F32)]
    lbuf = [sbf("lbuf0", [128, 1], F32), sbf("lbuf1", [128, 1], F32)]
    linv_b = [sbf("linv_b0", [128, 1], F32), sbf("linv_b1", [128, 1], F32)]
    osb = [sbf("osb0", [128, 128], BF16), sbf("osb1", [128, 128], BF16)]
    mt = sbf("mt", [128, 8 * 128], BF16)

    l_oc2 = em.dma("sync", oc2e[:, :], onescol2[:, :])
    l_sel2 = em.dma("sync", sel2e[:, :], sel2[:, :])
    l_zpf = em.dma("sync", zp32s[:, :], zp32[:, :])
    l_bff = em.dma("sync", bf32s[:, :], bf32[:, :])
    l_sc32 = em.dma("sync", sc32s[:, :], sc32[:, :])
    l_idf = em.dma("sync", identfs[:, :], identf[:, :])
    l_idb = em.dma("sync", identbs[:, :], identb[:, :])
    l_mt = em.dma("sync", mt[:, :], mtin[:, :])
    em.vec(lambda e: e.memset(ones1[:, :], 1.0))
    eps_ready_v = em.vec(lambda e: e.memset(epse[:, :], EPS_LN))
    em.vec(lambda e: e.memset(neginf[:, :], -3.0e38))
    vnzp = em.vec(lambda e: e.tensor_scalar_mul(nzp32[:, :], zp32s[:, :], -1.0),
                  waits=[l_zpf])
    em.vec(lambda e: e.reciprocal(rs32[:, :], sc32s[:, :]), waits=[l_sc32])
    vdiv = em.vec(lambda e: e.tensor_tensor(bs32[:, :], bf32s[:, :], rs32[:, :],
                                            ALU.mult), waits=[l_bff])
    d_zpb0 = em.dma("gpsimd", F["zpb2"][0:1, :], nzp32[:, :], waits=[("vch", vnzp)], stream="zpb0")
    d_zpb1 = em.dma("gpsimd", F["zpb2"][1:2, :], bs32[:, :], waits=[("vch", vdiv)], stream="zpb1")
    vone = em.vec(lambda e: e.memset(srow1b[0:1, 0:TPC], 1.0))
    F["zr1_done"] = em.dma("gpsimd", F["zrow2"][1:2, :], srow1b[0:1, 0:TPC],
                           waits=[("vch", vone), d_zpb0, d_zpb1], stream="zr1")
    t_hs = em.mm(lambda e: e.transpose(bank(PS, 7)[:, 0:32], sc32s[:, :], identfs[0:32, 0:32]),
                 waits=[l_idf, l_sc32], inc=True)
    v_hs = em.vec(lambda e: e.tensor_scalar_mul(F["hs"][:, :], bank(PS, 7)[:, 0:32], 0.5),
                  waits=[("tch", t_hs)])
    pro_gq = em.qbar("gq")

    es_a = ExitStack()
    sba = lambda name, shape, dt: es_a.enter_context(nc.sbuf_tensor(name, shape, dt))
    kth = sba("kth", [128, S], BF16)
    vh = sba("vh", [128, S], BF16)
    qth = sba("qth", [128, TPC], BF16)
    arh = sba("arh", [6, S], BF16)
    alh = sba("alh", [6, TPC], BF16)
    srow = [sba("srow0", [128, S], F32), sba("srow1", [128, S], F32)]
    prow = [sba("prow0", [128, S], BF16), sba("prow1", [128, S], BF16)]
    ptrow = sba("ptrow", [128, 2 * 1024], BF16)

    kv_last_use = 0
    q_last_use = 0
    exp_p = [0, 0]
    recip_p = [0, 0]
    avmark_p = [0, 0]
    oev_p = [0, 0]
    ost_p = [None, None]
    wave_slot_user = [0, 0]
    wave_i = 0
    psum_free = [v_hs] * 4
    combo_i = 0

    for h in range(NKV):
        ld_k = em.dma("gpsimd", kth[:, :], ktall[h, :, :],
                      waits=[("tch", kv_last_use)], stream="kth")
        ld_v = em.dma("gpsimd", vh[:, :], vpe[h, :, :, :],
                      waits=[("tch", kv_last_use)], stream="vh")
        for qh in range(4 * h, 4 * h + 4):
            ld_q = em.dma("gpsimd", qth[:, :], qt[qh, :, :],
                          waits=[("tch", q_last_use)], stream="qth")
            ld_ar = em.dma("gpsimd", arh[:, :], albr[qh, :, :],
                           waits=[("tch", q_last_use)], stream="arh")
            ld_al = em.dma("gpsimd", alh[:, :], albl[qh, :, :],
                           waits=[("tch", q_last_use)], stream="alh")
            def emit_pass1(T):
                nonlocal combo_i
                nch = 2 * (T + 1)
                par = combo_i % 2
                sr = srow[par]
                for k in range(nch):
                    b = k % 4
                    wl = [("vch", psum_free[b])]
                    if k == 0:
                        wl += [ld_k, ld_v, ld_q, ld_ar, ld_al, l_mt, l_idb,
                               ("ach", exp_p[par])]
                    em.mm(lambda e, T=T, k=k, b=b:
                          e.matmul(bank(PS, b), alh[0:6, 128 * T:128 * T + 128],
                                   arh[0:6, 512 * k:512 * k + 512],
                                   start=True, stop=False),
                          waits=wl)
                    diag = (k >= 2 * T)
                    if diag:
                        for u in range(4):
                            blk = 4 * (k - 2 * T) + u
                            em.mm(lambda e, b=b, u=u, blk=blk:
                                  e.matmul(bank(PS, b)[:, 128 * u:128 * u + 128],
                                           identbs[:, :], mt[:, 128 * blk:128 * blk + 128],
                                           start=False, stop=False))
                    mk = em.mm(lambda e, T=T, k=k, b=b:
                               e.matmul(bank(PS, b), qth[:, 128 * T:128 * T + 128],
                                        kth[:, 512 * k:512 * k + 512],
                                        start=False, stop=True),
                               inc=True)
                    ev = em.vec(lambda e, k=k, b=b, sr=sr, par=par, first=(k == 0):
                                e.tensor_scalar(sr[:, 512 * k:512 * k + 512],
                                                bank(PS, b), 0.0,
                                                neginf[:, 0:1] if first else mbuf[2 * par + (k - 1) % 2][:, 0:1],
                                                ALU.add, ALU.max,
                                                accum_out=mbuf[2 * par + k % 2][:, 0:1]),
                                waits=[("tch", mk)] + ([("ach", exp_p[par])] if k == 0 else []))
                    psum_free[b] = ev
                st = dict(T=T, par=par, combo=combo_i, qh=qh)
                combo_i += 1
                return st

            def emit_pass2(st):
                nonlocal wave_i
                T = st["T"]; par = st["par"]; qh_ = st["qh"]
                sr = srow[par]; pr = prow[par]
                nch = 2 * (T + 1)
                L = 1024 * (T + 1)
                vneg = em.vec(lambda e, k=nch - 1, par=par: e.tensor_scalar_mul(
                    negm_b[par][:, :], mbuf[2 * par + k % 2][:, :], -1.0))
                expv = em.act(lambda e, L=L, par=par, sr=sr, pr=pr: e.activation(
                    pr[:, 0:L], sr[:, 0:L], AF.Exp, bias=negm_b[par][:, 0:1],
                    accum_out=lbuf[par][:, 0:1]),
                    waits=[("vch", vneg), ("vch", recip_p[par]), ("tch", avmark_p[par])])
                exp_p[par] = expv
                vrec = em.vec(lambda e, par=par: e.reciprocal(linv_b[par][:, :], lbuf[par][:, :]),
                              waits=[("ach", expv)])
                recip_p[par] = vrec
                nblk = 8 * (T + 1)
                av = None
                for wave in range(nblk // 8):
                    slot = wave_i % 2
                    wtrt = []
                    for st_i in range(8):
                        bb = 8 * wave + st_i
                        wtrt.append(em.dmat(
                            "sync", ptrow[:, 1024 * slot + 128 * st_i:1024 * slot + 128 * st_i + 128],
                            pr[:, 128 * bb:128 * bb + 128],
                            waits=[("ach", expv), ("tch", wave_slot_user[slot])]
                            + (list(pro_gq) if st["combo"] == 0 and bb == 0 else []),
                            stream=f"tr{slot}_{st_i}"))
                    for st_i in range(8):
                        bb = 8 * wave + st_i
                        wl = [wtrt[st_i]]
                        if bb == 0:
                            wl.append(("vch", oev_p[par]))
                        av = em.mm(lambda e, bb=bb, nblk=nblk, slot=slot, st_i=st_i, par=par:
                                   e.matmul(PS[0:128, (4 + par) * 512:(4 + par) * 512 + 128],
                                            ptrow[:, 1024 * slot + 128 * st_i:1024 * slot + 128 * st_i + 128],
                                            vh[:, 128 * bb:128 * bb + 128],
                                            start=(bb == 0), stop=(bb == nblk - 1)),
                                   waits=wl, inc=(st_i == 7))
                    wave_slot_user[slot] = av
                    wave_i += 1
                avmark_p[par] = av
                oev = em.vec(lambda e, par=par: e.tensor_scalar(
                    osb[par][:, :], PS[0:128, (4 + par) * 512:(4 + par) * 512 + 128],
                    linv_b[par][:, 0:1], None, ALU.mult),
                    waits=[("tch", av), ("vch", vrec), ost_p[par]])
                oev_p[par] = oev
                ost_p[par] = em.dma("sync",
                                    o_park[128 * T:128 * T + 128,
                                           128 * qh_:128 * qh_ + 128],
                                    osb[par][:, :], waits=[("vch", oev)],
                                    stream=f"ost{par}")
                return av

            pend = None
            for T in range(8):
                st = emit_pass1(T)
                if pend is not None:
                    emit_pass2(pend)
                pend = st
            av_last = emit_pass2(pend)
            kv_last_use = av_last
            q_last_use = av_last

    attn_done = [("ach", max(exp_p)), ("tch", max(avmark_p)),
                 ("vch", max(oev_p)), ost_p[0], ost_p[1]]
    es_a.close()

    es_w = ExitStack()
    sbw = lambda name, shape, dt: es_w.enter_context(nc.sbuf_tensor(name, shape, dt))
    lb = LNBufs(em, sbw, PS)
    lb.onescol2 = oc2e
    lb.sel2 = sel2e
    lb.epsc = epse
    lb.eps_ready = eps_ready_v
    F["wby"] = [sbw("wby0", [128, 512], U8), sbw("wby1", [128, 512], U8)]
    F["lo8"] = [sbw("lo80", [128, 512], U8), sbw("lo81", [128, 512], U8)]
    F["hi8"] = [sbw("hi80", [128, 512], U8), sbw("hi81", [128, 512], U8)]
    F["lo"] = [sbw("lo0", [128, 512], BF16), sbw("lo1", [128, 512], BF16)]
    F["hi"] = [sbw("hi0", [128, 512], BF16), sbw("hi1", [128, 512], BF16)]
    F["x1b"] = [sbw("x1b0", [128, 512], F32), sbw("x1b1", [128, 512], F32)]
    F["x1o"] = [sbw("x1o0", [128, 512], F32), sbw("x1o1", [128, 512], F32)]
    zt = sbw("zt", [128, 32 * TPC], BF16)
    wob = [sbw("wob0", [128, 512], BF16), sbw("wob1", [128, 512], BF16)]
    otb = [sbw("otb00", [128, 512], BF16), sbw("otb01", [128, 512], BF16),
           sbw("otb10", [128, 512], BF16), sbw("otb11", [128, 512], BF16)]
    x2b = [sbw("x2b0", [128, 512], F32), sbw("x2b1", [128, 512], F32)]
    x2o = [sbw("x2o0", [128, 512], F32), sbw("x2o1", [128, 512], F32)]

    wo_psum_free = [max(oev_p)] * 8
    wob_free = [0, 0]
    otb_free = [0, 0, 0, 0]
    x2b_free = [em.cnt["vch"], em.cnt["vch"]]
    x2o_free = [None, None]
    x2_last = None
    for og in range(8):
        for dc in range(32):
            i2 = dc % 2
            wb = wob[i2]
            lw = em.dma("gpsimd", wb[:, :],
                        wot[128 * dc:128 * dc + 128, 512 * og:512 * og + 512],
                        waits=[("tch", wob_free[i2])] +
                              (attn_done if (og == 0 and dc == 0) else []),
                        stream=f"wob{i2}")
            ots = []
            for th in range(2):
                oi = 2 * (dc % 2) + th
                lo_ = em.dmat("sync", otb[oi][:, :],
                              o_park[512 * th:512 * th + 512, 128 * dc:128 * dc + 128],
                              waits=[("tch", otb_free[oi])] +
                                    (attn_done if (og == 0 and dc < 2) else []),
                              stream=f"ot{oi}")
                ots.append(lo_)
            for th in range(2):
                for oc in range(4):
                    psi = 4 * th + oc
                    oi = 2 * (dc % 2) + th
                    wl = []
                    if oc == 0:
                        wl = [lw, ots[th]]
                    if dc == 0 and th == 0 and oc == 0:
                        wl.append(("vch", max(wo_psum_free)))
                    mk = em.mm(lambda e, psi=psi, oc=oc, oi=oi, wb=wb, dc=dc:
                               e.matmul(bank(PS, psi), wb[:, 128 * oc:128 * oc + 128],
                                        otb[oi][:, :], start=(dc == 0), stop=(dc == 31)),
                               waits=wl, inc=(th == 1 and oc == 3))
            wob_free[i2] = mk
            otb_free[2 * (dc % 2)] = mk
            otb_free[2 * (dc % 2) + 1] = mk
        tm = em.cnt["tch"]
        for th in range(2):
            for oc in range(4):
                psi = 4 * th + oc
                oco = 4 * og + oc
                i2 = psi % 2
                lx = em.dma("gpsimd", x2b[i2][:, :],
                            x1t[128 * oco:128 * oco + 128, 512 * th:512 * th + 512],
                            waits=[("vch", x2b_free[i2])], stream=f"x2l{i2}")
                ev = em.vec(lambda e, psi=psi, i2=i2: e.tensor_tensor(
                    x2o[i2][:, :], bank(PS, psi), x2b[i2][:, :], ALU.add),
                    waits=[("tch", tm), lx, x2o_free[i2]])
                wo_psum_free[psi] = ev
                x2b_free[i2] = ev
                x2_last = em.dma("sync", x2t[128 * oco:128 * oco + 128,
                                             512 * th:512 * th + 512],
                                 x2o[i2][:, :], waits=[("vch", ev)], stream=f"x2s{i2}")
                x2o_free[i2] = x2_last

    z3 = emit_ln_z(em, lb, x2t, zt, prior_psum_free=em.cnt["vch"],
                   gate_dma=em.qbar("sq"))
    sz3 = emit_sumz(em, lb, zt, F["zrow2"], ones1, sumz_gate=z3)
    F["xin_gate"] = em.qbar("sq")
    emit_ffn(em, lb, F, zt, z3, sz3, x2t, yt)

    es_w.close()
    em.finalize([("vector", "vch", em.cnt["vch"])])
    return nc


def _consts_l1(inputs):
    Wq = np.asarray(inputs["L0_attn_Wq"])
    Wk = np.asarray(inputs["L0_attn_Wk"])
    Wv = np.asarray(inputs["L0_attn_Wv"])
    wp = np.asarray(inputs["L0_ffn_weight_packed"])
    scale = np.asarray(inputs["L0_ffn_scale"])[:, 0]
    zp = np.asarray(inputs["L0_ffn_zero_point"])[:, 0]
    biasf = np.asarray(inputs["L0_ffn_bias"])
    qn = np.asarray(inputs["L0_attn_q_norm_w"])
    kn = np.asarray(inputs["L0_attn_k_norm_w"])

    wqt = np.ascontiguousarray(Wq.T[PI])
    wkt = np.ascontiguousarray(Wk.T[PI])
    wvt = np.ascontiguousarray(Wv.T[PI])
    bytes_ = np.ascontiguousarray(wp).view(np.uint8).reshape(-1, 4)[:, 0]
    wpt = np.ascontiguousarray(bytes_.reshape(D, D // 2).T[:, PI])
    scp = np.ascontiguousarray(scale[PI])
    zpp = np.ascontiguousarray(zp[PI])
    bfp = np.ascontiguousarray(biasf[PI])

    sel2 = np.zeros((2, 256), np.float32)
    sel2[0, 0:128] = 1.0
    sel2[1, 128:256] = 1.0
    onescol2 = np.zeros((128, 4), BF)
    onescol2[:, 0] = 1
    onescol2[:, 3] = 1
    identf = np.eye(128, dtype=np.float32)
    qsel = np.zeros((128, 32 * 32), BF)
    for h in range(32):
        qsel[:, 32 * h + h] = 1
    hsel = np.zeros((32, 32 * 128), np.float32)
    for h in range(32):
        hsel[h, 128 * h:128 * h + 128] = 1

    return dict(wqt=wqt, wkt=wkt, wvt=wvt, wpt=wpt,
                zp32=np.ascontiguousarray(zpp.reshape(32, 128)),
                bf32=np.ascontiguousarray(bfp.reshape(32, 128)),
                sc32=np.ascontiguousarray(scp.reshape(32, 128)),
                sel2=sel2, onescol2=onescol2, identf=identf,
                qnw=np.ascontiguousarray(qn[None, :]),
                knw=np.ascontiguousarray(kn[None, :]),
                qselt=qsel, hsel=hsel)


def stage_l1(inputs):
    x = np.asarray(inputs["L0_attn_x"])[0]
    shared = _consts_l1(inputs)
    in_maps = []
    for c in range(NC):
        m = dict(shared)
        m["xt"] = np.ascontiguousarray(x[c::NC].T[PI])
        in_maps.append(m)
    return in_maps


_CACHE = {}


def run_l1(inputs):
    in_maps = stage_l1(inputs)
    if "l1" not in _CACHE:
        _CACHE["l1"] = build_l1()
    res = run_bass_kernel_spmd(_CACHE["l1"], in_maps, core_ids=list(range(NC)))
    return res.results


def stage_l2(inputs, r1):
    Wo = np.asarray(inputs["L0_attn_Wo"])
    wot = np.ascontiguousarray(Wo.T[:, PI])
    c1 = _consts_l1(inputs)

    ktall = np.empty((NKV, HD, S), BF)
    vall = np.empty((NKV, S, HD), BF)
    for c in range(NC):
        ktall[:, :, c::NC] = np.asarray(r1[c]["kt"]).reshape(NKV, HD, TPC)
        vall[:, c::NC, :] = np.asarray(r1[c]["vv"]).reshape(TPC, NKV, HD).transpose(1, 0, 2)
    vpe = np.ascontiguousarray(vall.reshape(NKV, 64, 128, HD).transpose(0, 2, 1, 3))

    slopes = _alibi_slopes(NH)
    g = np.arange(S, dtype=np.float64)
    albr = np.zeros((NH, 6, S), BF)
    for qh in range(NH):
        s0, s1, s2 = _split3(slopes[qh] * g)
        albr[qh, 0] = s0
        albr[qh, 1] = s1
        albr[qh, 2] = s2
        albr[qh, 3:6] = -1.0

    identb = np.eye(128, dtype=BF)

    shared = dict(wot=wot, wpt=c1["wpt"], zp32=c1["zp32"], bf32=c1["bf32"],
                  sc32=c1["sc32"], albr=albr, identb=identb,
                  identf=c1["identf"], sel2=c1["sel2"], onescol2=c1["onescol2"])
    in_maps = []
    jj = np.arange(TPC, dtype=np.float64)
    ii = np.arange(128)
    for c in range(NC):
        m = dict(shared)
        m["ktall"] = ktall
        m["vpe"] = vpe
        m["qt"] = np.asarray(r1[c]["qt"]).reshape(NH, HD, TPC)
        m["x1t"] = np.asarray(r1[c]["x1t"])
        p = 8.0 * jj + c
        albl = np.zeros((NH, 6, TPC), BF)
        for qh in range(NH):
            s0, s1, s2 = _split3(slopes[qh] * p)
            albl[qh, 0:3] = 1.0
            albl[qh, 3] = s0
            albl[qh, 4] = s1
            albl[qh, 5] = s2
        m["albl"] = albl
        mt = np.zeros((128, 8 * 128), np.float32)
        for bb in range(8):
            allowed = (128 * bb + ii[None, :]) <= (8 * np.arange(128)[:, None] + c)
            mt[:, 128 * bb:128 * bb + 128] = np.where(allowed, 0.0, NEG_BIG)
        m["mtin"] = mt.astype(BF)
        in_maps.append(m)
    return in_maps


def kernel(**inputs):
    in1 = stage_l1(inputs)
    if "l1" not in _CACHE:
        _CACHE["l1"] = build_l1()
    r1 = run_bass_kernel_spmd(_CACHE["l1"], in1, core_ids=list(range(NC))).results
    in2 = stage_l2(inputs, r1)
    if "l2" not in _CACHE:
        _CACHE["l2"] = build_l2()
    r2 = run_bass_kernel_spmd(_CACHE["l2"], in2, core_ids=list(range(NC))).results
    out = np.empty((S, D), np.float32)
    for c in range(NC):
        xc = np.empty((TPC, D), np.float32)
        xc[:, PI] = np.asarray(r2[c]["yt"]).T
        out[c::NC] = xc
    return out[None, :, :]


# version 03
# speedup vs baseline: 1.8782x; 1.7808x over previous
import math
import numpy as np
import ml_dtypes
from contextlib import ExitStack

import concourse.bass as bass
import concourse.mybir as mybir
from concourse.bass_utils import run_bass_kernel_spmd

F32 = mybir.dt.float32
BF16 = mybir.dt.bfloat16
U8 = mybir.dt.uint8
AF = mybir.ActivationFunctionType
ALU = mybir.AluOpType
BF = ml_dtypes.bfloat16

NC = 8
S = 8192
D = 4096
TPC = S // NC
NH = 32
NKV = 8
HD = 128
EPS_LN = 1e-5
EPS_RMS = 1e-6
NEG_BIG = -1.0e30
SQRT128 = float(np.sqrt(128.0))

PI = np.arange(D).reshape(D // 256, 128, 2).transpose(0, 2, 1).reshape(-1)

ENGINES = ("sync", "tensor", "vector", "scalar", "gpsimd")


def _alibi_slopes(n_heads):
    start = 2 ** (-(2 ** -(math.log2(n_heads) - 3)))
    return np.array([start * start ** i for i in range(n_heads)], dtype=np.float64)


def _split3(v):
    s0 = v.astype(BF).astype(np.float64)
    s1 = (v - s0).astype(BF).astype(np.float64)
    s2 = (v - s0 - s1).astype(BF)
    return s0.astype(BF), s1.astype(BF), s2


class Em:

    def __init__(self, nc):
        self.nc = nc
        self.es = ExitStack()
        self.ops = {e: [] for e in ENGINES}
        self.sems = {}
        self.cnt = {}
        for name in ("vch", "ach", "pch", "tch", "tbar"):
            self.sems[name] = self.es.enter_context(nc.semaphore(name))
            self.cnt[name] = 0
        self.qstreams = {"sq": set(), "gq": set(), "aq": set()}
        self.auto_i = 0

    def _app(self, eng, fn):
        self.ops[eng].append(fn)

    def _waits(self, waits):
        out = []
        for w in waits:
            if w is None:
                continue
            (n, v) = w
            if v > 0:
                out.append((self.sems[n], v))
        return tuple(out)

    def wait(self, eng, sem_name, val):
        s = self.sems[sem_name]
        self._app(eng, lambda e, s=s, v=val: e.wait_ge(s, v))

    def _chain(self, eng, ch, fn, waits=()):
        self.cnt[ch] += 1
        tgt = self.cnt[ch]
        sem = self.sems[ch]
        w = self._waits(waits)

        def run(e, fn=fn, sem=sem, tgt=tgt, w=w):
            for (ws, wv) in w:
                e.wait_ge(ws, wv)
            if tgt > 1:
                e.wait_ge(sem, tgt - 1)
            fn(e).then_inc(sem, 1)

        self._app(eng, run)
        return tgt

    def vec(self, fn, waits=()):
        return self._chain("vector", "vch", fn, waits)

    def act(self, fn, waits=()):
        return self._chain("scalar", "ach", fn, waits)

    def pool(self, fn, waits=()):
        return self._chain("gpsimd", "pch", fn, waits)

    def mm(self, fn, waits=(), inc=False):
        w = self._waits(waits)
        if inc:
            self.cnt["tch"] += 1
            tgt = self.cnt["tch"]
            sem = self.sems["tch"]

            def run(e, fn=fn, w=w, sem=sem):
                for (ws, wv) in w:
                    e.wait_ge(ws, wv)
                fn(e).then_inc(sem, 1)

            self._app("tensor", run)
            return tgt
        else:
            def run(e, fn=fn, w=w):
                for (ws, wv) in w:
                    e.wait_ge(ws, wv)
                fn(e)

            self._app("tensor", run)
            return None

    def tmark(self, waits=()):
        w = self._waits(waits)
        self.cnt["tbar"] += 1
        tgt = self.cnt["tbar"]
        sem = self.sems["tbar"]

        def run(e, w=w, sem=sem):
            for (ws, wv) in w:
                e.wait_ge(ws, wv)
            e.nop().then_inc(sem, 1)

        self._app("tensor", run)
        return tgt

    def _dma_emit(self, queue, q, stream, fn, waits):
        if stream is None:
            stream = f"auto{self.auto_i}"
            self.auto_i += 1
        n = f"{q}_{stream}"
        if n not in self.sems:
            self.sems[n] = self.es.enter_context(self.nc.semaphore(n))
            self.cnt[n] = 0
            self.qstreams[q].add(n)
        self.cnt[n] += 16
        tgt = self.cnt[n]
        sem = self.sems[n]
        w = self._waits(waits)

        def run(e, fn=fn, sem=sem, w=w):
            for (ws, wv) in w:
                e.wait_ge(ws, wv)
            fn(e).then_inc(sem, 16)

        self._app(queue, run)
        return (n, tgt)

    def dma(self, queue, out, in_, waits=(), stream=None, **kw):
        q = {"sync": "sq", "gpsimd": "gq", "scalar": "aq"}[queue]
        return self._dma_emit(queue, q, stream,
                              lambda e, out=out, in_=in_, kw=kw:
                              e.dma_start(out=out, in_=in_, **kw), waits)

    def dmat(self, queue, out, in_, waits=(), stream=None):
        q = {"sync": "sq", "scalar": "aq"}[queue]
        return self._dma_emit(queue, q, stream,
                              lambda e, out=out, in_=in_:
                              e.dma_start_transpose(out, in_), waits)

    def qbar(self, q):
        return tuple((n, self.cnt[n]) for n in sorted(self.qstreams[q])
                     if self.cnt[n] > 0)

    def finalize(self, final_waits=()):
        for (eng, sem_name, val) in final_waits:
            self.wait(eng, sem_name, val)
        for (eng, q) in (("sync", "sq"), ("gpsimd", "gq"), ("scalar", "aq")):
            for (n, v) in self.qbar(q):
                self.wait(eng, n, v)
        block = self.es.enter_context(self.nc.Block())
        for eng in ENGINES:
            ops = self.ops[eng]
            if not ops:
                continue
            getattr(block, eng)(lambda e, ops=ops: [f(e) for f in ops])
        self.es.close()



class LNBufs:

    def __init__(self, em, sbf, PS):
        self.xb = [sbf("ln_xb0", [128, TPC], BF16), sbf("ln_xb1", [128, TPC], BF16)]
        self.xf = [sbf("ln_xf0", [128, TPC], F32), sbf("ln_xf1", [128, TPC], F32)]
        self.sq = [sbf("ln_sq0", [128, TPC], BF16), sbf("ln_sq1", [128, TPC], BF16)]
        self.sxs = sbf("ln_sxs", [2, 512], F32)
        self.sqs = sbf("ln_sqs", [2, 512], F32)
        self.m2 = sbf("ln_m2", [2, 512], F32)
        self.var = sbf("ln_var", [2, 512], F32)
        self.u2 = sbf("ln_u2", [2, 512], F32)
        self.inv = sbf("ln_inv", [2, 512], F32)
        self.minv = sbf("ln_minv", [2, 512], F32)
        self.tmp2 = sbf("ln_tmp2", [2, 512], F32)
        self.invb = sbf("ln_invb", [128, TPC], F32)
        self.minvb = sbf("ln_minvb", [128, TPC], F32)
        self.t1 = sbf("ln_t1", [128, TPC], F32)
        self.epsc = sbf("ln_epsc", [128, 1], F32)
        self.onescol2 = sbf("ln_oc2", [128, 4], BF16)
        self.sel2 = sbf("ln_sel2", [2, 256], F32)
        self.PS = PS


def bank(PS, i, rows=128, cols=512):
    return PS[0:rows, 512 * i:512 * i + cols]


def emit_ln_z(em, lb, xsrc_dram, zt, prior_psum_free=0, gate_dma=()):
    PS = lb.PS
    stx = bank(PS, 0, rows=2)
    stq = bank(PS, 1, rows=2)
    t0 = em.cnt["tch"]
    v0 = em.cnt["vch"]
    a0 = em.cnt["ach"]
    xb_free = [t0, t0]
    sq_free = [t0, t0]
    for c in range(32):
        i = c % 2
        xb = lb.xb[i]
        ld = em.dma("gpsimd", xb[:, :], xsrc_dram[128 * c:128 * c + 128, :],
                    waits=[("tch", xb_free[i])] + (list(gate_dma) if c < 2 else []),
                    stream=f"lnxb{i}")
        sqt = lb.sq[i]
        sv = em.act(lambda e, xb=xb, sqt=sqt: e.activation(sqt[:, :], xb[:, :], AF.Square),
                    waits=[ld, ("tch", sq_free[i])])
        for h in range(2):
            w0 = []
            if c == 0 and h == 0:
                w0 = [("vch", prior_psum_free)] + list(gate_dma)
            em.mm(lambda e, h=h, c=c, xb=xb:
                  e.matmul(stx[:, :], lb.onescol2[:, 2 * h:2 * h + 2],
                           xb[:, 512 * h:512 * h + 512],
                           start=(c == 0 and h == 0), stop=(c == 31 and h == 1)),
                  waits=w0 + ([ld] if h == 0 else []))
            mk = em.mm(lambda e, h=h, c=c, sqt=sqt:
                       e.matmul(stq[:, :], lb.onescol2[:, 2 * h:2 * h + 2],
                                sqt[:, 512 * h:512 * h + 512],
                                start=(c == 0 and h == 0), stop=(c == 31 and h == 1)),
                       waits=[("ach", sv)] if h == 0 else [], inc=(h == 1))
        xb_free[i] = mk
        sq_free[i] = mk

    tdone = em.cnt["tch"]
    em.vec(lambda e: e.tensor_copy(lb.sxs[:, :], stx[:, :]), waits=[("tch", tdone)])
    em.vec(lambda e: e.tensor_copy(lb.sqs[:, :], stq[:, :]))
    em.vec(lambda e: e.tensor_scalar_mul(lb.m2[:, :], lb.sxs[:, :], 1.0 / D))
    em.vec(lambda e: e.tensor_scalar_mul(lb.var[:, :], lb.sqs[:, :], 1.0 / D))
    em.vec(lambda e: e.tensor_tensor(lb.tmp2[:, :], lb.m2[:, :], lb.m2[:, :], ALU.mult))
    vv = em.vec(lambda e: e.tensor_tensor(lb.var[:, :], lb.var[:, :], lb.tmp2[:, :], ALU.subtract))
    uval = em.act(lambda e: e.activation(lb.u2[:, :], lb.var[:, :], AF.Sqrt,
                                         bias=lb.epsc[0:2, 0:1]),
                  waits=[("vch", vv), ("vch", lb.eps_ready)])
    em.vec(lambda e: e.reciprocal(lb.inv[:, :], lb.u2[:, :]), waits=[("ach", uval)])
    vready = em.vec(lambda e: e.tensor_tensor(lb.minv[:, :], lb.m2[:, :], lb.inv[:, :], ALU.mult))

    prev_evac = 0
    bdone = 0
    for (vecs, dst) in ((lb.inv, lb.invb), (lb.minv, lb.minvb)):
        for h in range(2):
            em.mm(lambda e, vecs=vecs, h=h:
                  e.matmul(bank(PS, 2 + h), lb.sel2[:, 128 * h:128 * h + 128],
                           vecs[:, :], start=True, stop=True),
                  waits=[("vch", vready), ("vch", prev_evac)], inc=(h == 1))
        t = em.cnt["tch"]
        for h in range(2):
            bdone = em.vec(lambda e, dst=dst, h=h:
                           e.tensor_copy(dst[:, 512 * h:512 * h + 512], bank(PS, 2 + h)),
                           waits=[("tch", t)])
        prev_evac = bdone

    xf_free = [v0, v0]
    zready = 0
    for c in range(32):
        i = c % 2
        xf = lb.xf[i]
        ld = em.dma("sync", xf[:, :], xsrc_dram[128 * c:128 * c + 128, :],
                    waits=[("vch", xf_free[i])] + (list(gate_dma) if c < 2 else []),
                    stream=f"lnxf{i}")
        em.vec(lambda e, xf=xf: e.tensor_tensor(lb.t1[:, :], xf[:, :], lb.invb[:, :], ALU.mult),
               waits=[ld, ("vch", bdone)])
        zready = em.vec(lambda e, c=c: e.tensor_tensor(
            zt[:, TPC * c:TPC * c + TPC], lb.t1[:, :], lb.minvb[:, :], ALU.subtract))
        xf_free[i] = zready
    return zready


def emit_sumz(em, lb, zt, zrow2, ones1, sumz_gate=0):
    PS = lb.PS
    t = None
    for h in range(2):
        ps = bank(PS, h, rows=1)
        for c in range(32):
            t = em.mm(lambda e, ps=ps, c=c, h=h:
                      e.matmul(ps[:, :], ones1[:, :],
                               zt[:, TPC * c + 512 * h:TPC * c + 512 * h + 512],
                               start=(c == 0), stop=(c == 31)),
                      waits=[("vch", sumz_gate)] if (c == 0 and h == 0) else (),
                      inc=(h == 1 and c == 31))
    v = 0
    for h in range(2):
        v = em.vec(lambda e, h=h: e.tensor_copy(zrow2[0:1, 512 * h:512 * h + 512],
                                                bank(PS, h, rows=1)),
                   waits=[("tch", t)])
    return v


def emit_ffn(em, lb, F, zt, z_ready, sumz_v, xin_dram, xout_dram):
    PS = lb.PS
    v0 = em.cnt["vch"]
    t0 = em.cnt["tch"]
    psum_free = [v0] * 8
    wby_free = [v0, v0]
    x1b_free = [v0, v0]
    x1o_free = [None, None]
    last_store = None
    F["ext_free"] = [t0, t0]
    for og in range(8):
        grp_started = False
        for bc in range(16):
            i2 = bc % 2
            wb = F["wby"][i2]
            ld = em.dma("sync", wb[:, :],
                        F["wpt"][128 * bc:128 * bc + 128, 512 * og:512 * og + 512],
                        waits=[("vch", wby_free[i2])], stream=f"wby{i2}")
            lo8, hi8 = F["lo8"][i2], F["hi8"][i2]
            lo, hi = F["lo"][i2], F["hi"][i2]
            em.vec(lambda e, wb=wb, lo8=lo8: e.tensor_scalar(
                lo8[:, :], wb[:, :], 15, None, ALU.bitwise_and),
                waits=[ld, ("tch", F["ext_free"][i2])])
            em.vec(lambda e, wb=wb, hi8=hi8: e.tensor_scalar(
                hi8[:, :], wb[:, :], 4, None, ALU.logical_shift_right))
            em.vec(lambda e, lo8=lo8, lo=lo: e.tensor_copy(lo[:, :], lo8[:, :]))
            vex = em.vec(lambda e, hi8=hi8, hi=hi: e.tensor_copy(hi[:, :], hi8[:, :]))
            wby_free[i2] = vex
            for par, wsb in ((0, lo), (1, hi)):
                ic = 2 * bc + par
                for th in range(2):
                    for oc in range(4):
                        psi = 4 * th + oc
                        wl = []
                        if bc == 0 and par == 0:
                            if th == 0 and oc == 0:
                                wl = [("vch", max(z_ready, sumz_v, vex)),
                                      F["zr1_done"]]
                            wl.append(("vch", psum_free[psi]))
                        elif par == 0 and th == 0 and oc == 0:
                            wl = [("vch", vex)]
                        mk = em.mm(lambda e, psi=psi, wsb=wsb, oc=oc, ic=ic, th=th,
                                   s=(bc == 0 and par == 0):
                                   e.matmul(bank(PS, psi),
                                            wsb[:, 128 * oc:128 * oc + 128],
                                            zt[:, TPC * ic + 512 * th:TPC * ic + 512 * th + 512],
                                            start=s, stop=False),
                                   waits=wl,
                                   inc=(par == 1 and th == 1 and oc == 3))
            F["ext_free"][i2] = mk
        for th in range(2):
            for oc in range(4):
                psi = 4 * th + oc
                oco = 4 * og + oc
                tm = em.mm(lambda e, psi=psi, oco=oco, th=th:
                           e.matmul(bank(PS, psi),
                                    F["zpb2"][:, 128 * oco:128 * oco + 128],
                                    F["zrow2"][:, 512 * th:512 * th + 512],
                                    start=False, stop=True),
                           inc=True)
                i2 = psi % 2
                xb_ = F["x1b"][i2]
                xo_ = F["x1o"][i2]
                ldx = em.dma("gpsimd", xb_[:, :],
                             xin_dram[128 * oco:128 * oco + 128, 512 * th:512 * th + 512],
                             waits=[("vch", x1b_free[i2])] + list(F["xin_gate"]),
                             stream=f"ffnx{i2}")
                ev = em.vec(lambda e, psi=psi, oco=oco, xb_=xb_, xo_=xo_:
                            e.scalar_tensor_tensor(xo_[:, :], bank(PS, psi),
                                                   F["hs"][:, oco:oco + 1],
                                                   xb_[:, :], ALU.mult, ALU.add),
                            waits=[("tch", tm), ldx, x1o_free[i2]])
                psum_free[psi] = ev
                x1b_free[i2] = ev
                last_store = em.dma("sync",
                                    xout_dram[128 * oco:128 * oco + 128,
                                              512 * th:512 * th + 512],
                                    xo_[:, :], waits=[("vch", ev)],
                                    stream=f"ffno{i2}")
                x1o_free[i2] = last_store
    return last_store



def build_l1(trunc=None):
    nc = bass.Bass("TRN2", target_bir_lowering=False)
    xt = nc.dram_tensor("xt", [D, TPC], F32, kind="ExternalInput").ap()
    wqt = nc.dram_tensor("wqt", [D, D], F32, kind="ExternalInput").ap()
    wkt = nc.dram_tensor("wkt", [D, NKV * HD], F32, kind="ExternalInput").ap()
    wvt = nc.dram_tensor("wvt", [D, NKV * HD], F32, kind="ExternalInput").ap()
    wpt = nc.dram_tensor("wpt", [D // 2, D], U8, kind="ExternalInput").ap()
    zp32 = nc.dram_tensor("zp32", [32, 128], F32, kind="ExternalInput").ap()
    bf32 = nc.dram_tensor("bf32", [32, 128], F32, kind="ExternalInput").ap()
    sc32 = nc.dram_tensor("sc32", [32, 128], F32, kind="ExternalInput").ap()
    qnw = nc.dram_tensor("qnw", [1, 128], F32, kind="ExternalInput").ap()
    knw = nc.dram_tensor("knw", [1, 128], F32, kind="ExternalInput").ap()
    sel2 = nc.dram_tensor("sel2", [2, 256], F32, kind="ExternalInput").ap()
    onescol2 = nc.dram_tensor("onescol2", [128, 4], BF16, kind="ExternalInput").ap()
    identf = nc.dram_tensor("identf", [128, 128], F32, kind="ExternalInput").ap()
    qselt = nc.dram_tensor("qselt", [128, 32 * 32], BF16, kind="ExternalInput").ap()
    hsel = nc.dram_tensor("hsel", [32, 32 * 128], F32, kind="ExternalInput").ap()

    qraw_d = nc.dram_tensor("qraw_d", [NH, HD, TPC], BF16, kind="Internal").ap()
    x1t = nc.dram_tensor("x1t", [D, TPC], F32, kind="ExternalOutput").ap()
    kt = nc.dram_tensor("kt", [NKV, HD, TPC], BF16, kind="ExternalOutput").ap()
    vv = nc.dram_tensor("vv", [TPC, NKV * HD], BF16, kind="ExternalOutput").ap()
    qt = nc.dram_tensor("qt", [NH, HD, TPC], BF16, kind="ExternalOutput").ap()

    em = Em(nc)
    es = em.es
    sbf = lambda name, shape, dt: es.enter_context(nc.sbuf_tensor(name, shape, dt))
    PS = es.enter_context(nc.psum_tensor("PS", [128, 4096], F32))

    lb = LNBufs(em, sbf, PS)
    zt = sbf("zt", [128, 32 * TPC], BF16)

    F = {}
    F["wpt"] = wpt
    F["wby"] = [sbf("wby0", [128, 512], U8), sbf("wby1", [128, 512], U8)]
    F["lo8"] = [sbf("lo80", [128, 512], U8), sbf("lo81", [128, 512], U8)]
    F["hi8"] = [sbf("hi80", [128, 512], U8), sbf("hi81", [128, 512], U8)]
    F["lo"] = [sbf("lo0", [128, 512], BF16), sbf("lo1", [128, 512], BF16)]
    F["hi"] = [sbf("hi0", [128, 512], BF16), sbf("hi1", [128, 512], BF16)]
    F["zpb2"] = sbf("zpb2", [2, D], BF16)
    F["zrow2"] = sbf("zrow2", [2, TPC], BF16)
    F["hs"] = sbf("hs", [128, 32], F32)
    F["x1b"] = [sbf("x1b0", [128, 512], F32), sbf("x1b1", [128, 512], F32)]
    F["x1o"] = [sbf("x1o0", [128, 512], F32), sbf("x1o1", [128, 512], F32)]
    F["ext_free"] = [0, 0]

    srow1b = sbf("srow1b", [1, TPC], BF16)
    zp32s = sbf("zp32s", [32, 128], F32)
    bf32s = sbf("bf32s", [32, 128], F32)
    nzp32 = sbf("nzp32", [32, 128], BF16)
    bs32 = sbf("bs32", [32, 128], BF16)
    rs32 = sbf("rs32", [32, 128], F32)
    sc32s = sbf("sc32s", [32, 128], F32)
    identfs = sbf("identfs", [128, 128], F32)
    ones1 = sbf("ones1", [128, 1], BF16)
    qsels = sbf("qsels", [128, 32 * 32], BF16)
    hsels = sbf("hsels", [32, 32 * 128], F32)
    sqq = sbf("sqq", [128, TPC], BF16)
    stq32 = sbf("stq32", [32, TPC], F32)
    u32 = sbf("u32", [32, TPC], F32)
    inv32 = sbf("inv32", [32, TPC], F32)
    invqb = sbf("invqb", [128, TPC], F32)
    epsr = sbf("epsr", [128, 1], F32)
    qnwc = sbf("qnwc", [128, 1], F32)
    knwc = sbf("knwc", [128, 1], F32)
    hbuf = [sbf("hbuf0", [128, TPC], BF16), sbf("hbuf1", [128, TPC], BF16)]
    qout = [sbf("qout0", [128, TPC], BF16), sbf("qout1", [128, TPC], BF16)]
    wk_b = [sbf("wkb0", [128, 256], BF16), sbf("wkb1", [128, 256], BF16)]
    wq_b = [sbf("wqb0", [128, 512], BF16), sbf("wqb1", [128, 512], BF16)]
    wv_b = [sbf("wvb0", [128, 1024], BF16), sbf("wvb1", [128, 1024], BF16)]

    l_oc2 = em.dma("sync", lb.onescol2[:, :], onescol2[:, :])
    l_sel2 = em.dma("sync", lb.sel2[:, :], sel2[:, :])
    l_zpf = em.dma("sync", zp32s[:, :], zp32[:, :])
    l_bff = em.dma("sync", bf32s[:, :], bf32[:, :])
    l_sc32 = em.dma("sync", sc32s[:, :], sc32[:, :])
    l_idf = em.dma("sync", identfs[:, :], identf[:, :])
    l_qsel = em.dma("sync", qsels[:, :], qselt[:, :])
    l_hsel = em.dma("sync", hsels[:, :], hsel[:, :])
    l_qnw = em.dma("sync", qnwc[:, 0:1], qnw.rearrange("1 p -> p 1"))
    l_knw = em.dma("sync", knwc[:, 0:1], knw.rearrange("1 p -> p 1"))
    em.vec(lambda e: e.memset(ones1[:, :], 1.0))
    lb.eps_ready = em.vec(lambda e: e.memset(lb.epsc[:, :], EPS_LN))
    v_epsr = em.vec(lambda e: e.memset(epsr[:, :], 128.0 * EPS_RMS))
    vnzp = em.vec(lambda e: e.tensor_scalar_mul(nzp32[:, :], zp32s[:, :], -1.0),
                  waits=[l_zpf])
    em.vec(lambda e: e.reciprocal(rs32[:, :], sc32s[:, :]),
           waits=[l_sc32])
    vdiv = em.vec(lambda e: e.tensor_tensor(bs32[:, :], bf32s[:, :], rs32[:, :],
                                            ALU.mult),
                  waits=[l_bff])
    d_zpb0 = em.dma("gpsimd", F["zpb2"][0:1, :], nzp32[:, :], waits=[("vch", vnzp)], stream="zpb0")
    d_zpb1 = em.dma("gpsimd", F["zpb2"][1:2, :], bs32[:, :], waits=[("vch", vdiv)], stream="zpb1")
    vone = em.vec(lambda e: e.memset(srow1b[0:1, 0:TPC], 1.0))
    F["zr1_done"] = em.dma("gpsimd", F["zrow2"][1:2, :], srow1b[0:1, 0:TPC],
                           waits=[("vch", vone), d_zpb0, d_zpb1], stream="zr1")
    t_hs = em.mm(lambda e: e.transpose(bank(PS, 7)[:, 0:32], sc32s[:, :], identfs[0:32, 0:32]),
                 waits=[l_idf, l_sc32], inc=True)
    v_hs = em.vec(lambda e: e.tensor_scalar_mul(F["hs"][:, :], bank(PS, 7)[:, 0:32], 0.5),
                  waits=[("tch", t_hs)])

    if trunc == "pro":
        em.finalize([("vector", "vch", em.cnt["vch"])])
        return nc
    gate_consts = em.qbar("sq")
    z1 = emit_ln_z(em, lb, xt, zt, prior_psum_free=v_hs, gate_dma=gate_consts)
    sz1 = emit_sumz(em, lb, zt, F["zrow2"], ones1, sumz_gate=z1)

    if trunc == "ln1":
        em.finalize([("vector", "vch", em.cnt["vch"])])
        return nc
    F["xin_gate"] = ()
    ffn1_sq = emit_ffn(em, lb, F, zt, z1, sz1, xt, x1t)

    if trunc == "ffn1":
        em.finalize([("vector", "vch", em.cnt["vch"])])
        return nc
    z2 = emit_ln_z(em, lb, x1t, zt, prior_psum_free=em.cnt["vch"], gate_dma=em.qbar("sq"))

    if trunc == "ln2":
        em.finalize([("vector", "vch", em.cnt["vch"])])
        return nc
    pstat = PS[0:32, 4 * 512:4 * 512 + 1024]

    kraw = sbf("kraw", [128, 8 * TPC], BF16)

    def stats_flags(h, th, first, total):
        return (h == first and th == 0, h == first + total - 1 and th == 1)

    kproj_psum_free = 0
    kstats_mark = 0
    ngroups = 1 if (trunc or "").startswith("kg0") else 4
    nostat = (trunc or "").endswith("nostat")
    nosq = (trunc or "").endswith("nosq")
    if (trunc or "").endswith("nokraw"):
        pass
    for g in range(ngroups):
        heads = (2 * g, 2 * g + 1)
        wb_free = [em.cnt["tch"], em.cnt["tch"]]
        for i in range(32):
            i2 = i % 2
            wb = wk_b[i2]
            ld = em.dma("gpsimd", wb[:, :],
                        wkt[128 * i:128 * i + 128, HD * heads[0]:HD * heads[0] + 256],
                        waits=[("tch", wb_free[i2])], stream=f"wk{i2}")
            for hh in range(2):
                for th in range(2):
                    wl = []
                    if hh == 0 and th == 0:
                        wl.append(ld)
                    if i == 0 and hh == 0 and th == 0:
                        wl.append(("vch", z2))
                        wl.append(("tbar", kproj_psum_free))
                    mk = em.mm(lambda e, hh=hh, th=th, i=i, wb=wb:
                               e.matmul(PS[0:128, (2 * hh + th) * 512:(2 * hh + th) * 512 + 512],
                                        wb[:, 128 * hh:128 * hh + 128],
                                        zt[:, TPC * i + 512 * th:TPC * i + 512 * th + 512],
                                        start=(i == 0), stop=(i == 31)),
                               waits=wl, inc=(hh == 1 and th == 1))
            wb_free[i2] = mk
        tdone = em.cnt["tch"]
        for hh, h in enumerate(heads):
            em.vec(lambda e, hh=hh, h=h: e.tensor_copy(
                kraw[:, TPC * h:TPC * h + TPC],
                PS[0:128, (2 * hh) * 512:(2 * hh) * 512 + 1024]),
                waits=[("tch", tdone)])
            sv = em.vec(lambda e, h=h: e.tensor_tensor(
                sqq[:, :], kraw[:, TPC * h:TPC * h + TPC],
                kraw[:, TPC * h:TPC * h + TPC], ALU.mult),
                waits=[("tch", kstats_mark)])
            for th in range(2):
                kstats_mark = em.mm(lambda e, h=h, th=th:
                      e.matmul(pstat[:, 512 * th:512 * th + 512],
                               qsels[:, 32 * h:32 * h + 32],
                               sqq[:, 512 * th:512 * th + 512],
                               start=(h == 0), stop=(h == 7 if ngroups == 4 else h == 1)),
                      waits=[("vch", sv)] if th == 0 else (), inc=(th == 1)) or kstats_mark
        kproj_psum_free = em.tmark(waits=[("vch", em.cnt["vch"])])
    t_kstats = em.cnt["tch"]

    if trunc is not None and trunc.startswith(("kgrp", "kg0")):
        em.finalize([("vector", "vch", em.cnt["vch"])])
        return nc
    vks = em.vec(lambda e: e.tensor_copy(stq32[:, :], pstat[:, :]),
                 waits=[("tch", t_kstats)])
    uk = em.act(lambda e: e.activation(u32[:, :], stq32[:, :], AF.Sqrt,
                                       bias=epsr[0:32, 0:1]),
                waits=[("vch", vks), ("vch", v_epsr)])
    em.vec(lambda e: e.reciprocal(inv32[:, :], u32[:, :]), waits=[("ach", uk)])
    vk_inv = em.vec(lambda e: e.tensor_scalar_mul(inv32[:, :], inv32[:, :], SQRT128))
    kstat_free = vk_inv

    if trunc == "kstat":
        em.finalize([("vector", "vch", em.cnt["vch"])])
        return nc
    kn_store_p = [None, None]
    prev_vb = vk_inv
    for h in range(NKV):
        for th in range(2):
            em.mm(lambda e, h=h, th=th:
                  e.matmul(bank(PS, 6 + th), hsels[:, 128 * h:128 * h + 128],
                           inv32[:, 512 * th:512 * th + 512], start=True, stop=True),
                  waits=[("vch", prev_vb)], inc=(th == 1))
        tb = em.cnt["tch"]
        vb = 0
        for th in range(2):
            vb = em.vec(lambda e, th=th: e.tensor_copy(
                invqb[:, 512 * th:512 * th + 512], bank(PS, 6 + th)),
                waits=[("tch", tb)])
        i2 = h % 2
        prev_vb = vb
        ev = em.vec(lambda e, h=h, i2=i2: e.scalar_tensor_tensor(
            hbuf[i2][:, :], kraw[:, TPC * h:TPC * h + TPC], knwc[:, 0:1],
            invqb[:, :], ALU.mult, ALU.mult),
            waits=[("vch", vb), kn_store_p[i2], l_knw])
        kn_store_p[i2] = em.dma("sync", kt[h, :, :], hbuf[i2][:, :], waits=[("vch", ev)], stream=f"kst{i2}")

    if trunc == "kproj":
        em.finalize([("vector", "vch", em.cnt["vch"])])
        return nc
    v_psum_free = em.tmark(waits=[("vch", em.cnt["vch"])])
    v_store_p = [kn_store_p[0], kn_store_p[1]]
    for ttg in range(2):
        wb_free = [em.cnt["tch"], em.cnt["tch"]]
        for i in range(32):
            i2 = i % 2
            wb = wv_b[i2]
            ld = em.dma("gpsimd", wb[:, :], wvt[128 * i:128 * i + 128, :],
                        waits=[("tch", wb_free[i2])], stream=f"wv{i2}")
            for tt in range(4):
                t_idx = 4 * ttg + tt
                for th in range(2):
                    wl = []
                    if tt == 0 and th == 0:
                        wl.append(ld)
                    if i == 0 and tt == 0 and th == 0:
                        wl.append(("tbar", v_psum_free))
                    mk = em.mm(lambda e, i=i, tt=tt, th=th, t_idx=t_idx, wb=wb:
                               e.matmul(bank(PS, 2 * tt + th),
                                        zt[:, TPC * i + 128 * t_idx:TPC * i + 128 * t_idx + 128],
                                        wb[:, 512 * th:512 * th + 512],
                                        start=(i == 0), stop=(i == 31)),
                               waits=wl, inc=(tt == 3 and th == 1))
            wb_free[i2] = mk
        tv = em.cnt["tch"]
        for tt in range(4):
            t_idx = 4 * ttg + tt
            i2 = tt % 2
            em.vec(lambda e, tt=tt, i2=i2: e.tensor_copy(
                hbuf[i2][:, 0:512], bank(PS, 2 * tt)),
                waits=[("tch", tv), v_store_p[i2]])
            ev2 = em.vec(lambda e, tt=tt, i2=i2: e.tensor_copy(
                hbuf[i2][:, 512:1024], bank(PS, 2 * tt + 1)))
            v_store_p[i2] = em.dma("sync", vv[128 * t_idx:128 * t_idx + 128, :],
                                   hbuf[i2][:, :], waits=[("vch", ev2)], stream=f"vst{i2}")
        v_psum_free = em.tmark(waits=[("vch", em.cnt["vch"])])

    if trunc == "v":
        em.finalize([("vector", "vch", em.cnt["vch"])])
        return nc
    q_psum_free = v_psum_free
    qstats_mark = kstats_mark
    qraw_store = [v_store_p[0], v_store_p[1]]
    for g in range(16):
        heads = (2 * g, 2 * g + 1)
        wb_free = [em.cnt["tch"], em.cnt["tch"]]
        for i in range(32):
            i2 = i % 2
            wb = wq_b[i2]
            ld = em.dma("gpsimd", wb[:, 0:256],
                        wqt[128 * i:128 * i + 128, HD * heads[0]:HD * heads[0] + 256],
                        waits=[("tch", wb_free[i2])], stream=f"wq{i2}")
            for hh in range(2):
                for th in range(2):
                    wl = []
                    if hh == 0 and th == 0:
                        wl.append(ld)
                    if i == 0 and hh == 0 and th == 0:
                        wl.append(("tbar", q_psum_free))
                    mk = em.mm(lambda e, hh=hh, th=th, i=i, wb=wb:
                               e.matmul(PS[0:128, (2 * hh + th) * 512:(2 * hh + th) * 512 + 512],
                                        wb[:, 128 * hh:128 * hh + 128],
                                        zt[:, TPC * i + 512 * th:TPC * i + 512 * th + 512],
                                        start=(i == 0), stop=(i == 31)),
                               waits=wl, inc=(hh == 1 and th == 1))
            wb_free[i2] = mk
        tdone = em.cnt["tch"]
        for hh, h in enumerate(heads):
            qev = em.vec(lambda e, hh=hh: e.tensor_copy(
                hbuf[hh][:, :],
                PS[0:128, (2 * hh) * 512:(2 * hh) * 512 + 1024]),
                waits=[("tch", tdone), qraw_store[hh]])
            sv = em.vec(lambda e, hh=hh: e.tensor_tensor(
                sqq[:, :], hbuf[hh][:, :], hbuf[hh][:, :], ALU.mult),
                waits=[("tch", qstats_mark)])
            for th in range(2):
                qstats_mark = em.mm(lambda e, h=h, th=th:
                      e.matmul(pstat[:, 512 * th:512 * th + 512],
                               qsels[:, 32 * h:32 * h + 32],
                               sqq[:, 512 * th:512 * th + 512],
                               start=(h == 0), stop=(h == 31)),
                      waits=[("vch", sv), ("vch", kstat_free)] if th == 0 else (),
                      inc=(th == 1)) or qstats_mark
            qraw_store[hh] = em.dma("sync", qraw_d[h, :, :], hbuf[hh][:, :],
                                    waits=[("vch", qev)], stream=f"qrst{hh}")
        q_psum_free = em.tmark(waits=[("vch", em.cnt["vch"])])
    t_qstats = em.cnt["tch"]

    vqs = em.vec(lambda e: e.tensor_copy(stq32[:, :], pstat[:, :]),
                 waits=[("tch", t_qstats)])
    uq = em.act(lambda e: e.activation(u32[:, :], stq32[:, :], AF.Sqrt,
                                       bias=epsr[0:32, 0:1]),
                waits=[("vch", vqs)])
    vq_inv = em.vec(lambda e: e.reciprocal(inv32[:, :], u32[:, :]), waits=[("ach", uq)])

    qn_store_p = [None, None]
    qn_read = [em.cnt["vch"], em.cnt["vch"]]
    prev_vb = vq_inv
    for h in range(NH):
        for th in range(2):
            em.mm(lambda e, h=h, th=th:
                  e.matmul(bank(PS, 6 + th), hsels[:, 128 * (h % 32):128 * (h % 32) + 128],
                           inv32[:, 512 * th:512 * th + 512], start=True, stop=True),
                  waits=[("vch", prev_vb)], inc=(th == 1))
        tb = em.cnt["tch"]
        vb = 0
        for th in range(2):
            vb = em.vec(lambda e, th=th: e.tensor_copy(
                invqb[:, 512 * th:512 * th + 512], bank(PS, 6 + th)),
                waits=[("tch", tb)])
        i2 = h % 2
        lr = em.dma("gpsimd", hbuf[i2][:, :], qraw_d[h, :, :],
                    waits=[("vch", qn_read[i2])] +
                          ([qraw_store[0], qraw_store[1]] if h < 2 else []),
                    stream=f"qlr{i2}")
        prev_vb = vb
        ev = em.vec(lambda e, h=h, i2=i2: e.scalar_tensor_tensor(
            qout[i2][:, :], hbuf[i2][:, :], qnwc[:, 0:1],
            invqb[:, :], ALU.mult, ALU.mult),
            waits=[("vch", vb), lr, l_qnw, qn_store_p[i2]])
        qn_read[i2] = ev
        qn_store_p[i2] = em.dma("sync", qt[h, :, :], qout[i2][:, :], waits=[("vch", ev)], stream=f"qst{i2}")

    em.finalize([("vector", "vch", em.cnt["vch"])])
    return nc




def build_l2(core_mod=None):
    nc = bass.Bass("TRN2", target_bir_lowering=False)
    ktall = nc.dram_tensor("ktall", [NKV, HD, S], BF16, kind="ExternalInput").ap()
    vpe = nc.dram_tensor("vpe", [NKV, 128, 64, 128], BF16, kind="ExternalInput").ap()
    qt = nc.dram_tensor("qt", [NH, HD, TPC], BF16, kind="ExternalInput").ap()
    x1t = nc.dram_tensor("x1t", [D, TPC], F32, kind="ExternalInput").ap()
    wot = nc.dram_tensor("wot", [D, D], F32, kind="ExternalInput").ap()
    wpt = nc.dram_tensor("wpt", [D // 2, D], U8, kind="ExternalInput").ap()
    zp32 = nc.dram_tensor("zp32", [32, 128], F32, kind="ExternalInput").ap()
    bf32 = nc.dram_tensor("bf32", [32, 128], F32, kind="ExternalInput").ap()
    sc32 = nc.dram_tensor("sc32", [32, 128], F32, kind="ExternalInput").ap()
    albl = nc.dram_tensor("albl", [NH, 6, TPC], BF16, kind="ExternalInput").ap()
    albr = nc.dram_tensor("albr", [NH, 6, S], BF16, kind="ExternalInput").ap()
    mtin = nc.dram_tensor("mtin", [128, 8 * 128], BF16, kind="ExternalInput").ap()
    identb = nc.dram_tensor("identb", [128, 128], BF16, kind="ExternalInput").ap()
    identf = nc.dram_tensor("identf", [128, 128], F32, kind="ExternalInput").ap()
    sel2 = nc.dram_tensor("sel2", [2, 256], F32, kind="ExternalInput").ap()
    onescol2 = nc.dram_tensor("onescol2", [128, 4], BF16, kind="ExternalInput").ap()

    yt = nc.dram_tensor("yt", [D, TPC], F32, kind="ExternalOutput").ap()
    o_park = nc.dram_tensor("o_park", [TPC, D], BF16, kind="Internal").ap()
    x2t = nc.dram_tensor("x2t", [D, TPC], F32, kind="Internal").ap()

    em = Em(nc)
    es = em.es
    sbf = lambda name, shape, dt: es.enter_context(nc.sbuf_tensor(name, shape, dt))
    PS = es.enter_context(nc.psum_tensor("PS", [128, 4096], F32))

    F = {}
    F["wpt"] = wpt
    F["zpb2"] = sbf("zpb2", [2, D], BF16)
    F["zrow2"] = sbf("zrow2", [2, TPC], BF16)
    F["hs"] = sbf("hs", [128, 32], F32)
    F["ext_free"] = [0, 0]
    oc2e = sbf("oc2e", [128, 4], BF16)
    sel2e = sbf("sel2e", [2, 256], F32)
    epse = sbf("epse", [128, 1], F32)
    srow1b = sbf("srow1b", [1, TPC], BF16)
    zp32s = sbf("zp32s", [32, 128], F32)
    bf32s = sbf("bf32s", [32, 128], F32)
    nzp32 = sbf("nzp32", [32, 128], BF16)
    bs32 = sbf("bs32", [32, 128], BF16)
    rs32 = sbf("rs32", [32, 128], F32)
    sc32s = sbf("sc32s", [32, 128], F32)
    identfs = sbf("identfs", [128, 128], F32)
    identbs = sbf("identbs", [128, 128], BF16)
    ones1 = sbf("ones1", [128, 1], BF16)
    neginf = sbf("neginf", [128, 1], F32)
    mbuf = [sbf("mbuf0", [128, 1], F32), sbf("mbuf1", [128, 1], F32),
            sbf("mbuf2", [128, 1], F32), sbf("mbuf3", [128, 1], F32)]
    negm_b = [sbf("negm_b0", [128, 1], F32), sbf("negm_b1", [128, 1], F32)]
    lbuf = [sbf("lbuf0", [128, 1], F32), sbf("lbuf1", [128, 1], F32)]
    linv_b = [sbf("linv_b0", [128, 1], F32), sbf("linv_b1", [128, 1], F32)]
    osb = [sbf("osb0", [128, 128], BF16), sbf("osb1", [128, 128], BF16)]
    mt = sbf("mt", [128, 8 * 128], BF16)

    l_oc2 = em.dma("sync", oc2e[:, :], onescol2[:, :])
    l_sel2 = em.dma("sync", sel2e[:, :], sel2[:, :])
    l_zpf = em.dma("sync", zp32s[:, :], zp32[:, :])
    l_bff = em.dma("sync", bf32s[:, :], bf32[:, :])
    l_sc32 = em.dma("sync", sc32s[:, :], sc32[:, :])
    l_idf = em.dma("sync", identfs[:, :], identf[:, :])
    l_idb = em.dma("sync", identbs[:, :], identb[:, :])
    l_mt = em.dma("sync", mt[:, :], mtin[:, :])
    em.vec(lambda e: e.memset(ones1[:, :], 1.0))
    eps_ready_v = em.vec(lambda e: e.memset(epse[:, :], EPS_LN))
    em.vec(lambda e: e.memset(neginf[:, :], -3.0e38))
    vnzp = em.vec(lambda e: e.tensor_scalar_mul(nzp32[:, :], zp32s[:, :], -1.0),
                  waits=[l_zpf])
    em.vec(lambda e: e.reciprocal(rs32[:, :], sc32s[:, :]), waits=[l_sc32])
    vdiv = em.vec(lambda e: e.tensor_tensor(bs32[:, :], bf32s[:, :], rs32[:, :],
                                            ALU.mult), waits=[l_bff])
    d_zpb0 = em.dma("gpsimd", F["zpb2"][0:1, :], nzp32[:, :], waits=[("vch", vnzp)], stream="zpb0")
    d_zpb1 = em.dma("gpsimd", F["zpb2"][1:2, :], bs32[:, :], waits=[("vch", vdiv)], stream="zpb1")
    vone = em.vec(lambda e: e.memset(srow1b[0:1, 0:TPC], 1.0))
    F["zr1_done"] = em.dma("gpsimd", F["zrow2"][1:2, :], srow1b[0:1, 0:TPC],
                           waits=[("vch", vone), d_zpb0, d_zpb1], stream="zr1")
    t_hs = em.mm(lambda e: e.transpose(bank(PS, 7)[:, 0:32], sc32s[:, :], identfs[0:32, 0:32]),
                 waits=[l_idf, l_sc32], inc=True)
    v_hs = em.vec(lambda e: e.tensor_scalar_mul(F["hs"][:, :], bank(PS, 7)[:, 0:32], 0.5),
                  waits=[("tch", t_hs)])
    pro_gq = em.qbar("gq")

    es_a = ExitStack()
    sba = lambda name, shape, dt: es_a.enter_context(nc.sbuf_tensor(name, shape, dt))
    kth = sba("kth", [128, S], BF16)
    vh = sba("vh", [128, S], BF16)
    qth = sba("qth", [128, TPC], BF16)
    arh = sba("arh", [6, S], BF16)
    alh = sba("alh", [6, TPC], BF16)
    srow = [sba("srow0", [128, S], F32), sba("srow1", [128, S], F32)]
    prow = [sba("prow0", [128, S], BF16), sba("prow1", [128, S], BF16)]
    ptrow = sba("ptrow", [128, 2 * 2048], BF16)

    kv_last_use = 0
    q_last_use = 0
    exp_p = [0, 0]
    recip_p = [0, 0]
    avmark_p = [0, 0]
    oev_p = [0, 0]
    ost_p = [None, None]
    wave_slot_user = [0, 0]
    wave_i = 0
    psum_free = [v_hs] * 4
    combo_i = 0

    for h in range(NKV):
        ld_k = em.dma("gpsimd", kth[:, :], ktall[h, :, :],
                      waits=[("tch", kv_last_use)], stream="kth")
        ld_v = em.dma("gpsimd", vh[:, :], vpe[h, :, :, :],
                      waits=[("tch", kv_last_use)], stream="vh")
        for qh in range(4 * h, 4 * h + 4):
            ld_q = em.dma("gpsimd", qth[:, :], qt[qh, :, :],
                          waits=[("tch", q_last_use)], stream="qth")
            ld_ar = em.dma("gpsimd", arh[:, :], albr[qh, :, :],
                           waits=[("tch", q_last_use)], stream="arh")
            ld_al = em.dma("gpsimd", alh[:, :], albl[qh, :, :],
                           waits=[("tch", q_last_use)], stream="alh")
            def emit_pass1(T):
                nonlocal combo_i
                nch = 2 * (T + 1)
                par = combo_i % 2
                sr = srow[par]
                for k in range(nch):
                    b = k % 4
                    wl = [("vch", psum_free[b])]
                    if k == 0:
                        wl += [ld_k, ld_v, ld_q, ld_ar, ld_al, l_mt, l_idb,
                               ("ach", exp_p[par])]
                    em.mm(lambda e, T=T, k=k, b=b:
                          e.matmul(bank(PS, b), alh[0:6, 128 * T:128 * T + 128],
                                   arh[0:6, 512 * k:512 * k + 512],
                                   start=True, stop=False),
                          waits=wl)
                    diag = (k >= 2 * T)
                    if diag:
                        for u in range(4):
                            blk = 4 * (k - 2 * T) + u
                            em.mm(lambda e, b=b, u=u, blk=blk:
                                  e.matmul(bank(PS, b)[:, 128 * u:128 * u + 128],
                                           identbs[:, :], mt[:, 128 * blk:128 * blk + 128],
                                           start=False, stop=False))
                    mk = em.mm(lambda e, T=T, k=k, b=b:
                               e.matmul(bank(PS, b), qth[:, 128 * T:128 * T + 128],
                                        kth[:, 512 * k:512 * k + 512],
                                        start=False, stop=True),
                               inc=True)
                    ev = em.vec(lambda e, k=k, b=b, sr=sr, par=par, first=(k == 0):
                                e.tensor_scalar(sr[:, 512 * k:512 * k + 512],
                                                bank(PS, b), 0.0,
                                                neginf[:, 0:1] if first else mbuf[2 * par + (k - 1) % 2][:, 0:1],
                                                ALU.add, ALU.max,
                                                accum_out=mbuf[2 * par + k % 2][:, 0:1]),
                                waits=[("tch", mk)] + ([("ach", exp_p[par])] if k == 0 else []))
                    psum_free[b] = ev
                st = dict(T=T, par=par, combo=combo_i, qh=qh)
                combo_i += 1
                return st

            def emit_pass2(st):
                nonlocal wave_i
                T = st["T"]; par = st["par"]; qh_ = st["qh"]
                sr = srow[par]; pr = prow[par]
                nch = 2 * (T + 1)
                L = 1024 * (T + 1)
                vneg = em.vec(lambda e, k=nch - 1, par=par: e.tensor_scalar_mul(
                    negm_b[par][:, :], mbuf[2 * par + k % 2][:, :], -1.0))
                expv = em.act(lambda e, L=L, par=par, sr=sr, pr=pr: e.activation(
                    pr[:, 0:L], sr[:, 0:L], AF.Exp, bias=negm_b[par][:, 0:1],
                    accum_out=lbuf[par][:, 0:1]),
                    waits=[("vch", vneg), ("vch", recip_p[par]), ("tch", avmark_p[par])])
                exp_p[par] = expv
                vrec = em.vec(lambda e, par=par: e.reciprocal(linv_b[par][:, :], lbuf[par][:, :]),
                              waits=[("ach", expv)])
                recip_p[par] = vrec
                nblk = 8 * (T + 1)
                av = None
                g0 = 0
                while g0 < nblk:
                    gn = min(16, nblk - g0)
                    slot = wave_i % 2
                    trt = em.dmat(
                        "sync",
                        ptrow.ap()[:, 2048 * slot:2048 * slot + 128 * gn].rearrange(
                            "p (b f) -> p b f", b=gn),
                        pr.ap()[:, 128 * g0:128 * g0 + 128 * gn].rearrange(
                            "p (b f) -> p b f", b=gn),
                        waits=[("ach", expv), ("tch", wave_slot_user[slot])]
                        + (list(pro_gq) if st["combo"] == 0 and g0 == 0 else []),
                        stream=f"tr{slot}")
                    for j in range(gn):
                        bb = g0 + j
                        wl = [trt] if j == 0 else []
                        if bb == 0:
                            wl.append(("vch", oev_p[par]))
                        av = em.mm(lambda e, bb=bb, nblk=nblk, slot=slot, j=j, par=par:
                                   e.matmul(PS[0:128, (4 + par) * 512:(4 + par) * 512 + 128],
                                            ptrow[:, 2048 * slot + 128 * j:2048 * slot + 128 * j + 128],
                                            vh[:, 128 * bb:128 * bb + 128],
                                            start=(bb == 0), stop=(bb == nblk - 1)),
                                   waits=wl, inc=(j == gn - 1))
                    wave_slot_user[slot] = av
                    wave_i += 1
                    g0 += gn
                avmark_p[par] = av
                oev = em.vec(lambda e, par=par: e.tensor_scalar(
                    osb[par][:, :], PS[0:128, (4 + par) * 512:(4 + par) * 512 + 128],
                    linv_b[par][:, 0:1], None, ALU.mult),
                    waits=[("tch", av), ("vch", vrec), ost_p[par]])
                oev_p[par] = oev
                ost_p[par] = em.dma("sync",
                                    o_park[128 * T:128 * T + 128,
                                           128 * qh_:128 * qh_ + 128],
                                    osb[par][:, :], waits=[("vch", oev)],
                                    stream=f"ost{par}")
                return av

            pend = None
            for T in range(8):
                st = emit_pass1(T)
                if pend is not None:
                    emit_pass2(pend)
                pend = st
            av_last = emit_pass2(pend)
            kv_last_use = av_last
            q_last_use = av_last

    attn_done = [("ach", max(exp_p)), ("tch", max(avmark_p)),
                 ("vch", max(oev_p)), ost_p[0], ost_p[1]]
    es_a.close()

    es_w = ExitStack()
    sbw = lambda name, shape, dt: es_w.enter_context(nc.sbuf_tensor(name, shape, dt))
    lb = LNBufs(em, sbw, PS)
    lb.onescol2 = oc2e
    lb.sel2 = sel2e
    lb.epsc = epse
    lb.eps_ready = eps_ready_v
    F["wby"] = [sbw("wby0", [128, 512], U8), sbw("wby1", [128, 512], U8)]
    F["lo8"] = [sbw("lo80", [128, 512], U8), sbw("lo81", [128, 512], U8)]
    F["hi8"] = [sbw("hi80", [128, 512], U8), sbw("hi81", [128, 512], U8)]
    F["lo"] = [sbw("lo0", [128, 512], BF16), sbw("lo1", [128, 512], BF16)]
    F["hi"] = [sbw("hi0", [128, 512], BF16), sbw("hi1", [128, 512], BF16)]
    F["x1b"] = [sbw("x1b0", [128, 512], F32), sbw("x1b1", [128, 512], F32)]
    F["x1o"] = [sbw("x1o0", [128, 512], F32), sbw("x1o1", [128, 512], F32)]
    zt = sbw("zt", [128, 32 * TPC], BF16)
    wob = [sbw("wob0", [128, 512], BF16), sbw("wob1", [128, 512], BF16)]
    otb = [sbw("otb00", [128, 512], BF16), sbw("otb01", [128, 512], BF16),
           sbw("otb10", [128, 512], BF16), sbw("otb11", [128, 512], BF16)]
    x2b = [sbw("x2b0", [128, 512], F32), sbw("x2b1", [128, 512], F32)]
    x2o = [sbw("x2o0", [128, 512], F32), sbw("x2o1", [128, 512], F32)]

    wo_psum_free = [max(oev_p)] * 8
    wob_free = [0, 0]
    otb_free = [0, 0, 0, 0]
    x2b_free = [em.cnt["vch"], em.cnt["vch"]]
    x2o_free = [None, None]
    x2_last = None
    for og in range(8):
        for dc in range(32):
            i2 = dc % 2
            wb = wob[i2]
            lw = em.dma("gpsimd", wb[:, :],
                        wot[128 * dc:128 * dc + 128, 512 * og:512 * og + 512],
                        waits=[("tch", wob_free[i2])] +
                              (attn_done if (og == 0 and dc == 0) else []),
                        stream=f"wob{i2}")
            ots = []
            for th in range(2):
                oi = 2 * (dc % 2) + th
                lo_ = em.dmat("sync", otb[oi][:, :],
                              o_park[512 * th:512 * th + 512, 128 * dc:128 * dc + 128],
                              waits=[("tch", otb_free[oi])] +
                                    (attn_done if (og == 0 and dc < 2) else []),
                              stream=f"ot{oi}")
                ots.append(lo_)
            for th in range(2):
                for oc in range(4):
                    psi = 4 * th + oc
                    oi = 2 * (dc % 2) + th
                    wl = []
                    if oc == 0:
                        wl = [lw, ots[th]]
                    if dc == 0 and th == 0 and oc == 0:
                        wl.append(("vch", max(wo_psum_free)))
                    mk = em.mm(lambda e, psi=psi, oc=oc, oi=oi, wb=wb, dc=dc:
                               e.matmul(bank(PS, psi), wb[:, 128 * oc:128 * oc + 128],
                                        otb[oi][:, :], start=(dc == 0), stop=(dc == 31)),
                               waits=wl, inc=(th == 1 and oc == 3))
            wob_free[i2] = mk
            otb_free[2 * (dc % 2)] = mk
            otb_free[2 * (dc % 2) + 1] = mk
        tm = em.cnt["tch"]
        for th in range(2):
            for oc in range(4):
                psi = 4 * th + oc
                oco = 4 * og + oc
                i2 = psi % 2
                lx = em.dma("gpsimd", x2b[i2][:, :],
                            x1t[128 * oco:128 * oco + 128, 512 * th:512 * th + 512],
                            waits=[("vch", x2b_free[i2])], stream=f"x2l{i2}")
                ev = em.vec(lambda e, psi=psi, i2=i2: e.tensor_tensor(
                    x2o[i2][:, :], bank(PS, psi), x2b[i2][:, :], ALU.add),
                    waits=[("tch", tm), lx, x2o_free[i2]])
                wo_psum_free[psi] = ev
                x2b_free[i2] = ev
                x2_last = em.dma("sync", x2t[128 * oco:128 * oco + 128,
                                             512 * th:512 * th + 512],
                                 x2o[i2][:, :], waits=[("vch", ev)], stream=f"x2s{i2}")
                x2o_free[i2] = x2_last

    z3 = emit_ln_z(em, lb, x2t, zt, prior_psum_free=em.cnt["vch"],
                   gate_dma=em.qbar("sq"))
    sz3 = emit_sumz(em, lb, zt, F["zrow2"], ones1, sumz_gate=z3)
    F["xin_gate"] = em.qbar("sq")
    emit_ffn(em, lb, F, zt, z3, sz3, x2t, yt)

    es_w.close()
    em.finalize([("vector", "vch", em.cnt["vch"])])
    return nc


def _consts_l1(inputs):
    Wq = np.asarray(inputs["L0_attn_Wq"])
    Wk = np.asarray(inputs["L0_attn_Wk"])
    Wv = np.asarray(inputs["L0_attn_Wv"])
    wp = np.asarray(inputs["L0_ffn_weight_packed"])
    scale = np.asarray(inputs["L0_ffn_scale"])[:, 0]
    zp = np.asarray(inputs["L0_ffn_zero_point"])[:, 0]
    biasf = np.asarray(inputs["L0_ffn_bias"])
    qn = np.asarray(inputs["L0_attn_q_norm_w"])
    kn = np.asarray(inputs["L0_attn_k_norm_w"])

    wqt = np.ascontiguousarray(Wq.T[PI])
    wkt = np.ascontiguousarray(Wk.T[PI])
    wvt = np.ascontiguousarray(Wv.T[PI])
    bytes_ = np.ascontiguousarray(wp).view(np.uint8).reshape(-1, 4)[:, 0]
    wpt = np.ascontiguousarray(bytes_.reshape(D, D // 2).T[:, PI])
    scp = np.ascontiguousarray(scale[PI])
    zpp = np.ascontiguousarray(zp[PI])
    bfp = np.ascontiguousarray(biasf[PI])

    sel2 = np.zeros((2, 256), np.float32)
    sel2[0, 0:128] = 1.0
    sel2[1, 128:256] = 1.0
    onescol2 = np.zeros((128, 4), BF)
    onescol2[:, 0] = 1
    onescol2[:, 3] = 1
    identf = np.eye(128, dtype=np.float32)
    qsel = np.zeros((128, 32 * 32), BF)
    for h in range(32):
        qsel[:, 32 * h + h] = 1
    hsel = np.zeros((32, 32 * 128), np.float32)
    for h in range(32):
        hsel[h, 128 * h:128 * h + 128] = 1

    return dict(wqt=wqt, wkt=wkt, wvt=wvt, wpt=wpt,
                zp32=np.ascontiguousarray(zpp.reshape(32, 128)),
                bf32=np.ascontiguousarray(bfp.reshape(32, 128)),
                sc32=np.ascontiguousarray(scp.reshape(32, 128)),
                sel2=sel2, onescol2=onescol2, identf=identf,
                qnw=np.ascontiguousarray(qn[None, :]),
                knw=np.ascontiguousarray(kn[None, :]),
                qselt=qsel, hsel=hsel)


def stage_l1(inputs):
    x = np.asarray(inputs["L0_attn_x"])[0]
    shared = _consts_l1(inputs)
    in_maps = []
    for c in range(NC):
        m = dict(shared)
        m["xt"] = np.ascontiguousarray(x[c::NC].T[PI])
        in_maps.append(m)
    return in_maps


_CACHE = {}


def run_l1(inputs):
    in_maps = stage_l1(inputs)
    if "l1" not in _CACHE:
        _CACHE["l1"] = build_l1()
    res = run_bass_kernel_spmd(_CACHE["l1"], in_maps, core_ids=list(range(NC)))
    return res.results


def stage_l2(inputs, r1):
    Wo = np.asarray(inputs["L0_attn_Wo"])
    wot = np.ascontiguousarray(Wo.T[:, PI])
    c1 = _consts_l1(inputs)

    ktall = np.empty((NKV, HD, S), BF)
    vall = np.empty((NKV, S, HD), BF)
    for c in range(NC):
        ktall[:, :, c::NC] = np.asarray(r1[c]["kt"]).reshape(NKV, HD, TPC)
        vall[:, c::NC, :] = np.asarray(r1[c]["vv"]).reshape(TPC, NKV, HD).transpose(1, 0, 2)
    vpe = np.ascontiguousarray(vall.reshape(NKV, 64, 128, HD).transpose(0, 2, 1, 3))

    slopes = _alibi_slopes(NH)
    g = np.arange(S, dtype=np.float64)
    albr = np.zeros((NH, 6, S), BF)
    for qh in range(NH):
        s0, s1, s2 = _split3(slopes[qh] * g)
        albr[qh, 0] = s0
        albr[qh, 1] = s1
        albr[qh, 2] = s2
        albr[qh, 3:6] = -1.0

    identb = np.eye(128, dtype=BF)

    shared = dict(wot=wot, wpt=c1["wpt"], zp32=c1["zp32"], bf32=c1["bf32"],
                  sc32=c1["sc32"], albr=albr, identb=identb,
                  identf=c1["identf"], sel2=c1["sel2"], onescol2=c1["onescol2"])
    in_maps = []
    jj = np.arange(TPC, dtype=np.float64)
    ii = np.arange(128)
    for c in range(NC):
        m = dict(shared)
        m["ktall"] = ktall
        m["vpe"] = vpe
        m["qt"] = np.asarray(r1[c]["qt"]).reshape(NH, HD, TPC)
        m["x1t"] = np.asarray(r1[c]["x1t"])
        p = 8.0 * jj + c
        albl = np.zeros((NH, 6, TPC), BF)
        for qh in range(NH):
            s0, s1, s2 = _split3(slopes[qh] * p)
            albl[qh, 0:3] = 1.0
            albl[qh, 3] = s0
            albl[qh, 4] = s1
            albl[qh, 5] = s2
        m["albl"] = albl
        mt = np.zeros((128, 8 * 128), np.float32)
        for bb in range(8):
            allowed = (128 * bb + ii[None, :]) <= (8 * np.arange(128)[:, None] + c)
            mt[:, 128 * bb:128 * bb + 128] = np.where(allowed, 0.0, NEG_BIG)
        m["mtin"] = mt.astype(BF)
        in_maps.append(m)
    return in_maps


def kernel(**inputs):
    in1 = stage_l1(inputs)
    if "l1" not in _CACHE:
        _CACHE["l1"] = build_l1()
    r1 = run_bass_kernel_spmd(_CACHE["l1"], in1, core_ids=list(range(NC))).results
    in2 = stage_l2(inputs, r1)
    if "l2" not in _CACHE:
        _CACHE["l2"] = build_l2()
    r2 = run_bass_kernel_spmd(_CACHE["l2"], in2, core_ids=list(range(NC))).results
    out = np.empty((S, D), np.float32)
    for c in range(NC):
        xc = np.empty((TPC, D), np.float32)
        xc[:, PI] = np.asarray(r2[c]["yt"]).T
        out[c::NC] = xc
    return out[None, :, :]


# version 07
# speedup vs baseline: 1.8797x; 1.0008x over previous
import math
import numpy as np
import ml_dtypes
from contextlib import ExitStack

import concourse.bass as bass
import concourse.mybir as mybir
from concourse.bass_utils import run_bass_kernel_spmd

F32 = mybir.dt.float32
BF16 = mybir.dt.bfloat16
U8 = mybir.dt.uint8
AF = mybir.ActivationFunctionType
ALU = mybir.AluOpType
BF = ml_dtypes.bfloat16

NC = 8
S = 8192
D = 4096
TPC = S // NC
NH = 32
NKV = 8
HD = 128
EPS_LN = 1e-5
EPS_RMS = 1e-6
NEG_BIG = -1.0e30
SQRT128 = float(np.sqrt(128.0))

PI = np.arange(D).reshape(D // 256, 128, 2).transpose(0, 2, 1).reshape(-1)

ENGINES = ("sync", "tensor", "vector", "scalar", "gpsimd")


def _alibi_slopes(n_heads):
    start = 2 ** (-(2 ** -(math.log2(n_heads) - 3)))
    return np.array([start * start ** i for i in range(n_heads)], dtype=np.float64)


def _split3(v):
    s0 = v.astype(BF).astype(np.float64)
    s1 = (v - s0).astype(BF).astype(np.float64)
    s2 = (v - s0 - s1).astype(BF)
    return s0.astype(BF), s1.astype(BF), s2


class Em:

    def __init__(self, nc):
        self.nc = nc
        self.es = ExitStack()
        self.ops = {e: [] for e in ENGINES}
        self.sems = {}
        self.cnt = {}
        for name in ("vch", "ach", "pch", "tch", "tbar"):
            self.sems[name] = self.es.enter_context(nc.semaphore(name))
            self.cnt[name] = 0
        self.qstreams = {"sq": set(), "gq": set(), "aq": set()}
        self.auto_i = 0

    def _app(self, eng, fn):
        self.ops[eng].append(fn)

    def _waits(self, waits):
        out = []
        for w in waits:
            if w is None:
                continue
            (n, v) = w
            if v > 0:
                out.append((self.sems[n], v))
        return tuple(out)

    def wait(self, eng, sem_name, val):
        s = self.sems[sem_name]
        self._app(eng, lambda e, s=s, v=val: e.wait_ge(s, v))

    def _chain(self, eng, ch, fn, waits=()):
        self.cnt[ch] += 1
        tgt = self.cnt[ch]
        sem = self.sems[ch]
        w = self._waits(waits)

        def run(e, fn=fn, sem=sem, tgt=tgt, w=w):
            for (ws, wv) in w:
                e.wait_ge(ws, wv)
            if tgt > 1:
                e.wait_ge(sem, tgt - 1)
            fn(e).then_inc(sem, 1)

        self._app(eng, run)
        return tgt

    def vec(self, fn, waits=()):
        return self._chain("vector", "vch", fn, waits)

    def act(self, fn, waits=()):
        return self._chain("scalar", "ach", fn, waits)

    def pool(self, fn, waits=()):
        return self._chain("gpsimd", "pch", fn, waits)

    def mm(self, fn, waits=(), inc=False):
        w = self._waits(waits)
        if inc:
            self.cnt["tch"] += 1
            tgt = self.cnt["tch"]
            sem = self.sems["tch"]

            def run(e, fn=fn, w=w, sem=sem):
                for (ws, wv) in w:
                    e.wait_ge(ws, wv)
                fn(e).then_inc(sem, 1)

            self._app("tensor", run)
            return tgt
        else:
            def run(e, fn=fn, w=w):
                for (ws, wv) in w:
                    e.wait_ge(ws, wv)
                fn(e)

            self._app("tensor", run)
            return None

    def tmark(self, waits=()):
        w = self._waits(waits)
        self.cnt["tbar"] += 1
        tgt = self.cnt["tbar"]
        sem = self.sems["tbar"]

        def run(e, w=w, sem=sem):
            for (ws, wv) in w:
                e.wait_ge(ws, wv)
            e.nop().then_inc(sem, 1)

        self._app("tensor", run)
        return tgt

    def _dma_emit(self, queue, q, stream, fn, waits):
        if stream is None:
            stream = f"auto{self.auto_i}"
            self.auto_i += 1
        n = f"{q}_{stream}"
        if n not in self.sems:
            self.sems[n] = self.es.enter_context(self.nc.semaphore(n))
            self.cnt[n] = 0
            self.qstreams[q].add(n)
        self.cnt[n] += 16
        tgt = self.cnt[n]
        sem = self.sems[n]
        w = self._waits(waits)

        def run(e, fn=fn, sem=sem, w=w):
            for (ws, wv) in w:
                e.wait_ge(ws, wv)
            fn(e).then_inc(sem, 16)

        self._app(queue, run)
        return (n, tgt)

    def dma(self, queue, out, in_, waits=(), stream=None, **kw):
        q = {"sync": "sq", "gpsimd": "gq", "scalar": "aq"}[queue]
        return self._dma_emit(queue, q, stream,
                              lambda e, out=out, in_=in_, kw=kw:
                              e.dma_start(out=out, in_=in_, **kw), waits)

    def dmat(self, queue, out, in_, waits=(), stream=None):
        q = {"sync": "sq", "scalar": "aq"}[queue]
        return self._dma_emit(queue, q, stream,
                              lambda e, out=out, in_=in_:
                              e.dma_start_transpose(out, in_), waits)

    def qbar(self, q):
        return tuple((n, self.cnt[n]) for n in sorted(self.qstreams[q])
                     if self.cnt[n] > 0)

    def finalize(self, final_waits=()):
        for (eng, sem_name, val) in final_waits:
            self.wait(eng, sem_name, val)
        for (eng, q) in (("sync", "sq"), ("gpsimd", "gq"), ("scalar", "aq")):
            for (n, v) in self.qbar(q):
                self.wait(eng, n, v)
        block = self.es.enter_context(self.nc.Block())
        for eng in ENGINES:
            ops = self.ops[eng]
            if not ops:
                continue
            getattr(block, eng)(lambda e, ops=ops: [f(e) for f in ops])
        self.es.close()



class LNBufs:

    def __init__(self, em, sbf, PS):
        self.xb = [sbf("ln_xb0", [128, TPC], BF16), sbf("ln_xb1", [128, TPC], BF16)]
        self.xf = [sbf("ln_xf0", [128, TPC], F32), sbf("ln_xf1", [128, TPC], F32)]
        self.sq = [sbf("ln_sq0", [128, TPC], BF16), sbf("ln_sq1", [128, TPC], BF16)]
        self.sxs = sbf("ln_sxs", [2, 512], F32)
        self.sqs = sbf("ln_sqs", [2, 512], F32)
        self.m2 = sbf("ln_m2", [2, 512], F32)
        self.var = sbf("ln_var", [2, 512], F32)
        self.u2 = sbf("ln_u2", [2, 512], F32)
        self.inv = sbf("ln_inv", [2, 512], F32)
        self.minv = sbf("ln_minv", [2, 512], F32)
        self.tmp2 = sbf("ln_tmp2", [2, 512], F32)
        self.invb = sbf("ln_invb", [128, TPC], F32)
        self.minvb = sbf("ln_minvb", [128, TPC], F32)
        self.t1 = sbf("ln_t1", [128, TPC], F32)
        self.epsc = sbf("ln_epsc", [128, 1], F32)
        self.onescol2 = sbf("ln_oc2", [128, 4], BF16)
        self.sel2 = sbf("ln_sel2", [2, 256], F32)
        self.PS = PS


def bank(PS, i, rows=128, cols=512):
    return PS[0:rows, 512 * i:512 * i + cols]


def emit_ln_z(em, lb, xsrc_dram, zt, prior_psum_free=0, gate_dma=()):
    PS = lb.PS
    stx = bank(PS, 0, rows=2)
    stq = bank(PS, 1, rows=2)
    t0 = em.cnt["tch"]
    v0 = em.cnt["vch"]
    a0 = em.cnt["ach"]
    xb_free = [t0, t0]
    sq_free = [t0, t0]
    for c in range(32):
        i = c % 2
        xb = lb.xb[i]
        ld = em.dma("gpsimd", xb[:, :], xsrc_dram[128 * c:128 * c + 128, :],
                    waits=[("tch", xb_free[i])] + (list(gate_dma) if c < 2 else []),
                    stream=f"lnxb{i}")
        sqt = lb.sq[i]
        sv = em.act(lambda e, xb=xb, sqt=sqt: e.activation(sqt[:, :], xb[:, :], AF.Square),
                    waits=[ld, ("tch", sq_free[i])])
        for h in range(2):
            w0 = []
            if c == 0 and h == 0:
                w0 = [("vch", prior_psum_free)] + list(gate_dma)
            em.mm(lambda e, h=h, c=c, xb=xb:
                  e.matmul(stx[:, :], lb.onescol2[:, 2 * h:2 * h + 2],
                           xb[:, 512 * h:512 * h + 512],
                           start=(c == 0 and h == 0), stop=(c == 31 and h == 1)),
                  waits=w0 + ([ld] if h == 0 else []))
            mk = em.mm(lambda e, h=h, c=c, sqt=sqt:
                       e.matmul(stq[:, :], lb.onescol2[:, 2 * h:2 * h + 2],
                                sqt[:, 512 * h:512 * h + 512],
                                start=(c == 0 and h == 0), stop=(c == 31 and h == 1)),
                       waits=[("ach", sv)] if h == 0 else [], inc=(h == 1))
        xb_free[i] = mk
        sq_free[i] = mk

    tdone = em.cnt["tch"]
    em.vec(lambda e: e.tensor_copy(lb.sxs[:, :], stx[:, :]), waits=[("tch", tdone)])
    em.vec(lambda e: e.tensor_copy(lb.sqs[:, :], stq[:, :]))
    em.vec(lambda e: e.tensor_scalar_mul(lb.m2[:, :], lb.sxs[:, :], 1.0 / D))
    em.vec(lambda e: e.tensor_scalar_mul(lb.var[:, :], lb.sqs[:, :], 1.0 / D))
    em.vec(lambda e: e.tensor_tensor(lb.tmp2[:, :], lb.m2[:, :], lb.m2[:, :], ALU.mult))
    vv = em.vec(lambda e: e.tensor_tensor(lb.var[:, :], lb.var[:, :], lb.tmp2[:, :], ALU.subtract))
    uval = em.act(lambda e: e.activation(lb.u2[:, :], lb.var[:, :], AF.Sqrt,
                                         bias=lb.epsc[0:2, 0:1]),
                  waits=[("vch", vv), ("vch", lb.eps_ready)])
    em.vec(lambda e: e.reciprocal(lb.inv[:, :], lb.u2[:, :]), waits=[("ach", uval)])
    vready = em.vec(lambda e: e.tensor_tensor(lb.minv[:, :], lb.m2[:, :], lb.inv[:, :], ALU.mult))

    prev_evac = 0
    bdone = 0
    for (vecs, dst) in ((lb.inv, lb.invb), (lb.minv, lb.minvb)):
        for h in range(2):
            em.mm(lambda e, vecs=vecs, h=h:
                  e.matmul(bank(PS, 2 + h), lb.sel2[:, 128 * h:128 * h + 128],
                           vecs[:, :], start=True, stop=True),
                  waits=[("vch", vready), ("vch", prev_evac)], inc=(h == 1))
        t = em.cnt["tch"]
        for h in range(2):
            bdone = em.vec(lambda e, dst=dst, h=h:
                           e.tensor_copy(dst[:, 512 * h:512 * h + 512], bank(PS, 2 + h)),
                           waits=[("tch", t)])
        prev_evac = bdone

    xf_free = [v0, v0]
    zready = 0
    for c in range(32):
        i = c % 2
        xf = lb.xf[i]
        ld = em.dma("sync", xf[:, :], xsrc_dram[128 * c:128 * c + 128, :],
                    waits=[("vch", xf_free[i])] + (list(gate_dma) if c < 2 else []),
                    stream=f"lnxf{i}")
        em.vec(lambda e, xf=xf: e.tensor_tensor(lb.t1[:, :], xf[:, :], lb.invb[:, :], ALU.mult),
               waits=[ld, ("vch", bdone)])
        zready = em.vec(lambda e, c=c: e.tensor_tensor(
            zt[:, TPC * c:TPC * c + TPC], lb.t1[:, :], lb.minvb[:, :], ALU.subtract))
        xf_free[i] = zready
    return zready


def emit_sumz(em, lb, zt, zrow2, ones1, sumz_gate=0):
    PS = lb.PS
    t = None
    for h in range(2):
        ps = bank(PS, h, rows=1)
        for c in range(32):
            t = em.mm(lambda e, ps=ps, c=c, h=h:
                      e.matmul(ps[:, :], ones1[:, :],
                               zt[:, TPC * c + 512 * h:TPC * c + 512 * h + 512],
                               start=(c == 0), stop=(c == 31)),
                      waits=[("vch", sumz_gate)] if (c == 0 and h == 0) else (),
                      inc=(h == 1 and c == 31))
    v = 0
    for h in range(2):
        v = em.vec(lambda e, h=h: e.tensor_copy(zrow2[0:1, 512 * h:512 * h + 512],
                                                bank(PS, h, rows=1)),
                   waits=[("tch", t)])
    return v


def emit_ffn(em, lb, F, zt, z_ready, sumz_v, xin_dram, xout_dram):
    PS = lb.PS
    v0 = em.cnt["vch"]
    t0 = em.cnt["tch"]
    psum_free = [v0] * 8
    wby_free = [v0, v0]
    x1b_free = [v0, v0]
    x1o_free = [None, None]
    last_store = None
    F["ext_free"] = [t0, t0]
    for og in range(8):
        grp_started = False
        for bc in range(16):
            i2 = bc % 2
            wb = F["wby"][i2]
            ld = em.dma("sync", wb[:, :],
                        F["wpt"][128 * bc:128 * bc + 128, 512 * og:512 * og + 512],
                        waits=[("vch", wby_free[i2])], stream=f"wby{i2}")
            lo8, hi8 = F["lo8"][i2], F["hi8"][i2]
            lo, hi = F["lo"][i2], F["hi"][i2]
            em.vec(lambda e, wb=wb, lo8=lo8: e.tensor_scalar(
                lo8[:, :], wb[:, :], 15, None, ALU.bitwise_and),
                waits=[ld, ("tch", F["ext_free"][i2])])
            em.vec(lambda e, wb=wb, hi8=hi8: e.tensor_scalar(
                hi8[:, :], wb[:, :], 4, None, ALU.logical_shift_right))
            em.vec(lambda e, lo8=lo8, lo=lo: e.tensor_copy(lo[:, :], lo8[:, :]))
            vex = em.vec(lambda e, hi8=hi8, hi=hi: e.tensor_copy(hi[:, :], hi8[:, :]))
            wby_free[i2] = vex
            for par, wsb in ((0, lo), (1, hi)):
                ic = 2 * bc + par
                for th in range(2):
                    for oc in range(4):
                        psi = 4 * th + oc
                        wl = []
                        if bc == 0 and par == 0:
                            if th == 0 and oc == 0:
                                wl = [("vch", max(z_ready, sumz_v, vex)),
                                      F["zr1_done"]]
                            wl.append(("vch", psum_free[psi]))
                        elif par == 0 and th == 0 and oc == 0:
                            wl = [("vch", vex)]
                        mk = em.mm(lambda e, psi=psi, wsb=wsb, oc=oc, ic=ic, th=th,
                                   s=(bc == 0 and par == 0):
                                   e.matmul(bank(PS, psi),
                                            wsb[:, 128 * oc:128 * oc + 128],
                                            zt[:, TPC * ic + 512 * th:TPC * ic + 512 * th + 512],
                                            start=s, stop=False),
                                   waits=wl,
                                   inc=(par == 1 and th == 1 and oc == 3))
            F["ext_free"][i2] = mk
        for th in range(2):
            for oc in range(4):
                psi = 4 * th + oc
                oco = 4 * og + oc
                tm = em.mm(lambda e, psi=psi, oco=oco, th=th:
                           e.matmul(bank(PS, psi),
                                    F["zpb2"][:, 128 * oco:128 * oco + 128],
                                    F["zrow2"][:, 512 * th:512 * th + 512],
                                    start=False, stop=True),
                           inc=True)
                i2 = psi % 2
                xb_ = F["x1b"][i2]
                xo_ = F["x1o"][i2]
                ldx = em.dma("gpsimd", xb_[:, :],
                             xin_dram[128 * oco:128 * oco + 128, 512 * th:512 * th + 512],
                             waits=[("vch", x1b_free[i2])] + list(F["xin_gate"]),
                             stream=f"ffnx{i2}")
                ev = em.vec(lambda e, psi=psi, oco=oco, xb_=xb_, xo_=xo_:
                            e.scalar_tensor_tensor(xo_[:, :], bank(PS, psi),
                                                   F["hs"][:, oco:oco + 1],
                                                   xb_[:, :], ALU.mult, ALU.add),
                            waits=[("tch", tm), ldx, x1o_free[i2]])
                psum_free[psi] = ev
                x1b_free[i2] = ev
                last_store = em.dma("sync",
                                    xout_dram[128 * oco:128 * oco + 128,
                                              512 * th:512 * th + 512],
                                    xo_[:, :], waits=[("vch", ev)],
                                    stream=f"ffno{i2}")
                x1o_free[i2] = last_store
    return last_store



def build_l1(trunc=None):
    nc = bass.Bass("TRN2", target_bir_lowering=False)
    xt = nc.dram_tensor("xt", [D, TPC], F32, kind="ExternalInput").ap()
    wqt = nc.dram_tensor("wqt", [D, D], F32, kind="ExternalInput").ap()
    wkt = nc.dram_tensor("wkt", [D, NKV * HD], F32, kind="ExternalInput").ap()
    wvt = nc.dram_tensor("wvt", [D, NKV * HD], F32, kind="ExternalInput").ap()
    wpt = nc.dram_tensor("wpt", [D // 2, D], U8, kind="ExternalInput").ap()
    zp32 = nc.dram_tensor("zp32", [32, 128], F32, kind="ExternalInput").ap()
    bf32 = nc.dram_tensor("bf32", [32, 128], F32, kind="ExternalInput").ap()
    sc32 = nc.dram_tensor("sc32", [32, 128], F32, kind="ExternalInput").ap()
    qnw = nc.dram_tensor("qnw", [1, 128], F32, kind="ExternalInput").ap()
    knw = nc.dram_tensor("knw", [1, 128], F32, kind="ExternalInput").ap()
    sel2 = nc.dram_tensor("sel2", [2, 256], F32, kind="ExternalInput").ap()
    onescol2 = nc.dram_tensor("onescol2", [128, 4], BF16, kind="ExternalInput").ap()
    identf = nc.dram_tensor("identf", [128, 128], F32, kind="ExternalInput").ap()
    qselt = nc.dram_tensor("qselt", [128, 32 * 32], BF16, kind="ExternalInput").ap()
    hsel = nc.dram_tensor("hsel", [32, 32 * 128], F32, kind="ExternalInput").ap()

    qraw_d = nc.dram_tensor("qraw_d", [NH, HD, TPC], BF16, kind="Internal").ap()
    x1t = nc.dram_tensor("x1t", [D, TPC], F32, kind="ExternalOutput").ap()
    kt = nc.dram_tensor("kt", [NKV, HD, TPC], BF16, kind="ExternalOutput").ap()
    vv = nc.dram_tensor("vv", [TPC, NKV * HD], BF16, kind="ExternalOutput").ap()
    qt = nc.dram_tensor("qt", [NH, HD, TPC], BF16, kind="ExternalOutput").ap()

    em = Em(nc)
    es = em.es
    sbf = lambda name, shape, dt: es.enter_context(nc.sbuf_tensor(name, shape, dt))
    PS = es.enter_context(nc.psum_tensor("PS", [128, 4096], F32))

    lb = LNBufs(em, sbf, PS)
    zt = sbf("zt", [128, 32 * TPC], BF16)

    F = {}
    F["wpt"] = wpt
    F["wby"] = [sbf("wby0", [128, 512], U8), sbf("wby1", [128, 512], U8)]
    F["lo8"] = [sbf("lo80", [128, 512], U8), sbf("lo81", [128, 512], U8)]
    F["hi8"] = [sbf("hi80", [128, 512], U8), sbf("hi81", [128, 512], U8)]
    F["lo"] = [sbf("lo0", [128, 512], BF16), sbf("lo1", [128, 512], BF16)]
    F["hi"] = [sbf("hi0", [128, 512], BF16), sbf("hi1", [128, 512], BF16)]
    F["zpb2"] = sbf("zpb2", [2, D], BF16)
    F["zrow2"] = sbf("zrow2", [2, TPC], BF16)
    F["hs"] = sbf("hs", [128, 32], F32)
    F["x1b"] = [sbf("x1b0", [128, 512], F32), sbf("x1b1", [128, 512], F32)]
    F["x1o"] = [sbf("x1o0", [128, 512], F32), sbf("x1o1", [128, 512], F32)]
    F["ext_free"] = [0, 0]

    srow1b = sbf("srow1b", [1, TPC], BF16)
    zp32s = sbf("zp32s", [32, 128], F32)
    bf32s = sbf("bf32s", [32, 128], F32)
    nzp32 = sbf("nzp32", [32, 128], BF16)
    bs32 = sbf("bs32", [32, 128], BF16)
    rs32 = sbf("rs32", [32, 128], F32)
    sc32s = sbf("sc32s", [32, 128], F32)
    identfs = sbf("identfs", [128, 128], F32)
    ones1 = sbf("ones1", [128, 1], BF16)
    qsels = sbf("qsels", [128, 32 * 32], BF16)
    hsels = sbf("hsels", [32, 32 * 128], F32)
    sqq = sbf("sqq", [128, TPC], BF16)
    stq32 = sbf("stq32", [32, TPC], F32)
    u32 = sbf("u32", [32, TPC], F32)
    inv32 = sbf("inv32", [32, TPC], F32)
    invqb = sbf("invqb", [128, TPC], F32)
    epsr = sbf("epsr", [128, 1], F32)
    qnwc = sbf("qnwc", [128, 1], F32)
    knwc = sbf("knwc", [128, 1], F32)
    hbuf = [sbf("hbuf0", [128, TPC], BF16), sbf("hbuf1", [128, TPC], BF16)]
    qout = [sbf("qout0", [128, TPC], BF16), sbf("qout1", [128, TPC], BF16)]
    wk_b = [sbf("wkb0", [128, 256], BF16), sbf("wkb1", [128, 256], BF16)]
    wq_b = [sbf("wqb0", [128, 512], BF16), sbf("wqb1", [128, 512], BF16)]
    wv_b = [sbf("wvb0", [128, 1024], BF16), sbf("wvb1", [128, 1024], BF16)]

    l_oc2 = em.dma("sync", lb.onescol2[:, :], onescol2[:, :])
    l_sel2 = em.dma("sync", lb.sel2[:, :], sel2[:, :])
    l_zpf = em.dma("sync", zp32s[:, :], zp32[:, :])
    l_bff = em.dma("sync", bf32s[:, :], bf32[:, :])
    l_sc32 = em.dma("sync", sc32s[:, :], sc32[:, :])
    l_idf = em.dma("sync", identfs[:, :], identf[:, :])
    l_qsel = em.dma("sync", qsels[:, :], qselt[:, :])
    l_hsel = em.dma("sync", hsels[:, :], hsel[:, :])
    l_qnw = em.dma("sync", qnwc[:, 0:1], qnw.rearrange("1 p -> p 1"))
    l_knw = em.dma("sync", knwc[:, 0:1], knw.rearrange("1 p -> p 1"))
    em.vec(lambda e: e.memset(ones1[:, :], 1.0))
    lb.eps_ready = em.vec(lambda e: e.memset(lb.epsc[:, :], EPS_LN))
    v_epsr = em.vec(lambda e: e.memset(epsr[:, :], 128.0 * EPS_RMS))
    vnzp = em.vec(lambda e: e.tensor_scalar_mul(nzp32[:, :], zp32s[:, :], -1.0),
                  waits=[l_zpf])
    em.vec(lambda e: e.reciprocal(rs32[:, :], sc32s[:, :]),
           waits=[l_sc32])
    vdiv = em.vec(lambda e: e.tensor_tensor(bs32[:, :], bf32s[:, :], rs32[:, :],
                                            ALU.mult),
                  waits=[l_bff])
    d_zpb0 = em.dma("gpsimd", F["zpb2"][0:1, :], nzp32[:, :], waits=[("vch", vnzp)], stream="zpb0")
    d_zpb1 = em.dma("gpsimd", F["zpb2"][1:2, :], bs32[:, :], waits=[("vch", vdiv)], stream="zpb1")
    vone = em.vec(lambda e: e.memset(srow1b[0:1, 0:TPC], 1.0))
    F["zr1_done"] = em.dma("gpsimd", F["zrow2"][1:2, :], srow1b[0:1, 0:TPC],
                           waits=[("vch", vone), d_zpb0, d_zpb1], stream="zr1")
    t_hs = em.mm(lambda e: e.transpose(bank(PS, 7)[:, 0:32], sc32s[:, :], identfs[0:32, 0:32]),
                 waits=[l_idf, l_sc32], inc=True)
    v_hs = em.vec(lambda e: e.tensor_scalar_mul(F["hs"][:, :], bank(PS, 7)[:, 0:32], 0.5),
                  waits=[("tch", t_hs)])

    if trunc == "pro":
        em.finalize([("vector", "vch", em.cnt["vch"])])
        return nc
    gate_consts = em.qbar("sq")
    z1 = emit_ln_z(em, lb, xt, zt, prior_psum_free=v_hs, gate_dma=gate_consts)
    sz1 = emit_sumz(em, lb, zt, F["zrow2"], ones1, sumz_gate=z1)

    if trunc == "ln1":
        em.finalize([("vector", "vch", em.cnt["vch"])])
        return nc
    F["xin_gate"] = ()
    ffn1_sq = emit_ffn(em, lb, F, zt, z1, sz1, xt, x1t)

    if trunc == "ffn1":
        em.finalize([("vector", "vch", em.cnt["vch"])])
        return nc
    z2 = emit_ln_z(em, lb, x1t, zt, prior_psum_free=em.cnt["vch"], gate_dma=em.qbar("sq"))

    if trunc == "ln2":
        em.finalize([("vector", "vch", em.cnt["vch"])])
        return nc
    pstat = PS[0:32, 4 * 512:4 * 512 + 1024]

    kraw = sbf("kraw", [128, 8 * TPC], BF16)

    def stats_flags(h, th, first, total):
        return (h == first and th == 0, h == first + total - 1 and th == 1)

    kproj_psum_free = 0
    kstats_mark = 0
    ngroups = 1 if (trunc or "").startswith("kg0") else 4
    nostat = (trunc or "").endswith("nostat")
    nosq = (trunc or "").endswith("nosq")
    if (trunc or "").endswith("nokraw"):
        pass
    for g in range(ngroups):
        heads = (2 * g, 2 * g + 1)
        wb_free = [em.cnt["tch"], em.cnt["tch"]]
        for i in range(32):
            i2 = i % 2
            wb = wk_b[i2]
            ld = em.dma("gpsimd", wb[:, :],
                        wkt[128 * i:128 * i + 128, HD * heads[0]:HD * heads[0] + 256],
                        waits=[("tch", wb_free[i2])], stream=f"wk{i2}")
            for hh in range(2):
                for th in range(2):
                    wl = []
                    if hh == 0 and th == 0:
                        wl.append(ld)
                    if i == 0 and hh == 0 and th == 0:
                        wl.append(("vch", z2))
                        wl.append(("tbar", kproj_psum_free))
                    mk = em.mm(lambda e, hh=hh, th=th, i=i, wb=wb:
                               e.matmul(PS[0:128, (2 * hh + th) * 512:(2 * hh + th) * 512 + 512],
                                        wb[:, 128 * hh:128 * hh + 128],
                                        zt[:, TPC * i + 512 * th:TPC * i + 512 * th + 512],
                                        start=(i == 0), stop=(i == 31)),
                               waits=wl, inc=(hh == 1 and th == 1))
            wb_free[i2] = mk
        tdone = em.cnt["tch"]
        for hh, h in enumerate(heads):
            em.vec(lambda e, hh=hh, h=h: e.tensor_copy(
                kraw[:, TPC * h:TPC * h + TPC],
                PS[0:128, (2 * hh) * 512:(2 * hh) * 512 + 1024]),
                waits=[("tch", tdone)])
            sv = em.vec(lambda e, h=h: e.tensor_tensor(
                sqq[:, :], kraw[:, TPC * h:TPC * h + TPC],
                kraw[:, TPC * h:TPC * h + TPC], ALU.mult),
                waits=[("tch", kstats_mark)])
            for th in range(2):
                kstats_mark = em.mm(lambda e, h=h, th=th:
                      e.matmul(pstat[:, 512 * th:512 * th + 512],
                               qsels[:, 32 * h:32 * h + 32],
                               sqq[:, 512 * th:512 * th + 512],
                               start=(h == 0), stop=(h == 7 if ngroups == 4 else h == 1)),
                      waits=[("vch", sv)] if th == 0 else (), inc=(th == 1)) or kstats_mark
        kproj_psum_free = em.tmark(waits=[("vch", em.cnt["vch"])])
    t_kstats = em.cnt["tch"]

    if trunc is not None and trunc.startswith(("kgrp", "kg0")):
        em.finalize([("vector", "vch", em.cnt["vch"])])
        return nc
    vks = em.vec(lambda e: e.tensor_copy(stq32[:, :], pstat[:, :]),
                 waits=[("tch", t_kstats)])
    uk = em.act(lambda e: e.activation(u32[:, :], stq32[:, :], AF.Sqrt,
                                       bias=epsr[0:32, 0:1]),
                waits=[("vch", vks), ("vch", v_epsr)])
    em.vec(lambda e: e.reciprocal(inv32[:, :], u32[:, :]), waits=[("ach", uk)])
    vk_inv = em.vec(lambda e: e.tensor_scalar_mul(inv32[:, :], inv32[:, :], SQRT128))
    kstat_free = vk_inv

    if trunc == "kstat":
        em.finalize([("vector", "vch", em.cnt["vch"])])
        return nc
    kn_store_p = [None, None]
    prev_vb = vk_inv
    for h in range(NKV):
        for th in range(2):
            em.mm(lambda e, h=h, th=th:
                  e.matmul(bank(PS, 6 + th), hsels[:, 128 * h:128 * h + 128],
                           inv32[:, 512 * th:512 * th + 512], start=True, stop=True),
                  waits=[("vch", prev_vb)], inc=(th == 1))
        tb = em.cnt["tch"]
        vb = 0
        for th in range(2):
            vb = em.vec(lambda e, th=th: e.tensor_copy(
                invqb[:, 512 * th:512 * th + 512], bank(PS, 6 + th)),
                waits=[("tch", tb)])
        i2 = h % 2
        prev_vb = vb
        ev = em.vec(lambda e, h=h, i2=i2: e.scalar_tensor_tensor(
            hbuf[i2][:, :], kraw[:, TPC * h:TPC * h + TPC], knwc[:, 0:1],
            invqb[:, :], ALU.mult, ALU.mult),
            waits=[("vch", vb), kn_store_p[i2], l_knw])
        kn_store_p[i2] = em.dma("sync", kt[h, :, :], hbuf[i2][:, :], waits=[("vch", ev)], stream=f"kst{i2}")

    if trunc == "kproj":
        em.finalize([("vector", "vch", em.cnt["vch"])])
        return nc
    v_psum_free = em.tmark(waits=[("vch", em.cnt["vch"])])
    v_store_p = [kn_store_p[0], kn_store_p[1]]
    for ttg in range(2):
        wb_free = [em.cnt["tch"], em.cnt["tch"]]
        for i in range(32):
            i2 = i % 2
            wb = wv_b[i2]
            ld = em.dma("gpsimd", wb[:, :], wvt[128 * i:128 * i + 128, :],
                        waits=[("tch", wb_free[i2])], stream=f"wv{i2}")
            for tt in range(4):
                t_idx = 4 * ttg + tt
                for th in range(2):
                    wl = []
                    if tt == 0 and th == 0:
                        wl.append(ld)
                    if i == 0 and tt == 0 and th == 0:
                        wl.append(("tbar", v_psum_free))
                    mk = em.mm(lambda e, i=i, tt=tt, th=th, t_idx=t_idx, wb=wb:
                               e.matmul(bank(PS, 2 * tt + th),
                                        zt[:, TPC * i + 128 * t_idx:TPC * i + 128 * t_idx + 128],
                                        wb[:, 512 * th:512 * th + 512],
                                        start=(i == 0), stop=(i == 31)),
                               waits=wl, inc=(tt == 3 and th == 1))
            wb_free[i2] = mk
        tv = em.cnt["tch"]
        for tt in range(4):
            t_idx = 4 * ttg + tt
            i2 = tt % 2
            em.vec(lambda e, tt=tt, i2=i2: e.tensor_copy(
                hbuf[i2][:, 0:512], bank(PS, 2 * tt)),
                waits=[("tch", tv), v_store_p[i2]])
            ev2 = em.vec(lambda e, tt=tt, i2=i2: e.tensor_copy(
                hbuf[i2][:, 512:1024], bank(PS, 2 * tt + 1)))
            v_store_p[i2] = em.dma("sync", vv[128 * t_idx:128 * t_idx + 128, :],
                                   hbuf[i2][:, :], waits=[("vch", ev2)], stream=f"vst{i2}")
        v_psum_free = em.tmark(waits=[("vch", em.cnt["vch"])])

    if trunc == "v":
        em.finalize([("vector", "vch", em.cnt["vch"])])
        return nc
    q_psum_free = v_psum_free
    qstats_mark = kstats_mark
    qraw_store = [v_store_p[0], v_store_p[1]]
    for g in range(16):
        heads = (2 * g, 2 * g + 1)
        wb_free = [em.cnt["tch"], em.cnt["tch"]]
        for i in range(32):
            i2 = i % 2
            wb = wq_b[i2]
            ld = em.dma("gpsimd", wb[:, 0:256],
                        wqt[128 * i:128 * i + 128, HD * heads[0]:HD * heads[0] + 256],
                        waits=[("tch", wb_free[i2])], stream=f"wq{i2}")
            for hh in range(2):
                for th in range(2):
                    wl = []
                    if hh == 0 and th == 0:
                        wl.append(ld)
                    if i == 0 and hh == 0 and th == 0:
                        wl.append(("tbar", q_psum_free))
                    mk = em.mm(lambda e, hh=hh, th=th, i=i, wb=wb:
                               e.matmul(PS[0:128, (2 * hh + th) * 512:(2 * hh + th) * 512 + 512],
                                        wb[:, 128 * hh:128 * hh + 128],
                                        zt[:, TPC * i + 512 * th:TPC * i + 512 * th + 512],
                                        start=(i == 0), stop=(i == 31)),
                               waits=wl, inc=(hh == 1 and th == 1))
            wb_free[i2] = mk
        tdone = em.cnt["tch"]
        for hh, h in enumerate(heads):
            qev = em.vec(lambda e, hh=hh: e.tensor_copy(
                hbuf[hh][:, :],
                PS[0:128, (2 * hh) * 512:(2 * hh) * 512 + 1024]),
                waits=[("tch", tdone), qraw_store[hh]])
            sv = em.vec(lambda e, hh=hh: e.tensor_tensor(
                sqq[:, :], hbuf[hh][:, :], hbuf[hh][:, :], ALU.mult),
                waits=[("tch", qstats_mark)])
            for th in range(2):
                qstats_mark = em.mm(lambda e, h=h, th=th:
                      e.matmul(pstat[:, 512 * th:512 * th + 512],
                               qsels[:, 32 * h:32 * h + 32],
                               sqq[:, 512 * th:512 * th + 512],
                               start=(h == 0), stop=(h == 31)),
                      waits=[("vch", sv), ("vch", kstat_free)] if th == 0 else (),
                      inc=(th == 1)) or qstats_mark
            qraw_store[hh] = em.dma("sync", qraw_d[h, :, :], hbuf[hh][:, :],
                                    waits=[("vch", qev)], stream=f"qrst{hh}")
        q_psum_free = em.tmark(waits=[("vch", em.cnt["vch"])])
    t_qstats = em.cnt["tch"]

    vqs = em.vec(lambda e: e.tensor_copy(stq32[:, :], pstat[:, :]),
                 waits=[("tch", t_qstats)])
    uq = em.act(lambda e: e.activation(u32[:, :], stq32[:, :], AF.Sqrt,
                                       bias=epsr[0:32, 0:1]),
                waits=[("vch", vqs)])
    vq_inv = em.vec(lambda e: e.reciprocal(inv32[:, :], u32[:, :]), waits=[("ach", uq)])

    qn_store_p = [None, None]
    qn_read = [em.cnt["vch"], em.cnt["vch"]]
    prev_vb = vq_inv
    for h in range(NH):
        for th in range(2):
            em.mm(lambda e, h=h, th=th:
                  e.matmul(bank(PS, 6 + th), hsels[:, 128 * (h % 32):128 * (h % 32) + 128],
                           inv32[:, 512 * th:512 * th + 512], start=True, stop=True),
                  waits=[("vch", prev_vb)], inc=(th == 1))
        tb = em.cnt["tch"]
        vb = 0
        for th in range(2):
            vb = em.vec(lambda e, th=th: e.tensor_copy(
                invqb[:, 512 * th:512 * th + 512], bank(PS, 6 + th)),
                waits=[("tch", tb)])
        i2 = h % 2
        lr = em.dma("gpsimd", hbuf[i2][:, :], qraw_d[h, :, :],
                    waits=[("vch", qn_read[i2])] +
                          ([qraw_store[0], qraw_store[1]] if h < 2 else []),
                    stream=f"qlr{i2}")
        prev_vb = vb
        ev = em.vec(lambda e, h=h, i2=i2: e.scalar_tensor_tensor(
            qout[i2][:, :], hbuf[i2][:, :], qnwc[:, 0:1],
            invqb[:, :], ALU.mult, ALU.mult),
            waits=[("vch", vb), lr, l_qnw, qn_store_p[i2]])
        qn_read[i2] = ev
        qn_store_p[i2] = em.dma("sync", qt[h, :, :], qout[i2][:, :], waits=[("vch", ev)], stream=f"qst{i2}")

    em.finalize([("vector", "vch", em.cnt["vch"])])
    return nc




def build_l2(core_mod=None):
    nc = bass.Bass("TRN2", target_bir_lowering=False)
    ktall = nc.dram_tensor("ktall", [NKV, HD, S], BF16, kind="ExternalInput").ap()
    vpe = nc.dram_tensor("vpe", [NKV, 128, 64, 128], BF16, kind="ExternalInput").ap()
    qt = nc.dram_tensor("qt", [NH, HD, TPC], BF16, kind="ExternalInput").ap()
    x1t = nc.dram_tensor("x1t", [D, TPC], F32, kind="ExternalInput").ap()
    wot = nc.dram_tensor("wot", [D, D], F32, kind="ExternalInput").ap()
    wpt = nc.dram_tensor("wpt", [D // 2, D], U8, kind="ExternalInput").ap()
    zp32 = nc.dram_tensor("zp32", [32, 128], F32, kind="ExternalInput").ap()
    bf32 = nc.dram_tensor("bf32", [32, 128], F32, kind="ExternalInput").ap()
    sc32 = nc.dram_tensor("sc32", [32, 128], F32, kind="ExternalInput").ap()
    albl = nc.dram_tensor("albl", [NH, 6, TPC], BF16, kind="ExternalInput").ap()
    albr = nc.dram_tensor("albr", [NH, 6, S], BF16, kind="ExternalInput").ap()
    mtin = nc.dram_tensor("mtin", [128, 8 * 128], BF16, kind="ExternalInput").ap()
    identb = nc.dram_tensor("identb", [128, 128], BF16, kind="ExternalInput").ap()
    identf = nc.dram_tensor("identf", [128, 128], F32, kind="ExternalInput").ap()
    sel2 = nc.dram_tensor("sel2", [2, 256], F32, kind="ExternalInput").ap()
    onescol2 = nc.dram_tensor("onescol2", [128, 4], BF16, kind="ExternalInput").ap()

    yt = nc.dram_tensor("yt", [D, TPC], F32, kind="ExternalOutput").ap()
    o_park = nc.dram_tensor("o_park", [TPC, D], BF16, kind="Internal").ap()
    x2t = nc.dram_tensor("x2t", [D, TPC], F32, kind="Internal").ap()

    em = Em(nc)
    es = em.es
    sbf = lambda name, shape, dt: es.enter_context(nc.sbuf_tensor(name, shape, dt))
    PS = es.enter_context(nc.psum_tensor("PS", [128, 4096], F32))

    F = {}
    F["wpt"] = wpt
    F["zpb2"] = sbf("zpb2", [2, D], BF16)
    F["zrow2"] = sbf("zrow2", [2, TPC], BF16)
    F["hs"] = sbf("hs", [128, 32], F32)
    F["ext_free"] = [0, 0]
    oc2e = sbf("oc2e", [128, 4], BF16)
    sel2e = sbf("sel2e", [2, 256], F32)
    epse = sbf("epse", [128, 1], F32)
    srow1b = sbf("srow1b", [1, TPC], BF16)
    zp32s = sbf("zp32s", [32, 128], F32)
    bf32s = sbf("bf32s", [32, 128], F32)
    nzp32 = sbf("nzp32", [32, 128], BF16)
    bs32 = sbf("bs32", [32, 128], BF16)
    rs32 = sbf("rs32", [32, 128], F32)
    sc32s = sbf("sc32s", [32, 128], F32)
    identfs = sbf("identfs", [128, 128], F32)
    identbs = sbf("identbs", [128, 128], BF16)
    ones1 = sbf("ones1", [128, 1], BF16)
    neginf = sbf("neginf", [128, 1], F32)
    mbuf = [sbf("mbuf0", [128, 1], F32), sbf("mbuf1", [128, 1], F32),
            sbf("mbuf2", [128, 1], F32), sbf("mbuf3", [128, 1], F32)]
    negm_b = [sbf("negm_b0", [128, 1], F32), sbf("negm_b1", [128, 1], F32)]
    lbuf = [sbf("lbuf0", [128, 1], F32), sbf("lbuf1", [128, 1], F32)]
    linv_b = [sbf("linv_b0", [128, 1], F32), sbf("linv_b1", [128, 1], F32)]
    osb = [sbf("osb0", [128, 128], BF16), sbf("osb1", [128, 128], BF16)]
    mt = sbf("mt", [128, 8 * 128], BF16)

    l_oc2 = em.dma("sync", oc2e[:, :], onescol2[:, :])
    l_sel2 = em.dma("sync", sel2e[:, :], sel2[:, :])
    l_zpf = em.dma("sync", zp32s[:, :], zp32[:, :])
    l_bff = em.dma("sync", bf32s[:, :], bf32[:, :])
    l_sc32 = em.dma("sync", sc32s[:, :], sc32[:, :])
    l_idf = em.dma("sync", identfs[:, :], identf[:, :])
    l_idb = em.dma("sync", identbs[:, :], identb[:, :])
    l_mt = em.dma("sync", mt[:, :], mtin[:, :])
    em.vec(lambda e: e.memset(ones1[:, :], 1.0))
    eps_ready_v = em.vec(lambda e: e.memset(epse[:, :], EPS_LN))
    em.vec(lambda e: e.memset(neginf[:, :], -3.0e38))
    vnzp = em.vec(lambda e: e.tensor_scalar_mul(nzp32[:, :], zp32s[:, :], -1.0),
                  waits=[l_zpf])
    em.vec(lambda e: e.reciprocal(rs32[:, :], sc32s[:, :]), waits=[l_sc32])
    vdiv = em.vec(lambda e: e.tensor_tensor(bs32[:, :], bf32s[:, :], rs32[:, :],
                                            ALU.mult), waits=[l_bff])
    d_zpb0 = em.dma("gpsimd", F["zpb2"][0:1, :], nzp32[:, :], waits=[("vch", vnzp)], stream="zpb0")
    d_zpb1 = em.dma("gpsimd", F["zpb2"][1:2, :], bs32[:, :], waits=[("vch", vdiv)], stream="zpb1")
    vone = em.vec(lambda e: e.memset(srow1b[0:1, 0:TPC], 1.0))
    F["zr1_done"] = em.dma("gpsimd", F["zrow2"][1:2, :], srow1b[0:1, 0:TPC],
                           waits=[("vch", vone), d_zpb0, d_zpb1], stream="zr1")
    t_hs = em.mm(lambda e: e.transpose(bank(PS, 7)[:, 0:32], sc32s[:, :], identfs[0:32, 0:32]),
                 waits=[l_idf, l_sc32], inc=True)
    v_hs = em.vec(lambda e: e.tensor_scalar_mul(F["hs"][:, :], bank(PS, 7)[:, 0:32], 0.5),
                  waits=[("tch", t_hs)])
    pro_gq = em.qbar("gq")

    es_a = ExitStack()
    sba = lambda name, shape, dt: es_a.enter_context(nc.sbuf_tensor(name, shape, dt))
    kth = sba("kth", [128, S], BF16)
    vh = sba("vh", [128, S], BF16)
    qth = sba("qth", [128, TPC], BF16)
    arh = sba("arh", [6, S], BF16)
    alh = sba("alh", [6, TPC], BF16)
    srow = [sba("srow0", [128, S], F32), sba("srow1", [128, S], F32)]
    prow = [sba("prow0", [128, S], BF16), sba("prow1", [128, S], BF16)]
    ptrow = sba("ptrow", [128, 2 * 2048], BF16)

    kv_last_use = 0
    q_last_use = 0
    exp_p = [0, 0]
    recip_p = [0, 0]
    avmark_p = [0, 0]
    oev_p = [0, 0]
    ost_p = [None, None]
    wave_slot_user = [0, 0]
    wave_i = 0
    psum_free = [v_hs] * 4
    combo_i = 0

    for h in range(NKV):
        ld_k = em.dma("gpsimd", kth[:, :], ktall[h, :, :],
                      waits=[("tch", kv_last_use)], stream="kth")
        ld_v = em.dma("gpsimd", vh[:, :], vpe[h, :, :, :],
                      waits=[("tch", kv_last_use)], stream="vh")
        for qh in range(4 * h, 4 * h + 4):
            ld_q = em.dma("gpsimd", qth[:, :], qt[qh, :, :],
                          waits=[("tch", q_last_use)], stream="qth")
            ld_ar = em.dma("gpsimd", arh[:, :], albr[qh, :, :],
                           waits=[("tch", q_last_use)], stream="arh")
            ld_al = em.dma("gpsimd", alh[:, :], albl[qh, :, :],
                           waits=[("tch", q_last_use)], stream="alh")
            def emit_pass1(T):
                nonlocal combo_i
                nch = 2 * (T + 1)
                par = combo_i % 2
                sr = srow[par]
                for k in range(nch):
                    b = k % 4
                    wl = [("vch", psum_free[b])]
                    if k == 0:
                        wl += [ld_k, ld_v, ld_q, ld_ar, ld_al, l_mt, l_idb,
                               ("ach", exp_p[par])]
                    em.mm(lambda e, T=T, k=k, b=b:
                          e.matmul(bank(PS, b), alh[0:6, 128 * T:128 * T + 128],
                                   arh[0:6, 512 * k:512 * k + 512],
                                   start=True, stop=False),
                          waits=wl)
                    diag = (k >= 2 * T)
                    if diag:
                        for u in range(4):
                            blk = 4 * (k - 2 * T) + u
                            em.mm(lambda e, b=b, u=u, blk=blk:
                                  e.matmul(bank(PS, b)[:, 128 * u:128 * u + 128],
                                           identbs[:, :], mt[:, 128 * blk:128 * blk + 128],
                                           start=False, stop=False))
                    mk = em.mm(lambda e, T=T, k=k, b=b:
                               e.matmul(bank(PS, b), qth[:, 128 * T:128 * T + 128],
                                        kth[:, 512 * k:512 * k + 512],
                                        start=False, stop=True),
                               inc=True)
                    ev = em.vec(lambda e, k=k, b=b, sr=sr, par=par, first=(k == 0):
                                e.tensor_scalar(sr[:, 512 * k:512 * k + 512],
                                                bank(PS, b), 0.0,
                                                neginf[:, 0:1] if first else mbuf[2 * par + (k - 1) % 2][:, 0:1],
                                                ALU.add, ALU.max,
                                                accum_out=mbuf[2 * par + k % 2][:, 0:1]),
                                waits=[("tch", mk)] + ([("ach", exp_p[par])] if k == 0 else []))
                    psum_free[b] = ev
                st = dict(T=T, par=par, combo=combo_i, qh=qh)
                combo_i += 1
                return st

            def emit_pass2(st):
                nonlocal wave_i
                T = st["T"]; par = st["par"]; qh_ = st["qh"]
                sr = srow[par]; pr = prow[par]
                nch = 2 * (T + 1)
                L = 1024 * (T + 1)
                Lh = L
                vneg = em.vec(lambda e, k=nch - 1, par=par: e.tensor_scalar_mul(
                    negm_b[par][:, :], mbuf[2 * par + k % 2][:, :], -1.0))
                expv1 = em.act(lambda e, L=L, par=par, sr=sr, pr=pr: e.activation(
                    pr[:, 0:L], sr[:, 0:L], AF.Exp, bias=negm_b[par][:, 0:1],
                    accum_out=lbuf[par][:, 0:1]),
                    waits=[("vch", vneg), ("vch", recip_p[par]), ("tch", avmark_p[par])])
                expv = expv1
                exp_p[par] = expv
                vrec = em.vec(lambda e, par=par: e.reciprocal(linv_b[par][:, :], lbuf[par][:, :]),
                              waits=[("ach", expv)])
                recip_p[par] = vrec
                nblk = 8 * (T + 1)
                av = None
                g0 = 0
                while g0 < nblk:
                    gn = min(16, nblk - g0)
                    slot = wave_i % 2
                    egate = expv1 if (128 * (g0 + gn) <= Lh) else expv
                    trt = em.dmat(
                        "sync",
                        ptrow.ap()[:, 2048 * slot:2048 * slot + 128 * gn].rearrange(
                            "p (b f) -> p b f", b=gn),
                        pr.ap()[:, 128 * g0:128 * g0 + 128 * gn].rearrange(
                            "p (b f) -> p b f", b=gn),
                        waits=[("ach", egate), ("tch", wave_slot_user[slot])]
                        + (list(pro_gq) if st["combo"] == 0 and g0 == 0 else []),
                        stream=f"tr{slot}")
                    for j in range(gn):
                        bb = g0 + j
                        wl = [trt] if j == 0 else []
                        if bb == 0:
                            wl.append(("vch", oev_p[par]))
                        av = em.mm(lambda e, bb=bb, nblk=nblk, slot=slot, j=j, par=par:
                                   e.matmul(PS[0:128, (4 + par) * 512:(4 + par) * 512 + 128],
                                            ptrow[:, 2048 * slot + 128 * j:2048 * slot + 128 * j + 128],
                                            vh[:, 128 * bb:128 * bb + 128],
                                            start=(bb == 0), stop=(bb == nblk - 1)),
                                   waits=wl, inc=(j == gn - 1))
                    wave_slot_user[slot] = av
                    wave_i += 1
                    g0 += gn
                avmark_p[par] = av
                oev = em.vec(lambda e, par=par: e.tensor_scalar(
                    osb[par][:, :], PS[0:128, (4 + par) * 512:(4 + par) * 512 + 128],
                    linv_b[par][:, 0:1], None, ALU.mult),
                    waits=[("tch", av), ("vch", vrec), ost_p[par]])
                oev_p[par] = oev
                ost_p[par] = em.dma("gpsimd",
                                    o_park[128 * T:128 * T + 128,
                                           128 * qh_:128 * qh_ + 128],
                                    osb[par][:, :], waits=[("vch", oev)],
                                    stream=f"ost{par}")
                return av

            pend = None
            for T in range(8):
                st = emit_pass1(T)
                if pend is not None:
                    emit_pass2(pend)
                pend = st
            av_last = emit_pass2(pend)
            kv_last_use = av_last
            q_last_use = av_last

    attn_done = [("ach", max(exp_p)), ("tch", max(avmark_p)),
                 ("vch", max(oev_p)), ost_p[0], ost_p[1]]
    es_a.close()

    es_w = ExitStack()
    sbw = lambda name, shape, dt: es_w.enter_context(nc.sbuf_tensor(name, shape, dt))
    lb = LNBufs(em, sbw, PS)
    lb.onescol2 = oc2e
    lb.sel2 = sel2e
    lb.epsc = epse
    lb.eps_ready = eps_ready_v
    F["wby"] = [sbw("wby0", [128, 512], U8), sbw("wby1", [128, 512], U8)]
    F["lo8"] = [sbw("lo80", [128, 512], U8), sbw("lo81", [128, 512], U8)]
    F["hi8"] = [sbw("hi80", [128, 512], U8), sbw("hi81", [128, 512], U8)]
    F["lo"] = [sbw("lo0", [128, 512], BF16), sbw("lo1", [128, 512], BF16)]
    F["hi"] = [sbw("hi0", [128, 512], BF16), sbw("hi1", [128, 512], BF16)]
    F["x1b"] = [sbw("x1b0", [128, 512], F32), sbw("x1b1", [128, 512], F32)]
    F["x1o"] = [sbw("x1o0", [128, 512], F32), sbw("x1o1", [128, 512], F32)]
    zt = sbw("zt", [128, 32 * TPC], BF16)
    wob = [sbw("wob0", [128, 512], BF16), sbw("wob1", [128, 512], BF16)]
    otb = [sbw("otb00", [128, 512], BF16), sbw("otb01", [128, 512], BF16),
           sbw("otb10", [128, 512], BF16), sbw("otb11", [128, 512], BF16)]
    x2b = [sbw("x2b0", [128, 512], F32), sbw("x2b1", [128, 512], F32)]
    x2o = [sbw("x2o0", [128, 512], F32), sbw("x2o1", [128, 512], F32)]

    wo_psum_free = [max(oev_p)] * 8
    wob_free = [0, 0]
    otb_free = [0, 0, 0, 0]
    x2b_free = [em.cnt["vch"], em.cnt["vch"]]
    x2o_free = [None, None]
    x2_last = None
    for og in range(8):
        for dc in range(32):
            i2 = dc % 2
            wb = wob[i2]
            lw = em.dma("gpsimd", wb[:, :],
                        wot[128 * dc:128 * dc + 128, 512 * og:512 * og + 512],
                        waits=[("tch", wob_free[i2])] +
                              (attn_done if (og == 0 and dc == 0) else []),
                        stream=f"wob{i2}")
            ots = []
            for th in range(2):
                oi = 2 * (dc % 2) + th
                lo_ = em.dmat("sync", otb[oi][:, :],
                              o_park[512 * th:512 * th + 512, 128 * dc:128 * dc + 128],
                              waits=[("tch", otb_free[oi])] +
                                    (attn_done if (og == 0 and dc < 2) else []),
                              stream=f"ot{oi}")
                ots.append(lo_)
            for th in range(2):
                for oc in range(4):
                    psi = 4 * th + oc
                    oi = 2 * (dc % 2) + th
                    wl = []
                    if oc == 0:
                        wl = [lw, ots[th]]
                    if dc == 0 and th == 0 and oc == 0:
                        wl.append(("vch", max(wo_psum_free)))
                    mk = em.mm(lambda e, psi=psi, oc=oc, oi=oi, wb=wb, dc=dc:
                               e.matmul(bank(PS, psi), wb[:, 128 * oc:128 * oc + 128],
                                        otb[oi][:, :], start=(dc == 0), stop=(dc == 31)),
                               waits=wl, inc=(th == 1 and oc == 3))
            wob_free[i2] = mk
            otb_free[2 * (dc % 2)] = mk
            otb_free[2 * (dc % 2) + 1] = mk
        tm = em.cnt["tch"]
        for th in range(2):
            for oc in range(4):
                psi = 4 * th + oc
                oco = 4 * og + oc
                i2 = psi % 2
                lx = em.dma("gpsimd", x2b[i2][:, :],
                            x1t[128 * oco:128 * oco + 128, 512 * th:512 * th + 512],
                            waits=[("vch", x2b_free[i2])], stream=f"x2l{i2}")
                ev = em.vec(lambda e, psi=psi, i2=i2: e.tensor_tensor(
                    x2o[i2][:, :], bank(PS, psi), x2b[i2][:, :], ALU.add),
                    waits=[("tch", tm), lx, x2o_free[i2]])
                wo_psum_free[psi] = ev
                x2b_free[i2] = ev
                x2_last = em.dma("sync", x2t[128 * oco:128 * oco + 128,
                                             512 * th:512 * th + 512],
                                 x2o[i2][:, :], waits=[("vch", ev)], stream=f"x2s{i2}")
                x2o_free[i2] = x2_last

    z3 = emit_ln_z(em, lb, x2t, zt, prior_psum_free=em.cnt["vch"],
                   gate_dma=em.qbar("sq"))
    sz3 = emit_sumz(em, lb, zt, F["zrow2"], ones1, sumz_gate=z3)
    F["xin_gate"] = em.qbar("sq")
    emit_ffn(em, lb, F, zt, z3, sz3, x2t, yt)

    es_w.close()
    em.finalize([("vector", "vch", em.cnt["vch"])])
    return nc


def _consts_l1(inputs):
    Wq = np.asarray(inputs["L0_attn_Wq"])
    Wk = np.asarray(inputs["L0_attn_Wk"])
    Wv = np.asarray(inputs["L0_attn_Wv"])
    wp = np.asarray(inputs["L0_ffn_weight_packed"])
    scale = np.asarray(inputs["L0_ffn_scale"])[:, 0]
    zp = np.asarray(inputs["L0_ffn_zero_point"])[:, 0]
    biasf = np.asarray(inputs["L0_ffn_bias"])
    qn = np.asarray(inputs["L0_attn_q_norm_w"])
    kn = np.asarray(inputs["L0_attn_k_norm_w"])

    wqt = np.ascontiguousarray(Wq.T[PI])
    wkt = np.ascontiguousarray(Wk.T[PI])
    wvt = np.ascontiguousarray(Wv.T[PI])
    bytes_ = np.ascontiguousarray(wp).view(np.uint8).reshape(-1, 4)[:, 0]
    wpt = np.ascontiguousarray(bytes_.reshape(D, D // 2).T[:, PI])
    scp = np.ascontiguousarray(scale[PI])
    zpp = np.ascontiguousarray(zp[PI])
    bfp = np.ascontiguousarray(biasf[PI])

    sel2 = np.zeros((2, 256), np.float32)
    sel2[0, 0:128] = 1.0
    sel2[1, 128:256] = 1.0
    onescol2 = np.zeros((128, 4), BF)
    onescol2[:, 0] = 1
    onescol2[:, 3] = 1
    identf = np.eye(128, dtype=np.float32)
    qsel = np.zeros((128, 32 * 32), BF)
    for h in range(32):
        qsel[:, 32 * h + h] = 1
    hsel = np.zeros((32, 32 * 128), np.float32)
    for h in range(32):
        hsel[h, 128 * h:128 * h + 128] = 1

    return dict(wqt=wqt, wkt=wkt, wvt=wvt, wpt=wpt,
                zp32=np.ascontiguousarray(zpp.reshape(32, 128)),
                bf32=np.ascontiguousarray(bfp.reshape(32, 128)),
                sc32=np.ascontiguousarray(scp.reshape(32, 128)),
                sel2=sel2, onescol2=onescol2, identf=identf,
                qnw=np.ascontiguousarray(qn[None, :]),
                knw=np.ascontiguousarray(kn[None, :]),
                qselt=qsel, hsel=hsel)


def stage_l1(inputs):
    x = np.asarray(inputs["L0_attn_x"])[0]
    shared = _consts_l1(inputs)
    in_maps = []
    for c in range(NC):
        m = dict(shared)
        m["xt"] = np.ascontiguousarray(x[c::NC].T[PI])
        in_maps.append(m)
    return in_maps


_CACHE = {}


def run_l1(inputs):
    in_maps = stage_l1(inputs)
    if "l1" not in _CACHE:
        _CACHE["l1"] = build_l1()
    res = run_bass_kernel_spmd(_CACHE["l1"], in_maps, core_ids=list(range(NC)))
    return res.results


def stage_l2(inputs, r1):
    Wo = np.asarray(inputs["L0_attn_Wo"])
    wot = np.ascontiguousarray(Wo.T[:, PI])
    c1 = _consts_l1(inputs)

    ktall = np.empty((NKV, HD, S), BF)
    vall = np.empty((NKV, S, HD), BF)
    for c in range(NC):
        ktall[:, :, c::NC] = np.asarray(r1[c]["kt"]).reshape(NKV, HD, TPC)
        vall[:, c::NC, :] = np.asarray(r1[c]["vv"]).reshape(TPC, NKV, HD).transpose(1, 0, 2)
    vpe = np.ascontiguousarray(vall.reshape(NKV, 64, 128, HD).transpose(0, 2, 1, 3))

    slopes = _alibi_slopes(NH)
    g = np.arange(S, dtype=np.float64)
    albr = np.zeros((NH, 6, S), BF)
    for qh in range(NH):
        s0, s1, s2 = _split3(slopes[qh] * g)
        albr[qh, 0] = s0
        albr[qh, 1] = s1
        albr[qh, 2] = s2
        albr[qh, 3:6] = -1.0

    identb = np.eye(128, dtype=BF)

    shared = dict(wot=wot, wpt=c1["wpt"], zp32=c1["zp32"], bf32=c1["bf32"],
                  sc32=c1["sc32"], albr=albr, identb=identb,
                  identf=c1["identf"], sel2=c1["sel2"], onescol2=c1["onescol2"])
    in_maps = []
    jj = np.arange(TPC, dtype=np.float64)
    ii = np.arange(128)
    for c in range(NC):
        m = dict(shared)
        m["ktall"] = ktall
        m["vpe"] = vpe
        m["qt"] = np.asarray(r1[c]["qt"]).reshape(NH, HD, TPC)
        m["x1t"] = np.asarray(r1[c]["x1t"])
        p = 8.0 * jj + c
        albl = np.zeros((NH, 6, TPC), BF)
        for qh in range(NH):
            s0, s1, s2 = _split3(slopes[qh] * p)
            albl[qh, 0:3] = 1.0
            albl[qh, 3] = s0
            albl[qh, 4] = s1
            albl[qh, 5] = s2
        m["albl"] = albl
        mt = np.zeros((128, 8 * 128), np.float32)
        for bb in range(8):
            allowed = (128 * bb + ii[None, :]) <= (8 * np.arange(128)[:, None] + c)
            mt[:, 128 * bb:128 * bb + 128] = np.where(allowed, 0.0, NEG_BIG)
        m["mtin"] = mt.astype(BF)
        in_maps.append(m)
    return in_maps


def kernel(**inputs):
    in1 = stage_l1(inputs)
    if "l1" not in _CACHE:
        _CACHE["l1"] = build_l1()
    r1 = run_bass_kernel_spmd(_CACHE["l1"], in1, core_ids=list(range(NC))).results
    in2 = stage_l2(inputs, r1)
    if "l2" not in _CACHE:
        _CACHE["l2"] = build_l2()
    r2 = run_bass_kernel_spmd(_CACHE["l2"], in2, core_ids=list(range(NC))).results
    out = np.empty((S, D), np.float32)
    for c in range(NC):
        xc = np.empty((TPC, D), np.float32)
        xc[:, PI] = np.asarray(r2[c]["yt"]).T
        out[c::NC] = xc
    return out[None, :, :]


# version 08 (final)
# speedup vs baseline: 1.8822x; 1.0013x over previous
import math
import numpy as np
import ml_dtypes
from contextlib import ExitStack

import concourse.bass as bass
import concourse.mybir as mybir
from concourse.bass_utils import run_bass_kernel_spmd

F32 = mybir.dt.float32
BF16 = mybir.dt.bfloat16
U8 = mybir.dt.uint8
AF = mybir.ActivationFunctionType
ALU = mybir.AluOpType
BF = ml_dtypes.bfloat16

NC = 8
S = 8192
D = 4096
TPC = S // NC
NH = 32
NKV = 8
HD = 128
EPS_LN = 1e-5
EPS_RMS = 1e-6
NEG_BIG = -1.0e30
SQRT128 = float(np.sqrt(128.0))

PI = np.arange(D).reshape(D // 256, 128, 2).transpose(0, 2, 1).reshape(-1)

ENGINES = ("sync", "tensor", "vector", "scalar", "gpsimd")


def _alibi_slopes(n_heads):
    start = 2 ** (-(2 ** -(math.log2(n_heads) - 3)))
    return np.array([start * start ** i for i in range(n_heads)], dtype=np.float64)


def _split3(v):
    s0 = v.astype(BF).astype(np.float64)
    s1 = (v - s0).astype(BF).astype(np.float64)
    s2 = (v - s0 - s1).astype(BF)
    return s0.astype(BF), s1.astype(BF), s2


class Em:

    def __init__(self, nc):
        self.nc = nc
        self.es = ExitStack()
        self.ops = {e: [] for e in ENGINES}
        self.sems = {}
        self.cnt = {}
        for name in ("vch", "ach", "pch", "tch", "tbar"):
            self.sems[name] = self.es.enter_context(nc.semaphore(name))
            self.cnt[name] = 0
        self.qstreams = {"sq": set(), "gq": set(), "aq": set()}
        self.auto_i = 0

    def _app(self, eng, fn):
        self.ops[eng].append(fn)

    def _waits(self, waits):
        out = []
        for w in waits:
            if w is None:
                continue
            (n, v) = w
            if v > 0:
                out.append((self.sems[n], v))
        return tuple(out)

    def wait(self, eng, sem_name, val):
        s = self.sems[sem_name]
        self._app(eng, lambda e, s=s, v=val: e.wait_ge(s, v))

    def _chain(self, eng, ch, fn, waits=()):
        self.cnt[ch] += 1
        tgt = self.cnt[ch]
        sem = self.sems[ch]
        w = self._waits(waits)

        def run(e, fn=fn, sem=sem, tgt=tgt, w=w):
            for (ws, wv) in w:
                e.wait_ge(ws, wv)
            if tgt > 1:
                e.wait_ge(sem, tgt - 1)
            fn(e).then_inc(sem, 1)

        self._app(eng, run)
        return tgt

    def vec(self, fn, waits=()):
        return self._chain("vector", "vch", fn, waits)

    def act(self, fn, waits=()):
        return self._chain("scalar", "ach", fn, waits)

    def pool(self, fn, waits=()):
        return self._chain("gpsimd", "pch", fn, waits)

    def mm(self, fn, waits=(), inc=False):
        w = self._waits(waits)
        if inc:
            self.cnt["tch"] += 1
            tgt = self.cnt["tch"]
            sem = self.sems["tch"]

            def run(e, fn=fn, w=w, sem=sem):
                for (ws, wv) in w:
                    e.wait_ge(ws, wv)
                fn(e).then_inc(sem, 1)

            self._app("tensor", run)
            return tgt
        else:
            def run(e, fn=fn, w=w):
                for (ws, wv) in w:
                    e.wait_ge(ws, wv)
                fn(e)

            self._app("tensor", run)
            return None

    def tmark(self, waits=()):
        w = self._waits(waits)
        self.cnt["tbar"] += 1
        tgt = self.cnt["tbar"]
        sem = self.sems["tbar"]

        def run(e, w=w, sem=sem):
            for (ws, wv) in w:
                e.wait_ge(ws, wv)
            e.nop().then_inc(sem, 1)

        self._app("tensor", run)
        return tgt

    def _dma_emit(self, queue, q, stream, fn, waits):
        if stream is None:
            stream = f"auto{self.auto_i}"
            self.auto_i += 1
        n = f"{q}_{stream}"
        if n not in self.sems:
            self.sems[n] = self.es.enter_context(self.nc.semaphore(n))
            self.cnt[n] = 0
            self.qstreams[q].add(n)
        self.cnt[n] += 16
        tgt = self.cnt[n]
        sem = self.sems[n]
        w = self._waits(waits)

        def run(e, fn=fn, sem=sem, w=w):
            for (ws, wv) in w:
                e.wait_ge(ws, wv)
            fn(e).then_inc(sem, 16)

        self._app(queue, run)
        return (n, tgt)

    def dma(self, queue, out, in_, waits=(), stream=None, **kw):
        q = {"sync": "sq", "gpsimd": "gq", "scalar": "aq"}[queue]
        return self._dma_emit(queue, q, stream,
                              lambda e, out=out, in_=in_, kw=kw:
                              e.dma_start(out=out, in_=in_, **kw), waits)

    def dmat(self, queue, out, in_, waits=(), stream=None):
        q = {"sync": "sq", "scalar": "aq"}[queue]
        return self._dma_emit(queue, q, stream,
                              lambda e, out=out, in_=in_:
                              e.dma_start_transpose(out, in_), waits)

    def qbar(self, q):
        return tuple((n, self.cnt[n]) for n in sorted(self.qstreams[q])
                     if self.cnt[n] > 0)

    def finalize(self, final_waits=()):
        for (eng, sem_name, val) in final_waits:
            self.wait(eng, sem_name, val)
        for (eng, q) in (("sync", "sq"), ("gpsimd", "gq"), ("scalar", "aq")):
            for (n, v) in self.qbar(q):
                self.wait(eng, n, v)
        block = self.es.enter_context(self.nc.Block())
        for eng in ENGINES:
            ops = self.ops[eng]
            if not ops:
                continue
            getattr(block, eng)(lambda e, ops=ops: [f(e) for f in ops])
        self.es.close()



class LNBufs:

    def __init__(self, em, sbf, PS):
        self.xb = [sbf("ln_xb0", [128, TPC], BF16), sbf("ln_xb1", [128, TPC], BF16)]
        self.xf = [sbf("ln_xf0", [128, TPC], F32), sbf("ln_xf1", [128, TPC], F32)]
        self.sq = [sbf("ln_sq0", [128, TPC], BF16), sbf("ln_sq1", [128, TPC], BF16)]
        self.sxs = sbf("ln_sxs", [2, 512], F32)
        self.sqs = sbf("ln_sqs", [2, 512], F32)
        self.m2 = sbf("ln_m2", [2, 512], F32)
        self.var = sbf("ln_var", [2, 512], F32)
        self.u2 = sbf("ln_u2", [2, 512], F32)
        self.inv = sbf("ln_inv", [2, 512], F32)
        self.minv = sbf("ln_minv", [2, 512], F32)
        self.tmp2 = sbf("ln_tmp2", [2, 512], F32)
        self.invb = sbf("ln_invb", [128, TPC], F32)
        self.minvb = sbf("ln_minvb", [128, TPC], F32)
        self.t1 = sbf("ln_t1", [128, TPC], F32)
        self.epsc = sbf("ln_epsc", [128, 1], F32)
        self.onescol2 = sbf("ln_oc2", [128, 4], BF16)
        self.sel2 = sbf("ln_sel2", [2, 256], F32)
        self.PS = PS


def bank(PS, i, rows=128, cols=512):
    return PS[0:rows, 512 * i:512 * i + cols]


def emit_ln_z(em, lb, xsrc_dram, zt, prior_psum_free=0, gate_dma=()):
    PS = lb.PS
    stx = bank(PS, 0, rows=2)
    stq = bank(PS, 1, rows=2)
    t0 = em.cnt["tch"]
    v0 = em.cnt["vch"]
    a0 = em.cnt["ach"]
    xb_free = [t0, t0]
    sq_free = [t0, t0]
    for c in range(32):
        i = c % 2
        xb = lb.xb[i]
        ld = em.dma("gpsimd", xb[:, :], xsrc_dram[128 * c:128 * c + 128, :],
                    waits=[("tch", xb_free[i])] + (list(gate_dma) if c < 2 else []),
                    stream=f"lnxb{i}")
        sqt = lb.sq[i]
        sv = em.act(lambda e, xb=xb, sqt=sqt: e.activation(sqt[:, :], xb[:, :], AF.Square),
                    waits=[ld, ("tch", sq_free[i])])
        for h in range(2):
            w0 = []
            if c == 0 and h == 0:
                w0 = [("vch", prior_psum_free)] + list(gate_dma)
            em.mm(lambda e, h=h, c=c, xb=xb:
                  e.matmul(stx[:, :], lb.onescol2[:, 2 * h:2 * h + 2],
                           xb[:, 512 * h:512 * h + 512],
                           start=(c == 0 and h == 0), stop=(c == 31 and h == 1)),
                  waits=w0 + ([ld] if h == 0 else []))
            mk = em.mm(lambda e, h=h, c=c, sqt=sqt:
                       e.matmul(stq[:, :], lb.onescol2[:, 2 * h:2 * h + 2],
                                sqt[:, 512 * h:512 * h + 512],
                                start=(c == 0 and h == 0), stop=(c == 31 and h == 1)),
                       waits=[("ach", sv)] if h == 0 else [], inc=(h == 1))
        xb_free[i] = mk
        sq_free[i] = mk

    tdone = em.cnt["tch"]
    em.vec(lambda e: e.tensor_copy(lb.sxs[:, :], stx[:, :]), waits=[("tch", tdone)])
    em.vec(lambda e: e.tensor_copy(lb.sqs[:, :], stq[:, :]))
    em.vec(lambda e: e.tensor_scalar_mul(lb.m2[:, :], lb.sxs[:, :], 1.0 / D))
    em.vec(lambda e: e.tensor_scalar_mul(lb.var[:, :], lb.sqs[:, :], 1.0 / D))
    em.vec(lambda e: e.tensor_tensor(lb.tmp2[:, :], lb.m2[:, :], lb.m2[:, :], ALU.mult))
    vv = em.vec(lambda e: e.tensor_tensor(lb.var[:, :], lb.var[:, :], lb.tmp2[:, :], ALU.subtract))
    uval = em.act(lambda e: e.activation(lb.u2[:, :], lb.var[:, :], AF.Sqrt,
                                         bias=lb.epsc[0:2, 0:1]),
                  waits=[("vch", vv), ("vch", lb.eps_ready)])
    em.vec(lambda e: e.reciprocal(lb.inv[:, :], lb.u2[:, :]), waits=[("ach", uval)])
    vready = em.vec(lambda e: e.tensor_tensor(lb.minv[:, :], lb.m2[:, :], lb.inv[:, :], ALU.mult))

    prev_evac = 0
    bdone = 0
    for (vecs, dst) in ((lb.inv, lb.invb), (lb.minv, lb.minvb)):
        for h in range(2):
            em.mm(lambda e, vecs=vecs, h=h:
                  e.matmul(bank(PS, 2 + h), lb.sel2[:, 128 * h:128 * h + 128],
                           vecs[:, :], start=True, stop=True),
                  waits=[("vch", vready), ("vch", prev_evac)], inc=(h == 1))
        t = em.cnt["tch"]
        for h in range(2):
            bdone = em.vec(lambda e, dst=dst, h=h:
                           e.tensor_copy(dst[:, 512 * h:512 * h + 512], bank(PS, 2 + h)),
                           waits=[("tch", t)])
        prev_evac = bdone

    xf_free = [v0, v0]
    zready = 0
    for c in range(32):
        i = c % 2
        xf = lb.xf[i]
        ld = em.dma("sync", xf[:, :], xsrc_dram[128 * c:128 * c + 128, :],
                    waits=[("vch", xf_free[i])] + (list(gate_dma) if c < 2 else []),
                    stream=f"lnxf{i}")
        em.vec(lambda e, xf=xf: e.tensor_tensor(lb.t1[:, :], xf[:, :], lb.invb[:, :], ALU.mult),
               waits=[ld, ("vch", bdone)])
        zready = em.vec(lambda e, c=c: e.tensor_tensor(
            zt[:, TPC * c:TPC * c + TPC], lb.t1[:, :], lb.minvb[:, :], ALU.subtract))
        xf_free[i] = zready
    return zready


def emit_sumz(em, lb, zt, zrow2, ones1, sumz_gate=0):
    PS = lb.PS
    t = None
    for h in range(2):
        ps = bank(PS, h, rows=1)
        for c in range(32):
            t = em.mm(lambda e, ps=ps, c=c, h=h:
                      e.matmul(ps[:, :], ones1[:, :],
                               zt[:, TPC * c + 512 * h:TPC * c + 512 * h + 512],
                               start=(c == 0), stop=(c == 31)),
                      waits=[("vch", sumz_gate)] if (c == 0 and h == 0) else (),
                      inc=(h == 1 and c == 31))
    v = 0
    for h in range(2):
        v = em.vec(lambda e, h=h: e.tensor_copy(zrow2[0:1, 512 * h:512 * h + 512],
                                                bank(PS, h, rows=1)),
                   waits=[("tch", t)])
    return v


def emit_ffn(em, lb, F, zt, z_ready, sumz_v, xin_dram, xout_dram):
    PS = lb.PS
    v0 = em.cnt["vch"]
    t0 = em.cnt["tch"]
    psum_free = [v0] * 8
    wby_free = [v0, v0]
    x1b_free = [v0, v0]
    x1o_free = [None, None]
    last_store = None
    F["ext_free"] = [t0, t0]
    for og in range(8):
        grp_started = False
        for bc in range(16):
            i2 = bc % 2
            wb = F["wby"][i2]
            ld = em.dma("sync", wb[:, :],
                        F["wpt"][128 * bc:128 * bc + 128, 512 * og:512 * og + 512],
                        waits=[("vch", wby_free[i2])], stream=f"wby{i2}")
            lo8, hi8 = F["lo8"][i2], F["hi8"][i2]
            lo, hi = F["lo"][i2], F["hi"][i2]
            em.vec(lambda e, wb=wb, lo8=lo8: e.tensor_scalar(
                lo8[:, :], wb[:, :], 15, None, ALU.bitwise_and),
                waits=[ld, ("tch", F["ext_free"][i2])])
            em.vec(lambda e, wb=wb, hi8=hi8: e.tensor_scalar(
                hi8[:, :], wb[:, :], 4, None, ALU.logical_shift_right))
            em.vec(lambda e, lo8=lo8, lo=lo: e.tensor_copy(lo[:, :], lo8[:, :]))
            vex = em.vec(lambda e, hi8=hi8, hi=hi: e.tensor_copy(hi[:, :], hi8[:, :]))
            wby_free[i2] = vex
            for par, wsb in ((0, lo), (1, hi)):
                ic = 2 * bc + par
                for th in range(2):
                    for oc in range(4):
                        psi = 4 * th + oc
                        wl = []
                        if bc == 0 and par == 0:
                            if th == 0 and oc == 0:
                                wl = [("vch", max(z_ready, sumz_v, vex)),
                                      F["zr1_done"]]
                            wl.append(("vch", psum_free[psi]))
                        elif par == 0 and th == 0 and oc == 0:
                            wl = [("vch", vex)]
                        mk = em.mm(lambda e, psi=psi, wsb=wsb, oc=oc, ic=ic, th=th,
                                   s=(bc == 0 and par == 0):
                                   e.matmul(bank(PS, psi),
                                            wsb[:, 128 * oc:128 * oc + 128],
                                            zt[:, TPC * ic + 512 * th:TPC * ic + 512 * th + 512],
                                            start=s, stop=False),
                                   waits=wl,
                                   inc=(par == 1 and th == 1 and oc == 3))
            F["ext_free"][i2] = mk
        for th in range(2):
            for oc in range(4):
                psi = 4 * th + oc
                oco = 4 * og + oc
                tm = em.mm(lambda e, psi=psi, oco=oco, th=th:
                           e.matmul(bank(PS, psi),
                                    F["zpb2"][:, 128 * oco:128 * oco + 128],
                                    F["zrow2"][:, 512 * th:512 * th + 512],
                                    start=False, stop=True),
                           inc=True)
                i2 = psi % 2
                xb_ = F["x1b"][i2]
                xo_ = F["x1o"][i2]
                ldx = em.dma("gpsimd", xb_[:, :],
                             xin_dram[128 * oco:128 * oco + 128, 512 * th:512 * th + 512],
                             waits=[("vch", x1b_free[i2])] + list(F["xin_gate"]),
                             stream=f"ffnx{i2}")
                ev = em.vec(lambda e, psi=psi, oco=oco, xb_=xb_, xo_=xo_:
                            e.scalar_tensor_tensor(xo_[:, :], bank(PS, psi),
                                                   F["hs"][:, oco:oco + 1],
                                                   xb_[:, :], ALU.mult, ALU.add),
                            waits=[("tch", tm), ldx, x1o_free[i2]])
                psum_free[psi] = ev
                x1b_free[i2] = ev
                last_store = em.dma("sync",
                                    xout_dram[128 * oco:128 * oco + 128,
                                              512 * th:512 * th + 512],
                                    xo_[:, :], waits=[("vch", ev)],
                                    stream=f"ffno{i2}")
                x1o_free[i2] = last_store
    return last_store



def build_l1(trunc=None):
    nc = bass.Bass("TRN2", target_bir_lowering=False)
    xt = nc.dram_tensor("xt", [D, TPC], F32, kind="ExternalInput").ap()
    wqt = nc.dram_tensor("wqt", [D, D], F32, kind="ExternalInput").ap()
    wkt = nc.dram_tensor("wkt", [D, NKV * HD], F32, kind="ExternalInput").ap()
    wvt = nc.dram_tensor("wvt", [D, NKV * HD], F32, kind="ExternalInput").ap()
    wpt = nc.dram_tensor("wpt", [D // 2, D], U8, kind="ExternalInput").ap()
    zp32 = nc.dram_tensor("zp32", [32, 128], F32, kind="ExternalInput").ap()
    bf32 = nc.dram_tensor("bf32", [32, 128], F32, kind="ExternalInput").ap()
    sc32 = nc.dram_tensor("sc32", [32, 128], F32, kind="ExternalInput").ap()
    qnw = nc.dram_tensor("qnw", [1, 128], F32, kind="ExternalInput").ap()
    knw = nc.dram_tensor("knw", [1, 128], F32, kind="ExternalInput").ap()
    sel2 = nc.dram_tensor("sel2", [2, 256], F32, kind="ExternalInput").ap()
    onescol2 = nc.dram_tensor("onescol2", [128, 4], BF16, kind="ExternalInput").ap()
    identf = nc.dram_tensor("identf", [128, 128], F32, kind="ExternalInput").ap()
    qselt = nc.dram_tensor("qselt", [128, 32 * 32], BF16, kind="ExternalInput").ap()
    hsel = nc.dram_tensor("hsel", [32, 32 * 128], F32, kind="ExternalInput").ap()

    qraw_d = nc.dram_tensor("qraw_d", [NH, HD, TPC], BF16, kind="Internal").ap()
    x1t = nc.dram_tensor("x1t", [D, TPC], F32, kind="ExternalOutput").ap()
    kt = nc.dram_tensor("kt", [NKV, HD, TPC], BF16, kind="ExternalOutput").ap()
    vv = nc.dram_tensor("vv", [TPC, NKV * HD], BF16, kind="ExternalOutput").ap()
    qt = nc.dram_tensor("qt", [NH, HD, TPC], BF16, kind="ExternalOutput").ap()

    em = Em(nc)
    es = em.es
    sbf = lambda name, shape, dt: es.enter_context(nc.sbuf_tensor(name, shape, dt))
    PS = es.enter_context(nc.psum_tensor("PS", [128, 4096], F32))

    lb = LNBufs(em, sbf, PS)
    zt = sbf("zt", [128, 32 * TPC], BF16)

    F = {}
    F["wpt"] = wpt
    F["wby"] = [sbf("wby0", [128, 512], U8), sbf("wby1", [128, 512], U8)]
    F["lo8"] = [sbf("lo80", [128, 512], U8), sbf("lo81", [128, 512], U8)]
    F["hi8"] = [sbf("hi80", [128, 512], U8), sbf("hi81", [128, 512], U8)]
    F["lo"] = [sbf("lo0", [128, 512], BF16), sbf("lo1", [128, 512], BF16)]
    F["hi"] = [sbf("hi0", [128, 512], BF16), sbf("hi1", [128, 512], BF16)]
    F["zpb2"] = sbf("zpb2", [2, D], BF16)
    F["zrow2"] = sbf("zrow2", [2, TPC], BF16)
    F["hs"] = sbf("hs", [128, 32], F32)
    F["x1b"] = [sbf("x1b0", [128, 512], F32), sbf("x1b1", [128, 512], F32)]
    F["x1o"] = [sbf("x1o0", [128, 512], F32), sbf("x1o1", [128, 512], F32)]
    F["ext_free"] = [0, 0]

    srow1b = sbf("srow1b", [1, TPC], BF16)
    zp32s = sbf("zp32s", [32, 128], F32)
    bf32s = sbf("bf32s", [32, 128], F32)
    nzp32 = sbf("nzp32", [32, 128], BF16)
    bs32 = sbf("bs32", [32, 128], BF16)
    rs32 = sbf("rs32", [32, 128], F32)
    sc32s = sbf("sc32s", [32, 128], F32)
    identfs = sbf("identfs", [128, 128], F32)
    ones1 = sbf("ones1", [128, 1], BF16)
    qsels = sbf("qsels", [128, 32 * 32], BF16)
    hsels = sbf("hsels", [32, 32 * 128], F32)
    sqq = sbf("sqq", [128, TPC], BF16)
    stq32 = sbf("stq32", [32, TPC], F32)
    u32 = sbf("u32", [32, TPC], F32)
    inv32 = sbf("inv32", [32, TPC], F32)
    invqb = sbf("invqb", [128, TPC], F32)
    epsr = sbf("epsr", [128, 1], F32)
    qnwc = sbf("qnwc", [128, 1], F32)
    knwc = sbf("knwc", [128, 1], F32)
    hbuf = [sbf("hbuf0", [128, TPC], BF16), sbf("hbuf1", [128, TPC], BF16)]
    qout = [sbf("qout0", [128, TPC], BF16), sbf("qout1", [128, TPC], BF16)]
    wk_b = [sbf("wkb0", [128, 256], BF16), sbf("wkb1", [128, 256], BF16)]
    wq_b = [sbf("wqb0", [128, 512], BF16), sbf("wqb1", [128, 512], BF16)]
    wv_b = [sbf("wvb0", [128, 1024], BF16), sbf("wvb1", [128, 1024], BF16)]

    l_oc2 = em.dma("sync", lb.onescol2[:, :], onescol2[:, :])
    l_sel2 = em.dma("sync", lb.sel2[:, :], sel2[:, :])
    l_zpf = em.dma("sync", zp32s[:, :], zp32[:, :])
    l_bff = em.dma("sync", bf32s[:, :], bf32[:, :])
    l_sc32 = em.dma("sync", sc32s[:, :], sc32[:, :])
    l_idf = em.dma("sync", identfs[:, :], identf[:, :])
    l_qsel = em.dma("sync", qsels[:, :], qselt[:, :])
    l_hsel = em.dma("sync", hsels[:, :], hsel[:, :])
    l_qnw = em.dma("sync", qnwc[:, 0:1], qnw.rearrange("1 p -> p 1"))
    l_knw = em.dma("sync", knwc[:, 0:1], knw.rearrange("1 p -> p 1"))
    em.vec(lambda e: e.memset(ones1[:, :], 1.0))
    lb.eps_ready = em.vec(lambda e: e.memset(lb.epsc[:, :], EPS_LN))
    v_epsr = em.vec(lambda e: e.memset(epsr[:, :], 128.0 * EPS_RMS))
    vnzp = em.vec(lambda e: e.tensor_scalar_mul(nzp32[:, :], zp32s[:, :], -1.0),
                  waits=[l_zpf])
    em.vec(lambda e: e.reciprocal(rs32[:, :], sc32s[:, :]),
           waits=[l_sc32])
    vdiv = em.vec(lambda e: e.tensor_tensor(bs32[:, :], bf32s[:, :], rs32[:, :],
                                            ALU.mult),
                  waits=[l_bff])
    d_zpb0 = em.dma("gpsimd", F["zpb2"][0:1, :], nzp32[:, :], waits=[("vch", vnzp)], stream="zpb0")
    d_zpb1 = em.dma("gpsimd", F["zpb2"][1:2, :], bs32[:, :], waits=[("vch", vdiv)], stream="zpb1")
    vone = em.vec(lambda e: e.memset(srow1b[0:1, 0:TPC], 1.0))
    F["zr1_done"] = em.dma("gpsimd", F["zrow2"][1:2, :], srow1b[0:1, 0:TPC],
                           waits=[("vch", vone), d_zpb0, d_zpb1], stream="zr1")
    t_hs = em.mm(lambda e: e.transpose(bank(PS, 7)[:, 0:32], sc32s[:, :], identfs[0:32, 0:32]),
                 waits=[l_idf, l_sc32], inc=True)
    v_hs = em.vec(lambda e: e.tensor_scalar_mul(F["hs"][:, :], bank(PS, 7)[:, 0:32], 0.5),
                  waits=[("tch", t_hs)])

    if trunc == "pro":
        em.finalize([("vector", "vch", em.cnt["vch"])])
        return nc
    gate_consts = em.qbar("sq")
    z1 = emit_ln_z(em, lb, xt, zt, prior_psum_free=v_hs, gate_dma=gate_consts)
    sz1 = emit_sumz(em, lb, zt, F["zrow2"], ones1, sumz_gate=z1)

    if trunc == "ln1":
        em.finalize([("vector", "vch", em.cnt["vch"])])
        return nc
    F["xin_gate"] = ()
    ffn1_sq = emit_ffn(em, lb, F, zt, z1, sz1, xt, x1t)

    if trunc == "ffn1":
        em.finalize([("vector", "vch", em.cnt["vch"])])
        return nc
    z2 = emit_ln_z(em, lb, x1t, zt, prior_psum_free=em.cnt["vch"], gate_dma=em.qbar("sq"))

    if trunc == "ln2":
        em.finalize([("vector", "vch", em.cnt["vch"])])
        return nc
    pstat = PS[0:32, 4 * 512:4 * 512 + 1024]

    kraw = sbf("kraw", [128, 8 * TPC], BF16)

    def stats_flags(h, th, first, total):
        return (h == first and th == 0, h == first + total - 1 and th == 1)

    kproj_psum_free = 0
    kstats_mark = 0
    ngroups = 1 if (trunc or "").startswith("kg0") else 4
    nostat = (trunc or "").endswith("nostat")
    nosq = (trunc or "").endswith("nosq")
    if (trunc or "").endswith("nokraw"):
        pass
    for g in range(ngroups):
        heads = (2 * g, 2 * g + 1)
        wb_free = [em.cnt["tch"], em.cnt["tch"]]
        for i in range(32):
            i2 = i % 2
            wb = wk_b[i2]
            ld = em.dma("gpsimd", wb[:, :],
                        wkt[128 * i:128 * i + 128, HD * heads[0]:HD * heads[0] + 256],
                        waits=[("tch", wb_free[i2])], stream=f"wk{i2}")
            for hh in range(2):
                for th in range(2):
                    wl = []
                    if hh == 0 and th == 0:
                        wl.append(ld)
                    if i == 0 and hh == 0 and th == 0:
                        wl.append(("vch", z2))
                        wl.append(("tbar", kproj_psum_free))
                    mk = em.mm(lambda e, hh=hh, th=th, i=i, wb=wb:
                               e.matmul(PS[0:128, (2 * hh + th) * 512:(2 * hh + th) * 512 + 512],
                                        wb[:, 128 * hh:128 * hh + 128],
                                        zt[:, TPC * i + 512 * th:TPC * i + 512 * th + 512],
                                        start=(i == 0), stop=(i == 31)),
                               waits=wl, inc=(hh == 1 and th == 1))
            wb_free[i2] = mk
        tdone = em.cnt["tch"]
        for hh, h in enumerate(heads):
            em.vec(lambda e, hh=hh, h=h: e.tensor_copy(
                kraw[:, TPC * h:TPC * h + TPC],
                PS[0:128, (2 * hh) * 512:(2 * hh) * 512 + 1024]),
                waits=[("tch", tdone)])
            sv = em.vec(lambda e, h=h: e.tensor_tensor(
                sqq[:, :], kraw[:, TPC * h:TPC * h + TPC],
                kraw[:, TPC * h:TPC * h + TPC], ALU.mult),
                waits=[("tch", kstats_mark)])
            for th in range(2):
                kstats_mark = em.mm(lambda e, h=h, th=th:
                      e.matmul(pstat[:, 512 * th:512 * th + 512],
                               qsels[:, 32 * h:32 * h + 32],
                               sqq[:, 512 * th:512 * th + 512],
                               start=(h == 0), stop=(h == 7 if ngroups == 4 else h == 1)),
                      waits=[("vch", sv)] if th == 0 else (), inc=(th == 1)) or kstats_mark
        kproj_psum_free = em.tmark(waits=[("vch", em.cnt["vch"])])
    t_kstats = em.cnt["tch"]

    if trunc is not None and trunc.startswith(("kgrp", "kg0")):
        em.finalize([("vector", "vch", em.cnt["vch"])])
        return nc
    vks = em.vec(lambda e: e.tensor_copy(stq32[:, :], pstat[:, :]),
                 waits=[("tch", t_kstats)])
    uk = em.act(lambda e: e.activation(u32[:, :], stq32[:, :], AF.Sqrt,
                                       bias=epsr[0:32, 0:1]),
                waits=[("vch", vks), ("vch", v_epsr)])
    em.vec(lambda e: e.reciprocal(inv32[:, :], u32[:, :]), waits=[("ach", uk)])
    vk_inv = em.vec(lambda e: e.tensor_scalar_mul(inv32[:, :], inv32[:, :], SQRT128))
    kstat_free = vk_inv

    if trunc == "kstat":
        em.finalize([("vector", "vch", em.cnt["vch"])])
        return nc
    kn_store_p = [None, None]
    prev_vb = vk_inv
    for h in range(NKV):
        for th in range(2):
            em.mm(lambda e, h=h, th=th:
                  e.matmul(bank(PS, 6 + th), hsels[:, 128 * h:128 * h + 128],
                           inv32[:, 512 * th:512 * th + 512], start=True, stop=True),
                  waits=[("vch", prev_vb)], inc=(th == 1))
        tb = em.cnt["tch"]
        vb = 0
        for th in range(2):
            vb = em.vec(lambda e, th=th: e.tensor_copy(
                invqb[:, 512 * th:512 * th + 512], bank(PS, 6 + th)),
                waits=[("tch", tb)])
        i2 = h % 2
        prev_vb = vb
        ev = em.vec(lambda e, h=h, i2=i2: e.scalar_tensor_tensor(
            hbuf[i2][:, :], kraw[:, TPC * h:TPC * h + TPC], knwc[:, 0:1],
            invqb[:, :], ALU.mult, ALU.mult),
            waits=[("vch", vb), kn_store_p[i2], l_knw])
        kn_store_p[i2] = em.dma("sync", kt[h, :, :], hbuf[i2][:, :], waits=[("vch", ev)], stream=f"kst{i2}")

    if trunc == "kproj":
        em.finalize([("vector", "vch", em.cnt["vch"])])
        return nc
    v_psum_free = em.tmark(waits=[("vch", em.cnt["vch"])])
    v_store_p = [kn_store_p[0], kn_store_p[1]]
    for ttg in range(2):
        wb_free = [em.cnt["tch"], em.cnt["tch"]]
        for i in range(32):
            i2 = i % 2
            wb = wv_b[i2]
            ld = em.dma("gpsimd", wb[:, :], wvt[128 * i:128 * i + 128, :],
                        waits=[("tch", wb_free[i2])], stream=f"wv{i2}")
            for tt in range(4):
                t_idx = 4 * ttg + tt
                for th in range(2):
                    wl = []
                    if tt == 0 and th == 0:
                        wl.append(ld)
                    if i == 0 and tt == 0 and th == 0:
                        wl.append(("tbar", v_psum_free))
                    mk = em.mm(lambda e, i=i, tt=tt, th=th, t_idx=t_idx, wb=wb:
                               e.matmul(bank(PS, 2 * tt + th),
                                        zt[:, TPC * i + 128 * t_idx:TPC * i + 128 * t_idx + 128],
                                        wb[:, 512 * th:512 * th + 512],
                                        start=(i == 0), stop=(i == 31)),
                               waits=wl, inc=(tt == 3 and th == 1))
            wb_free[i2] = mk
        tv = em.cnt["tch"]
        for tt in range(4):
            t_idx = 4 * ttg + tt
            i2 = tt % 2
            em.vec(lambda e, tt=tt, i2=i2: e.tensor_copy(
                hbuf[i2][:, 0:512], bank(PS, 2 * tt)),
                waits=[("tch", tv), v_store_p[i2]])
            ev2 = em.vec(lambda e, tt=tt, i2=i2: e.tensor_copy(
                hbuf[i2][:, 512:1024], bank(PS, 2 * tt + 1)))
            v_store_p[i2] = em.dma("sync", vv[128 * t_idx:128 * t_idx + 128, :],
                                   hbuf[i2][:, :], waits=[("vch", ev2)], stream=f"vst{i2}")
        v_psum_free = em.tmark(waits=[("vch", em.cnt["vch"])])

    if trunc == "v":
        em.finalize([("vector", "vch", em.cnt["vch"])])
        return nc
    q_psum_free = v_psum_free
    qstats_mark = kstats_mark
    qraw_store = [v_store_p[0], v_store_p[1]]
    for g in range(16):
        heads = (2 * g, 2 * g + 1)
        wb_free = [em.cnt["tch"], em.cnt["tch"]]
        for i in range(32):
            i2 = i % 2
            wb = wq_b[i2]
            ld = em.dma("gpsimd", wb[:, 0:256],
                        wqt[128 * i:128 * i + 128, HD * heads[0]:HD * heads[0] + 256],
                        waits=[("tch", wb_free[i2])], stream=f"wq{i2}")
            for hh in range(2):
                for th in range(2):
                    wl = []
                    if hh == 0 and th == 0:
                        wl.append(ld)
                    if i == 0 and hh == 0 and th == 0:
                        wl.append(("tbar", q_psum_free))
                    mk = em.mm(lambda e, hh=hh, th=th, i=i, wb=wb:
                               e.matmul(PS[0:128, (2 * hh + th) * 512:(2 * hh + th) * 512 + 512],
                                        wb[:, 128 * hh:128 * hh + 128],
                                        zt[:, TPC * i + 512 * th:TPC * i + 512 * th + 512],
                                        start=(i == 0), stop=(i == 31)),
                               waits=wl, inc=(hh == 1 and th == 1))
            wb_free[i2] = mk
        tdone = em.cnt["tch"]
        for hh, h in enumerate(heads):
            qev = em.vec(lambda e, hh=hh: e.tensor_copy(
                hbuf[hh][:, :],
                PS[0:128, (2 * hh) * 512:(2 * hh) * 512 + 1024]),
                waits=[("tch", tdone), qraw_store[hh]])
            sv = em.vec(lambda e, hh=hh: e.tensor_tensor(
                sqq[:, :], hbuf[hh][:, :], hbuf[hh][:, :], ALU.mult),
                waits=[("tch", qstats_mark)])
            for th in range(2):
                qstats_mark = em.mm(lambda e, h=h, th=th:
                      e.matmul(pstat[:, 512 * th:512 * th + 512],
                               qsels[:, 32 * h:32 * h + 32],
                               sqq[:, 512 * th:512 * th + 512],
                               start=(h == 0), stop=(h == 31)),
                      waits=[("vch", sv), ("vch", kstat_free)] if th == 0 else (),
                      inc=(th == 1)) or qstats_mark
            qraw_store[hh] = em.dma("sync", qraw_d[h, :, :], hbuf[hh][:, :],
                                    waits=[("vch", qev)], stream=f"qrst{hh}")
        q_psum_free = em.tmark(waits=[("vch", em.cnt["vch"])])
    t_qstats = em.cnt["tch"]

    vqs = em.vec(lambda e: e.tensor_copy(stq32[:, :], pstat[:, :]),
                 waits=[("tch", t_qstats)])
    uq = em.act(lambda e: e.activation(u32[:, :], stq32[:, :], AF.Sqrt,
                                       bias=epsr[0:32, 0:1]),
                waits=[("vch", vqs)])
    vq_inv = em.vec(lambda e: e.reciprocal(inv32[:, :], u32[:, :]), waits=[("ach", uq)])

    qn_store_p = [None, None]
    qn_read = [em.cnt["vch"], em.cnt["vch"]]
    prev_vb = vq_inv
    for h in range(NH):
        for th in range(2):
            em.mm(lambda e, h=h, th=th:
                  e.matmul(bank(PS, 6 + th), hsels[:, 128 * (h % 32):128 * (h % 32) + 128],
                           inv32[:, 512 * th:512 * th + 512], start=True, stop=True),
                  waits=[("vch", prev_vb)], inc=(th == 1))
        tb = em.cnt["tch"]
        vb = 0
        for th in range(2):
            vb = em.vec(lambda e, th=th: e.tensor_copy(
                invqb[:, 512 * th:512 * th + 512], bank(PS, 6 + th)),
                waits=[("tch", tb)])
        i2 = h % 2
        lr = em.dma("gpsimd", hbuf[i2][:, :], qraw_d[h, :, :],
                    waits=[("vch", qn_read[i2])] +
                          ([qraw_store[0], qraw_store[1]] if h < 2 else []),
                    stream=f"qlr{i2}")
        prev_vb = vb
        ev = em.vec(lambda e, h=h, i2=i2: e.scalar_tensor_tensor(
            qout[i2][:, :], hbuf[i2][:, :], qnwc[:, 0:1],
            invqb[:, :], ALU.mult, ALU.mult),
            waits=[("vch", vb), lr, l_qnw, qn_store_p[i2]])
        qn_read[i2] = ev
        qn_store_p[i2] = em.dma("sync", qt[h, :, :], qout[i2][:, :], waits=[("vch", ev)], stream=f"qst{i2}")

    em.finalize([("vector", "vch", em.cnt["vch"])])
    return nc




def build_l2(core_mod=None):
    nc = bass.Bass("TRN2", target_bir_lowering=False)
    ktall = nc.dram_tensor("ktall", [NKV, HD, S], BF16, kind="ExternalInput").ap()
    vpe = nc.dram_tensor("vpe", [NKV, 128, 64, 128], BF16, kind="ExternalInput").ap()
    qt = nc.dram_tensor("qt", [NH, HD, TPC], BF16, kind="ExternalInput").ap()
    x1t = nc.dram_tensor("x1t", [D, TPC], F32, kind="ExternalInput").ap()
    wot = nc.dram_tensor("wot", [D, D], F32, kind="ExternalInput").ap()
    wpt = nc.dram_tensor("wpt", [D // 2, D], U8, kind="ExternalInput").ap()
    zp32 = nc.dram_tensor("zp32", [32, 128], F32, kind="ExternalInput").ap()
    bf32 = nc.dram_tensor("bf32", [32, 128], F32, kind="ExternalInput").ap()
    sc32 = nc.dram_tensor("sc32", [32, 128], F32, kind="ExternalInput").ap()
    albl = nc.dram_tensor("albl", [NH, 6, TPC], BF16, kind="ExternalInput").ap()
    albr = nc.dram_tensor("albr", [NH, 6, S], BF16, kind="ExternalInput").ap()
    mtin = nc.dram_tensor("mtin", [128, 8 * 128], BF16, kind="ExternalInput").ap()
    identb = nc.dram_tensor("identb", [128, 128], BF16, kind="ExternalInput").ap()
    identf = nc.dram_tensor("identf", [128, 128], F32, kind="ExternalInput").ap()
    sel2 = nc.dram_tensor("sel2", [2, 256], F32, kind="ExternalInput").ap()
    onescol2 = nc.dram_tensor("onescol2", [128, 4], BF16, kind="ExternalInput").ap()

    yt = nc.dram_tensor("yt", [D, TPC], F32, kind="ExternalOutput").ap()
    o_park = nc.dram_tensor("o_park", [TPC, D], BF16, kind="Internal").ap()
    x2t = nc.dram_tensor("x2t", [D, TPC], F32, kind="Internal").ap()

    em = Em(nc)
    es = em.es
    sbf = lambda name, shape, dt: es.enter_context(nc.sbuf_tensor(name, shape, dt))
    PS = es.enter_context(nc.psum_tensor("PS", [128, 4096], F32))

    F = {}
    F["wpt"] = wpt
    F["zpb2"] = sbf("zpb2", [2, D], BF16)
    F["zrow2"] = sbf("zrow2", [2, TPC], BF16)
    F["hs"] = sbf("hs", [128, 32], F32)
    F["ext_free"] = [0, 0]
    oc2e = sbf("oc2e", [128, 4], BF16)
    sel2e = sbf("sel2e", [2, 256], F32)
    epse = sbf("epse", [128, 1], F32)
    srow1b = sbf("srow1b", [1, TPC], BF16)
    zp32s = sbf("zp32s", [32, 128], F32)
    bf32s = sbf("bf32s", [32, 128], F32)
    nzp32 = sbf("nzp32", [32, 128], BF16)
    bs32 = sbf("bs32", [32, 128], BF16)
    rs32 = sbf("rs32", [32, 128], F32)
    sc32s = sbf("sc32s", [32, 128], F32)
    identfs = sbf("identfs", [128, 128], F32)
    identbs = sbf("identbs", [128, 128], BF16)
    ones1 = sbf("ones1", [128, 1], BF16)
    neginf = sbf("neginf", [128, 1], F32)
    mbuf = [sbf("mbuf0", [128, 1], F32), sbf("mbuf1", [128, 1], F32),
            sbf("mbuf2", [128, 1], F32), sbf("mbuf3", [128, 1], F32)]
    negm_b = [sbf("negm_b0", [128, 1], F32), sbf("negm_b1", [128, 1], F32)]
    lbuf = [sbf("lbuf0", [128, 1], F32), sbf("lbuf1", [128, 1], F32)]
    linv_b = [sbf("linv_b0", [128, 1], F32), sbf("linv_b1", [128, 1], F32)]
    osb = [sbf("osb0", [128, 128], BF16), sbf("osb1", [128, 128], BF16)]
    mt = sbf("mt", [128, 8 * 128], BF16)

    l_oc2 = em.dma("sync", oc2e[:, :], onescol2[:, :])
    l_sel2 = em.dma("sync", sel2e[:, :], sel2[:, :])
    l_zpf = em.dma("sync", zp32s[:, :], zp32[:, :])
    l_bff = em.dma("sync", bf32s[:, :], bf32[:, :])
    l_sc32 = em.dma("sync", sc32s[:, :], sc32[:, :])
    l_idf = em.dma("sync", identfs[:, :], identf[:, :])
    l_idb = em.dma("sync", identbs[:, :], identb[:, :])
    l_mt = em.dma("sync", mt[:, :], mtin[:, :])
    em.vec(lambda e: e.memset(ones1[:, :], 1.0))
    eps_ready_v = em.vec(lambda e: e.memset(epse[:, :], EPS_LN))
    em.vec(lambda e: e.memset(neginf[:, :], -3.0e38))
    vnzp = em.vec(lambda e: e.tensor_scalar_mul(nzp32[:, :], zp32s[:, :], -1.0),
                  waits=[l_zpf])
    em.vec(lambda e: e.reciprocal(rs32[:, :], sc32s[:, :]), waits=[l_sc32])
    vdiv = em.vec(lambda e: e.tensor_tensor(bs32[:, :], bf32s[:, :], rs32[:, :],
                                            ALU.mult), waits=[l_bff])
    d_zpb0 = em.dma("gpsimd", F["zpb2"][0:1, :], nzp32[:, :], waits=[("vch", vnzp)], stream="zpb0")
    d_zpb1 = em.dma("gpsimd", F["zpb2"][1:2, :], bs32[:, :], waits=[("vch", vdiv)], stream="zpb1")
    vone = em.vec(lambda e: e.memset(srow1b[0:1, 0:TPC], 1.0))
    F["zr1_done"] = em.dma("gpsimd", F["zrow2"][1:2, :], srow1b[0:1, 0:TPC],
                           waits=[("vch", vone), d_zpb0, d_zpb1], stream="zr1")
    t_hs = em.mm(lambda e: e.transpose(bank(PS, 7)[:, 0:32], sc32s[:, :], identfs[0:32, 0:32]),
                 waits=[l_idf, l_sc32], inc=True)
    v_hs = em.vec(lambda e: e.tensor_scalar_mul(F["hs"][:, :], bank(PS, 7)[:, 0:32], 0.5),
                  waits=[("tch", t_hs)])
    pro_gq = em.qbar("gq")

    es_a = ExitStack()
    sba = lambda name, shape, dt: es_a.enter_context(nc.sbuf_tensor(name, shape, dt))
    kth = sba("kth", [128, S], BF16)
    vh = sba("vh", [128, S], BF16)
    qth = sba("qth", [128, TPC], BF16)
    arh = sba("arh", [6, S], BF16)
    alh = sba("alh", [6, TPC], BF16)
    srow = [sba("srow0", [128, S], F32), sba("srow1", [128, S], F32)]
    prow = [sba("prow0", [128, S], BF16), sba("prow1", [128, S], BF16)]
    ptrow = sba("ptrow", [128, 2 * 2048], BF16)

    kv_last_use = 0
    q_last_use = 0
    exp_p = [0, 0]
    recip_p = [0, 0]
    avmark_p = [0, 0]
    oev_p = [0, 0]
    ost_p = [None, None]
    wave_slot_user = [0, 0]
    wave_i = 0
    psum_free = [v_hs] * 4
    combo_i = 0

    for h in range(NKV):
        ld_k = em.dma("gpsimd", kth[:, :], ktall[h, :, :],
                      waits=[("tch", kv_last_use)], stream="kth")
        ld_v = em.dma("gpsimd", vh[:, :], vpe[h, :, :, :],
                      waits=[("tch", kv_last_use)], stream="vh")
        for qh in range(4 * h, 4 * h + 4):
            ld_q = em.dma("gpsimd", qth[:, :], qt[qh, :, :],
                          waits=[("tch", q_last_use)], stream="qth")
            ld_ar = em.dma("gpsimd", arh[:, :], albr[qh, :, :],
                           waits=[("tch", q_last_use)], stream="arh")
            ld_al = em.dma("gpsimd", alh[:, :], albl[qh, :, :],
                           waits=[("tch", q_last_use)], stream="alh")
            def emit_pass1(T):
                nonlocal combo_i
                nch = 2 * (T + 1)
                par = combo_i % 2
                sr = srow[par]
                for k in range(nch):
                    b = k % 4
                    wl = [("vch", psum_free[b])]
                    if k == 0:
                        wl += [ld_k, ld_v, ld_q, ld_ar, ld_al, l_mt, l_idb,
                               ("ach", exp_p[par])]
                    em.mm(lambda e, T=T, k=k, b=b:
                          e.matmul(bank(PS, b), alh[0:6, 128 * T:128 * T + 128],
                                   arh[0:6, 512 * k:512 * k + 512],
                                   start=True, stop=False),
                          waits=wl)
                    diag = (k >= 2 * T)
                    if diag:
                        for u in range(4):
                            blk = 4 * (k - 2 * T) + u
                            em.mm(lambda e, b=b, u=u, blk=blk:
                                  e.matmul(bank(PS, b)[:, 128 * u:128 * u + 128],
                                           identbs[:, :], mt[:, 128 * blk:128 * blk + 128],
                                           start=False, stop=False))
                    mk = em.mm(lambda e, T=T, k=k, b=b:
                               e.matmul(bank(PS, b), qth[:, 128 * T:128 * T + 128],
                                        kth[:, 512 * k:512 * k + 512],
                                        start=False, stop=True),
                               inc=(k % 2 == 1))
                    if k % 2 == 1:
                        ev = em.vec(lambda e, k=k, b=b, sr=sr, par=par, first=(k == 1):
                                    e.tensor_scalar(sr[:, 512 * (k - 1):512 * (k - 1) + 1024],
                                                    PS[0:128, 512 * (b - 1):512 * (b - 1) + 1024],
                                                    0.0,
                                                    neginf[:, 0:1] if first else mbuf[2 * par + (k // 2 - 1) % 2][:, 0:1],
                                                    ALU.add, ALU.max,
                                                    accum_out=mbuf[2 * par + (k // 2) % 2][:, 0:1]),
                                    waits=[("tch", mk)] + ([("ach", exp_p[par])] if k == 1 else []))
                        psum_free[b] = ev
                        psum_free[b - 1] = ev
                st = dict(T=T, par=par, combo=combo_i, qh=qh)
                combo_i += 1
                return st

            def emit_pass2(st):
                nonlocal wave_i
                T = st["T"]; par = st["par"]; qh_ = st["qh"]
                sr = srow[par]; pr = prow[par]
                nch = 2 * (T + 1)
                L = 1024 * (T + 1)
                Lh = (L // 2048) * 1024 if L >= 2048 else L
                vneg = em.vec(lambda e, k=(nch // 2 - 1), par=par: e.tensor_scalar_mul(
                    negm_b[par][:, :], mbuf[2 * par + k % 2][:, :], -1.0))
                expv1 = em.act(lambda e, Lh=Lh, par=par, sr=sr, pr=pr: e.activation(
                    pr[:, 0:Lh], sr[:, 0:Lh], AF.Exp, bias=negm_b[par][:, 0:1],
                    accum_out=lbuf[par][:, 0:1]),
                    waits=[("vch", vneg), ("vch", recip_p[par]), ("tch", avmark_p[par])])
                if Lh < L:
                    expv = em.act(lambda e, Lh=Lh, L=L, par=par, sr=sr, pr=pr: e.activation(
                        pr[:, Lh:L], sr[:, Lh:L], AF.Exp, bias=negm_b[par][:, 0:1],
                        accum_out=mbuf[2 * par][:, 0:1]))
                else:
                    expv = expv1
                exp_p[par] = expv
                if Lh < L:
                    em.vec(lambda e, par=par: e.tensor_tensor(
                        negm_b[par][:, :], lbuf[par][:, :], mbuf[2 * par][:, :], ALU.add),
                        waits=[("ach", expv)])
                    vrec = em.vec(lambda e, par=par: e.reciprocal(linv_b[par][:, :],
                                                                  negm_b[par][:, :]))
                else:
                    vrec = em.vec(lambda e, par=par: e.reciprocal(linv_b[par][:, :], lbuf[par][:, :]),
                                  waits=[("ach", expv)])
                recip_p[par] = vrec
                nblk = 8 * (T + 1)
                av = None
                g0 = 0
                while g0 < nblk:
                    gn = min(16, nblk - g0)
                    slot = wave_i % 2
                    egate = expv1 if (128 * (g0 + gn) <= Lh) else expv
                    trt = em.dmat(
                        "sync",
                        ptrow.ap()[:, 2048 * slot:2048 * slot + 128 * gn].rearrange(
                            "p (b f) -> p b f", b=gn),
                        pr.ap()[:, 128 * g0:128 * g0 + 128 * gn].rearrange(
                            "p (b f) -> p b f", b=gn),
                        waits=[("ach", egate), ("tch", wave_slot_user[slot])]
                        + (list(pro_gq) if st["combo"] == 0 and g0 == 0 else []),
                        stream=f"tr{slot}")
                    for j in range(gn):
                        bb = g0 + j
                        wl = [trt] if j == 0 else []
                        if bb == 0:
                            wl.append(("vch", oev_p[par]))
                        av = em.mm(lambda e, bb=bb, nblk=nblk, slot=slot, j=j, par=par:
                                   e.matmul(PS[0:128, (4 + par) * 512:(4 + par) * 512 + 128],
                                            ptrow[:, 2048 * slot + 128 * j:2048 * slot + 128 * j + 128],
                                            vh[:, 128 * bb:128 * bb + 128],
                                            start=(bb == 0), stop=(bb == nblk - 1)),
                                   waits=wl, inc=(j == gn - 1))
                    wave_slot_user[slot] = av
                    wave_i += 1
                    g0 += gn
                avmark_p[par] = av
                oev = em.vec(lambda e, par=par: e.tensor_scalar(
                    osb[par][:, :], PS[0:128, (4 + par) * 512:(4 + par) * 512 + 128],
                    linv_b[par][:, 0:1], None, ALU.mult),
                    waits=[("tch", av), ("vch", vrec), ost_p[par]])
                oev_p[par] = oev
                ost_p[par] = em.dma("gpsimd",
                                    o_park[128 * T:128 * T + 128,
                                           128 * qh_:128 * qh_ + 128],
                                    osb[par][:, :], waits=[("vch", oev)],
                                    stream=f"ost{par}")
                return av

            pend = None
            for T in range(8):
                st = emit_pass1(T)
                if pend is not None:
                    emit_pass2(pend)
                pend = st
            av_last = emit_pass2(pend)
            kv_last_use = av_last
            q_last_use = av_last

    attn_done = [("ach", max(exp_p)), ("tch", max(avmark_p)),
                 ("vch", max(oev_p)), ost_p[0], ost_p[1]]
    es_a.close()

    es_w = ExitStack()
    sbw = lambda name, shape, dt: es_w.enter_context(nc.sbuf_tensor(name, shape, dt))
    lb = LNBufs(em, sbw, PS)
    lb.onescol2 = oc2e
    lb.sel2 = sel2e
    lb.epsc = epse
    lb.eps_ready = eps_ready_v
    F["wby"] = [sbw("wby0", [128, 512], U8), sbw("wby1", [128, 512], U8)]
    F["lo8"] = [sbw("lo80", [128, 512], U8), sbw("lo81", [128, 512], U8)]
    F["hi8"] = [sbw("hi80", [128, 512], U8), sbw("hi81", [128, 512], U8)]
    F["lo"] = [sbw("lo0", [128, 512], BF16), sbw("lo1", [128, 512], BF16)]
    F["hi"] = [sbw("hi0", [128, 512], BF16), sbw("hi1", [128, 512], BF16)]
    F["x1b"] = [sbw("x1b0", [128, 512], F32), sbw("x1b1", [128, 512], F32)]
    F["x1o"] = [sbw("x1o0", [128, 512], F32), sbw("x1o1", [128, 512], F32)]
    zt = sbw("zt", [128, 32 * TPC], BF16)
    wob = [sbw("wob0", [128, 512], BF16), sbw("wob1", [128, 512], BF16)]
    otb = [sbw("otb00", [128, 512], BF16), sbw("otb01", [128, 512], BF16),
           sbw("otb10", [128, 512], BF16), sbw("otb11", [128, 512], BF16)]
    x2b = [sbw("x2b0", [128, 512], F32), sbw("x2b1", [128, 512], F32)]
    x2o = [sbw("x2o0", [128, 512], F32), sbw("x2o1", [128, 512], F32)]

    wo_psum_free = [max(oev_p)] * 8
    wob_free = [0, 0]
    otb_free = [0, 0, 0, 0]
    x2b_free = [em.cnt["vch"], em.cnt["vch"]]
    x2o_free = [None, None]
    x2_last = None
    for og in range(8):
        for dc in range(32):
            i2 = dc % 2
            wb = wob[i2]
            lw = em.dma("gpsimd", wb[:, :],
                        wot[128 * dc:128 * dc + 128, 512 * og:512 * og + 512],
                        waits=[("tch", wob_free[i2])] +
                              (attn_done if (og == 0 and dc == 0) else []),
                        stream=f"wob{i2}")
            ots = []
            for th in range(2):
                oi = 2 * (dc % 2) + th
                lo_ = em.dmat("sync", otb[oi][:, :],
                              o_park[512 * th:512 * th + 512, 128 * dc:128 * dc + 128],
                              waits=[("tch", otb_free[oi])] +
                                    (attn_done if (og == 0 and dc < 2) else []),
                              stream=f"ot{oi}")
                ots.append(lo_)
            for th in range(2):
                for oc in range(4):
                    psi = 4 * th + oc
                    oi = 2 * (dc % 2) + th
                    wl = []
                    if oc == 0:
                        wl = [lw, ots[th]]
                    if dc == 0 and th == 0 and oc == 0:
                        wl.append(("vch", max(wo_psum_free)))
                    mk = em.mm(lambda e, psi=psi, oc=oc, oi=oi, wb=wb, dc=dc:
                               e.matmul(bank(PS, psi), wb[:, 128 * oc:128 * oc + 128],
                                        otb[oi][:, :], start=(dc == 0), stop=(dc == 31)),
                               waits=wl, inc=(th == 1 and oc == 3))
            wob_free[i2] = mk
            otb_free[2 * (dc % 2)] = mk
            otb_free[2 * (dc % 2) + 1] = mk
        tm = em.cnt["tch"]
        for th in range(2):
            for oc in range(4):
                psi = 4 * th + oc
                oco = 4 * og + oc
                i2 = psi % 2
                lx = em.dma("gpsimd", x2b[i2][:, :],
                            x1t[128 * oco:128 * oco + 128, 512 * th:512 * th + 512],
                            waits=[("vch", x2b_free[i2])], stream=f"x2l{i2}")
                ev = em.vec(lambda e, psi=psi, i2=i2: e.tensor_tensor(
                    x2o[i2][:, :], bank(PS, psi), x2b[i2][:, :], ALU.add),
                    waits=[("tch", tm), lx, x2o_free[i2]])
                wo_psum_free[psi] = ev
                x2b_free[i2] = ev
                x2_last = em.dma("sync", x2t[128 * oco:128 * oco + 128,
                                             512 * th:512 * th + 512],
                                 x2o[i2][:, :], waits=[("vch", ev)], stream=f"x2s{i2}")
                x2o_free[i2] = x2_last

    z3 = emit_ln_z(em, lb, x2t, zt, prior_psum_free=em.cnt["vch"],
                   gate_dma=em.qbar("sq"))
    sz3 = emit_sumz(em, lb, zt, F["zrow2"], ones1, sumz_gate=z3)
    F["xin_gate"] = em.qbar("sq")
    emit_ffn(em, lb, F, zt, z3, sz3, x2t, yt)

    es_w.close()
    em.finalize([("vector", "vch", em.cnt["vch"])])
    return nc


def _consts_l1(inputs):
    Wq = np.asarray(inputs["L0_attn_Wq"])
    Wk = np.asarray(inputs["L0_attn_Wk"])
    Wv = np.asarray(inputs["L0_attn_Wv"])
    wp = np.asarray(inputs["L0_ffn_weight_packed"])
    scale = np.asarray(inputs["L0_ffn_scale"])[:, 0]
    zp = np.asarray(inputs["L0_ffn_zero_point"])[:, 0]
    biasf = np.asarray(inputs["L0_ffn_bias"])
    qn = np.asarray(inputs["L0_attn_q_norm_w"])
    kn = np.asarray(inputs["L0_attn_k_norm_w"])

    wqt = np.ascontiguousarray(Wq.T[PI])
    wkt = np.ascontiguousarray(Wk.T[PI])
    wvt = np.ascontiguousarray(Wv.T[PI])
    bytes_ = np.ascontiguousarray(wp).view(np.uint8).reshape(-1, 4)[:, 0]
    wpt = np.ascontiguousarray(bytes_.reshape(D, D // 2).T[:, PI])
    scp = np.ascontiguousarray(scale[PI])
    zpp = np.ascontiguousarray(zp[PI])
    bfp = np.ascontiguousarray(biasf[PI])

    sel2 = np.zeros((2, 256), np.float32)
    sel2[0, 0:128] = 1.0
    sel2[1, 128:256] = 1.0
    onescol2 = np.zeros((128, 4), BF)
    onescol2[:, 0] = 1
    onescol2[:, 3] = 1
    identf = np.eye(128, dtype=np.float32)
    qsel = np.zeros((128, 32 * 32), BF)
    for h in range(32):
        qsel[:, 32 * h + h] = 1
    hsel = np.zeros((32, 32 * 128), np.float32)
    for h in range(32):
        hsel[h, 128 * h:128 * h + 128] = 1

    return dict(wqt=wqt, wkt=wkt, wvt=wvt, wpt=wpt,
                zp32=np.ascontiguousarray(zpp.reshape(32, 128)),
                bf32=np.ascontiguousarray(bfp.reshape(32, 128)),
                sc32=np.ascontiguousarray(scp.reshape(32, 128)),
                sel2=sel2, onescol2=onescol2, identf=identf,
                qnw=np.ascontiguousarray(qn[None, :]),
                knw=np.ascontiguousarray(kn[None, :]),
                qselt=qsel, hsel=hsel)


def stage_l1(inputs):
    x = np.asarray(inputs["L0_attn_x"])[0]
    shared = _consts_l1(inputs)
    in_maps = []
    for c in range(NC):
        m = dict(shared)
        m["xt"] = np.ascontiguousarray(x[c::NC].T[PI])
        in_maps.append(m)
    return in_maps


_CACHE = {}


def run_l1(inputs):
    in_maps = stage_l1(inputs)
    if "l1" not in _CACHE:
        _CACHE["l1"] = build_l1()
    res = run_bass_kernel_spmd(_CACHE["l1"], in_maps, core_ids=list(range(NC)))
    return res.results


def stage_l2(inputs, r1):
    Wo = np.asarray(inputs["L0_attn_Wo"])
    wot = np.ascontiguousarray(Wo.T[:, PI])
    c1 = _consts_l1(inputs)

    ktall = np.empty((NKV, HD, S), BF)
    vall = np.empty((NKV, S, HD), BF)
    for c in range(NC):
        ktall[:, :, c::NC] = np.asarray(r1[c]["kt"]).reshape(NKV, HD, TPC)
        vall[:, c::NC, :] = np.asarray(r1[c]["vv"]).reshape(TPC, NKV, HD).transpose(1, 0, 2)
    vpe = np.ascontiguousarray(vall.reshape(NKV, 64, 128, HD).transpose(0, 2, 1, 3))

    slopes = _alibi_slopes(NH)
    g = np.arange(S, dtype=np.float64)
    albr = np.zeros((NH, 6, S), BF)
    for qh in range(NH):
        s0, s1, s2 = _split3(slopes[qh] * g)
        albr[qh, 0] = s0
        albr[qh, 1] = s1
        albr[qh, 2] = s2
        albr[qh, 3:6] = -1.0

    identb = np.eye(128, dtype=BF)

    shared = dict(wot=wot, wpt=c1["wpt"], zp32=c1["zp32"], bf32=c1["bf32"],
                  sc32=c1["sc32"], albr=albr, identb=identb,
                  identf=c1["identf"], sel2=c1["sel2"], onescol2=c1["onescol2"])
    in_maps = []
    jj = np.arange(TPC, dtype=np.float64)
    ii = np.arange(128)
    for c in range(NC):
        m = dict(shared)
        m["ktall"] = ktall
        m["vpe"] = vpe
        m["qt"] = np.asarray(r1[c]["qt"]).reshape(NH, HD, TPC)
        m["x1t"] = np.asarray(r1[c]["x1t"])
        p = 8.0 * jj + c
        albl = np.zeros((NH, 6, TPC), BF)
        for qh in range(NH):
            s0, s1, s2 = _split3(slopes[qh] * p)
            albl[qh, 0:3] = 1.0
            albl[qh, 3] = s0
            albl[qh, 4] = s1
            albl[qh, 5] = s2
        m["albl"] = albl
        mt = np.zeros((128, 8 * 128), np.float32)
        for bb in range(8):
            allowed = (128 * bb + ii[None, :]) <= (8 * np.arange(128)[:, None] + c)
            mt[:, 128 * bb:128 * bb + 128] = np.where(allowed, 0.0, NEG_BIG)
        m["mtin"] = mt.astype(BF)
        in_maps.append(m)
    return in_maps


def kernel(**inputs):
    in1 = stage_l1(inputs)
    if "l1" not in _CACHE:
        _CACHE["l1"] = build_l1()
    r1 = run_bass_kernel_spmd(_CACHE["l1"], in1, core_ids=list(range(NC))).results
    in2 = stage_l2(inputs, r1)
    if "l2" not in _CACHE:
        _CACHE["l2"] = build_l2()
    r2 = run_bass_kernel_spmd(_CACHE["l2"], in2, core_ids=list(range(NC))).results
    out = np.empty((S, D), np.float32)
    for c in range(NC):
        xc = np.empty((TPC, D), np.float32)
        xc[:, PI] = np.asarray(r2[c]["yt"]).T
        out[c::NC] = xc
    return out[None, :, :]
